# version 11
# baseline (speedup 1.0000x reference)
import os
import numpy as np
import ml_dtypes
from contextlib import ExitStack

import concourse.bass as bass
import concourse.mybir as mybir
from concourse.bass_utils import run_bass_kernel_spmd

F32 = mybir.dt.float32
BF16 = mybir.dt.bfloat16
AF = mybir.ActivationFunctionType
ALU = mybir.AluOpType
AX = mybir.AxisListType
BF = ml_dtypes.bfloat16

T = 2048
D = 1024
NPT = D // 128
NTT = T // 128
HID = 2730
NJ = 22
SC = 512
NSC = T // SC
CPS = SC // 128
STAGE = int(os.environ.get("K_STAGE", "3"))
PH = int(os.environ.get("K_PH", "99"))
NCHL = int(os.environ.get("K_NCHL", "99"))
SOP = int(os.environ.get("K_SOP", "63"))
NLAY = {1: 0, 2: 1, 3: 2}[STAGE]


def _host_consts():
    c = {}
    c["c_idf32"] = np.eye(128, dtype=np.float32)
    c["c_idbf"] = np.eye(128, dtype=BF)
    tri = (np.arange(128)[:, None] <= np.arange(128)[None, :]).astype(np.float32)
    c["c_maskA"] = np.tile(tri, (1, 8)).astype(BF)
    big = np.float32(-1e30)
    c["c_mko"] = np.where(np.arange(128)[:, None] <= np.arange(128)[None, :], 0.0, big).astype(BF)
    c["c_mkp"] = np.where(np.arange(128)[:, None] >= np.arange(128)[None, :], 0.0, big).astype(BF)
    c["c_ones_col_bf"] = np.ones((128, 1), dtype=BF)
    c["c_ones_row_f32"] = np.ones((1, 128), dtype=np.float32)
    bd2 = np.zeros((2, 128), dtype=np.float32)
    bd2[0, 0:64] = 1.0
    bd2[1, 64:128] = 1.0
    bd2r = np.zeros((128, 128), dtype=np.float32)
    bd2r[0:2] = bd2
    bd2r[64:66] = bd2
    c["c_bd2_f32"] = bd2r
    c["c_bd2T_bf"] = np.ascontiguousarray(bd2.T).astype(BF)
    P1 = np.zeros((64, 64), dtype=np.float32)
    for dp in range(32):
        P1[dp + 32, dp] = -1.0
    for dp in range(32, 64):
        P1[dp - 32, dp] = 1.0
    Pr = np.zeros((128, 128), dtype=np.float32)
    Pr[0:64, 0:64] = P1
    Pr[64:128, 64:128] = P1
    c["c_prot"] = Pr.astype(BF)
    c["c_eps"] = np.tile(np.array([[1e-5, 1e-24]], dtype=np.float32), (128, 1))
    DIM_HEAD, WINDOW = 64, 512
    half = np.arange(0, DIM_HEAD, 2, dtype=np.float64)
    inv_freq = 1.0 / 10000.0 ** (half / DIM_HEAD)
    t = np.arange(T, dtype=np.float64)
    freqs = np.concatenate([t[:, None] * inv_freq[None, :]] * 2, axis=-1)
    base = (half + 0.4 * DIM_HEAD) / (1.4 * DIM_HEAD)
    power = (t - T // 2) / float(WINDOW // 2)
    scale = np.concatenate([base[None, :] ** power[:, None]] * 2, axis=-1)
    cos, sin = np.cos(freqs), np.sin(freqs)
    QKS = 8.0
    for nm, tab in (("c_csq", QKS * (cos * scale).T), ("c_ssq", QKS * (sin * scale).T),
                    ("c_csk", (cos / scale).T), ("c_ssk", (sin / scale).T)):
        c[nm] = np.concatenate([tab, tab], axis=0).astype(np.float32).astype(BF)
    return c


_CONSTS = _host_consts()
DBGNAMES = {}


def _prep_weights(inp):
    w = {}
    asc = np.ascontiguousarray

    def tile_cols(M):
        K, Mf = M.shape
        nmt = -(-Mf // 128)
        P = np.zeros((nmt, K, 128), dtype=np.float32)
        for i in range(nmt):
            blk = M[:, i * 128:(i + 1) * 128]
            P[i, :, :blk.shape[1]] = blk
        return asc(P)

    def col_pt(v):
        return asc(v.reshape(-1, 128).T.astype(np.float32))

    for nm in ("Wq", "Wk", "Wv", "Wo"):
        w["A_" + nm] = tile_cols(asc(inp[f"L0_attn_{nm}"].astype(np.float32).T))
    for nm in ("bq", "bk", "bv", "bo"):
        w["A_" + nm] = col_pt(inp[f"L0_attn_{nm}"].astype(np.float32))
    for l in range(2):
        w[f"B{l}_wqkv"] = tile_cols(inp["L0_ffn_wqkv"][l].astype(np.float32))
        w[f"B{l}_wo"] = tile_cols(inp["L0_ffn_wo"][l].astype(np.float32))
        w1 = inp["L0_ffn_w1"][l].astype(np.float32)
        a_t = tile_cols(w1[:, :HID])
        g_t = tile_cols(w1[:, HID:])
        inter = np.empty((2 * NJ, D, 128), dtype=np.float32)
        inter[0::2] = a_t
        inter[1::2] = g_t
        w[f"B{l}_w1"] = asc(inter)
        w2p = np.zeros((NJ * 128, D), dtype=np.float32)
        w2p[:HID] = inp["L0_ffn_w2"][l].astype(np.float32)
        w[f"B{l}_w2"] = tile_cols(w2p)
        b1 = inp["L0_ffn_b1"][l].astype(np.float32)
        b1a = np.zeros(NJ * 128, dtype=np.float32)
        b1a[:HID] = b1[:HID]
        b1g = np.zeros(NJ * 128, dtype=np.float32)
        b1g[:HID] = b1[HID:]
        w[f"B{l}_b1a"] = col_pt(b1a)
        w[f"B{l}_b1g"] = col_pt(b1g)
        w[f"B{l}_b2"] = col_pt(inp["L0_ffn_b2"][l].astype(np.float32))
        w[f"B{l}_lng"] = col_pt(inp["L0_ffn_ln_g"][l].astype(np.float32))
        w[f"B{l}_lnb"] = col_pt(inp["L0_ffn_ln_b"][l].astype(np.float32))
        w[f"B{l}_qs"] = asc(inp["L0_ffn_q_scale"][l].astype(np.float32)[None, :])
        w[f"B{l}_ks"] = asc(inp["L0_ffn_k_scale"][l].astype(np.float32)[None, :])
    return w


class Ctr:
    def __init__(self, sem, step):
        self.sem = sem
        self.step = step
        self.val = 0

    def inc(self):
        self.val += self.step
        return self.val


class Sched:
    ENGS = ("t", "v", "s", "g", "y")

    def __init__(self):
        self.ops = {e: [] for e in self.ENGS}
        self.wmax = {e: {} for e in self.ENGS}

    def emit(self, eng, fn):
        self.ops[eng].append(fn)

    def wait(self, eng, ctr, val=None):
        v = ctr.val if val is None else val
        if v <= 0:
            return
        key = id(ctr.sem)
        if self.wmax[eng].get(key, -1) >= v:
            return
        self.wmax[eng][key] = v
        self.ops[eng].append(lambda e, s=ctr.sem, v=v: e.wait_ge(s, v))

    def chain_inc(self, eng, fns, ctr):
        self.wait(eng, ctr, ctr.val)
        v = None
        for f in fns[:-1]:
            v = self.last_inc(eng, [f], ctr)
            self.wait(eng, ctr, v)
        return self.last_inc(eng, [fns[-1]], ctr)

    def last_inc(self, eng, fns, ctr):
        assert fns
        for f in fns[:-1]:
            self.ops[eng].append(f)
        self.ops[eng].append(lambda e, f=fns[-1]: f(e).then_inc(ctr.sem, ctr.step))
        return ctr.inc()


class Gate:
    def __init__(self):
        self.ctr = None
        self.val = 0

    def set(self, ctr, val):
        self.ctr = ctr
        self.val = val

    def require(self, sch, eng):
        if self.ctr is not None and self.val > 0:
            sch.wait(eng, self.ctr, self.val)


def build_nc(wspecs):
    nc = bass.Bass("TRN2", target_bir_lowering=False)

    x_in = nc.dram_tensor("x_in", [T, D], F32, kind="ExternalInput").ap()
    y_out = nc.dram_tensor("y_out", [T, D], F32, kind="ExternalOutput").ap()
    x1s = nc.dram_tensor("x1s", [128, NPT * T], F32, kind="Internal").ap()

    cons = {}
    for nm, arr in _CONSTS.items():
        dt = BF16 if arr.dtype == BF else F32
        cons[nm] = nc.dram_tensor(nm, list(arr.shape), dt, kind="ExternalInput").ap()
    wdr = {}
    for nm, shape in wspecs.items():
        wdr[nm] = nc.dram_tensor(nm, list(shape), F32, kind="ExternalInput").ap()

    sch = Sched()
    ctx = ExitStack()

    _uid = [0]

    def _un(name):
        _uid[0] += 1
        un = f"{name}_{_uid[0]}"
        DBGNAMES.setdefault(name, []).append(un)
        return un

    def sb(name, shape, dt, st=None):
        return (st or ctx).enter_context(nc.sbuf_tensor(_un(name), shape, dt))

    def ps(name, shape, st=None):
        return (st or ctx).enter_context(nc.psum_tensor(_un(name), shape, F32))

    def sem(name, step=1):
        return Ctr(ctx.enter_context(nc.semaphore(name)), step)

    with ctx:
        XT = [sb(f"XT{i}", [128, T], F32) for i in range(NPT)]
        cid32 = sb("cid32", [128, 128], F32)
        cidbf = sb("cidbf", [128, 128], BF16)
        conesc = sb("conesc", [128, 1], BF16)
        conesrf = sb("conesrf", [1, 128], F32)
        cbd2f = sb("cbd2f", [128, 128], F32)
        cbd2t = sb("cbd2t", [128, 2], BF16)
        cprot = sb("cprot", [128, 128], BF16)
        cmko = sb("cmko", [128, 128], BF16)
        cmkp = sb("cmkp", [128, 128], BF16)
        cepsc = sb("cepsc", [128, 2], F32)
        nc.const_aps.register(F32, 1e-5, cepsc[:, 0:1])
        nc.const_aps.register(F32, 1e-24, cepsc[:, 1:2])
        wrA = [sb(f"wrA{i}", [128, 1024], BF16) for i in range(4)]

        sLD = sem("sLD", 16)
        sGD = sem("sGD", 16)
        sGW = [sem(f"sGW{i}", 16) for i in range(4)]
        sGO = [sem(f"sGO{i}", 16) for i in range(4)]
        sG2 = [sem("sG20", 16), sem("sG21", 16)]
        sPE = sem("sPE", 1)
        sVE = sem("sVE", 1)
        sSC = sem("sSC", 1)
        sST = sem("sST", 16)

        for nm, dst in (("c_idf32", cid32), ("c_idbf", cidbf),
                        ("c_ones_col_bf", conesc), ("c_ones_row_f32", conesrf),
                        ("c_bd2_f32", cbd2f), ("c_bd2T_bf", cbd2t),
                        ("c_prot", cprot),
                        ("c_mko", cmko), ("c_mkp", cmkp), ("c_eps", cepsc)):
            sch.emit("y", lambda e, nm=nm, dst=dst:
                     e.dma_start(out=dst[:, :], in_=cons[nm]).then_inc(sLD.sem, 16))
            sLD.inc()

        wrA_gate = [Gate(), Gate(), Gate(), Gate()]
        wrO_gate = [Gate(), Gate(), Gate(), Gate()]
        wr2_gate = [Gate(), Gate()]

        def wload(nm, sub, dstbuf, gate=None, ctr=None):
            src = wdr[nm][sub].rearrange("(a p) m -> p a m", p=128)
            dst = dstbuf[:, :].rearrange("p (a m) -> p a m", m=128)
            ctr = ctr or sGD
            if gate is not None:
                gate.require(sch, "g")
            sch.emit("g", lambda e, dst=dst, src=src, ctr=ctr:
                     e.dma_start(out=dst, in_=src).then_inc(ctr.sem, 16))
            return ctr


        def dma_phase_gate():
            for qq in ("y", "g"):
                sch.wait(qq, sPE, sPE.val)
                sch.wait(qq, sVE, sVE.val)
                sch.wait(qq, sSC, sSC.val)

        def colload(nm, dst, ctr=None):
            ctr = ctr or sLD
            sch.emit("y", lambda e, nm=nm, dst=dst, ctr=ctr:
                     e.dma_start(out=dst, in_=wdr[nm]).then_inc(ctr.sem, 16))
            return ctr.inc()

        def transpose_batch(srcs, dstw, pbufs, ev):
            nb = len(srcs) // 4
            for b in range(nb):
                buf = b % 2
                ev[buf].require(sch, "t")
                fns = []
                for j in range(4):
                    sap = srcs[b * 4 + j]
                    fns.append(lambda e, buf=buf, j=j, sap=sap: e.matmul(
                        pbufs[buf][:, j * 128:(j + 1) * 128], sap, cidbf[:, :],
                        start=True, stop=True))
                vp = sch.last_inc("t", fns, sPE)
                sch.wait("s", sPE, vp)
                vs = sch.last_inc("s", [lambda e, buf=buf, b=b: dstw[b](e, pbufs[buf])], sSC)
                ev[buf].set(sSC, vs)
            return sSC.val

        if STAGE >= 2:
            stPB = ExitStack()
            tabq = [sb("tcsq", [128, T], BF16, stPB), sb("tssq", [128, T], BF16, stPB)]
            tabk = [sb("tcsk", [128, T], BF16, stPB), sb("tssk", [128, T], BF16, stPB)]
            LCON = []
            sTB = sem("sTB", 16)
            sCL = [sem("sCL0", 16), sem("sCL1", 16)]
            for l in range(NLAY):
                cc = {}
                cc["qscol"] = sb("qscol", [128, 1], F32, stPB)
                cc["kscol"] = sb("kscol", [128, 1], F32, stPB)
                SROW = sb("SROW", [1, 256], F32, stPB)
                cc["qsrow"] = SROW[0:1, 0:64]
                cc["ksrow"] = SROW[0:1, 64:128]
                cc["m0r"] = SROW[0:1, 128:129]
                cc["m0r2"] = SROW[0:1, 129:130]
                cc["negm0"] = sb("negm0", [128, 1], F32, stPB)
                cc["lngc"] = sb("lngc", [128, 8], F32, stPB)
                cc["lnbc"] = sb("lnbc", [128, 8], F32, stPB)
                cc["b1ac"] = sb("b1ac", [128, NJ], F32, stPB)
                cc["b1gc"] = sb("b1gc", [128, NJ], F32, stPB)
                cc["b2c"] = sb("b2c", [128, 8], F32, stPB)
                LCON.append(cc)

        with ExitStack() as st0:
            xtok = sb("xtok", [128, NTT * D], F32, st0)
            xr = x_in.rearrange("(tt p) d -> p tt d", p=128)
            xtok3 = xtok[:, :].rearrange("p (tt d) -> p tt d", d=D)
            sch.emit("y", lambda e: e.dma_start(out=xtok3, in_=xr).then_inc(sLD.sem, 16))
            sLD.inc()
            sch.wait("t", sLD, sLD.val)
            pT = [ps("pT0", [128, 512], st0), ps("pT1", [128, 512], st0)]
            ev = [0, 0]
            idx = 0
            for tt in range(NTT):
                for dg in range(2):
                    buf = idx % 2
                    if idx >= 2:
                        sch.wait("t", sVE, ev[buf])
                    fns = []
                    for j in range(4):
                        dt_i = dg * 4 + j
                        fns.append(lambda e, buf=buf, j=j, tt=tt, dt_i=dt_i: e.matmul(
                            pT[buf][:, j * 128:(j + 1) * 128],
                            xtok[:, tt * D + dt_i * 128: tt * D + dt_i * 128 + 128],
                            cid32[:, :], start=True, stop=True))
                    vp = sch.last_inc("t", fns, sPE)
                    sch.wait("v", sPE, vp)
                    fns = [lambda e, buf=buf, j=j, tt=tt, dt_i=dg * 4 + j: e.tensor_copy(
                        XT[dt_i][:, tt * 128:(tt + 1) * 128],
                        pT[buf][:, j * 128:(j + 1) * 128]) for j in range(4)]
                    ev[buf] = sch.last_inc("v", fns, sVE)
                    idx += 1
        v_xt_ready = sVE.val

        if STAGE >= 2:
            for nm, dst in (("c_csq", tabq[0]), ("c_ssq", tabq[1]),
                            ("c_csk", tabk[0]), ("c_ssk", tabk[1])):
                sch.emit("y", lambda e, nm=nm, dst=dst:
                         e.dma_start(out=dst[:, :], in_=cons[nm]).then_inc(sTB.sem, 16))
                sTB.inc()
            v_tabs_all = sTB.val
            for l in range(NLAY):
                WN = f"B{l}_"
                cc = LCON[l]
                m0r = cc["m0r"]
                m0r2 = cc["m0r2"]
                for nm, dst in ((WN + "lng", cc["lngc"][:, :]), (WN + "lnb", cc["lnbc"][:, :]),
                                (WN + "b1a", cc["b1ac"][:, :]), (WN + "b1g", cc["b1gc"][:, :]),
                                (WN + "b2", cc["b2c"][:, :]), (WN + "qs", cc["qsrow"]),
                                (WN + "ks", cc["ksrow"])):
                    colload(nm, dst, sCL[l])
                for srcn, dst in ((WN + "qs", cc["qscol"]), (WN + "ks", cc["kscol"])):
                    for hb in range(2):
                        sch.emit("y", lambda e, srcn=srcn, dst=dst, hb=hb, l=l: e.dma_start(
                            out=dst[hb * 64:(hb + 1) * 64, 0:1],
                            in_=wdr[srcn].rearrange("a b -> b a")).then_inc(sCL[l].sem, 16))
                        sCL[l].inc()
                cc["ctr"] = sCL[l]
                cc["v_bias"] = sCL[l].val
                sch.wait("v", sCL[l], cc["v_bias"])
                vv = sch.chain_inc("v", [
                    lambda e, m0r=m0r, q=cc["qsrow"]: e.reduce_max(
                        m0r, q, AX.X, apply_absolute_value=True),
                    lambda e, m0r2=m0r2, ks=cc["ksrow"]: e.reduce_max(
                        m0r2, ks, AX.X, apply_absolute_value=True),
                    lambda e, m0r=m0r, m0r2=m0r2: e.tensor_mul(m0r, m0r, m0r2),
                    lambda e, m0r=m0r: e.tensor_scalar_mul(m0r, m0r, -8.0)], sVE)
                with ExitStack() as stm:
                    pM = ps("pM0", [128, 1], stm)
                    sch.wait("t", sVE, vv)
                    if l > 0:
                        sch.wait("t", sSC, LCON[l - 1]["v_m0"])
                    vp = sch.last_inc("t", [lambda e, m0r=m0r: e.matmul(
                        pM[:, :], conesrf[:, :], m0r, start=True, stop=True)], sPE)
                    sch.wait("s", sPE, vp)
                    cc["v_m0"] = sch.last_inc("s", [lambda e, n=cc["negm0"]: e.copy(
                        n[:, :], pM[:, :])], sSC)


        def emit_ln_plain(spill=False):
            sch.wait("v", sPE, sPE.val)
            with ExitStack() as stl:
                R = sb("lnrows", [1, 5 * T], F32, stl)
                lnbf = sb("lnbf", [128, T], BF16, stl)
                lnsq = sb("lnsq", [128, T], BF16, stl)
                r_sy = R[0:1, 0:T]
                r_sq = R[0:1, T:2 * T]
                r_t = R[0:1, 2 * T:3 * T]
                r_A = R[0:1, 3 * T:4 * T]
                r_B = R[0:1, 4 * T:5 * T]
                with ExitStack() as stp:
                    pSum = ps("pSum", [1, T], stp)
                    pSq = ps("pSq", [1, T], stp)
                    for pt in range(NPT):
                        fns = [lambda e, pt=pt: e.tensor_copy(lnbf[:, :], XT[pt][:, :]),
                               lambda e: e.tensor_mul(lnsq[:, :], lnbf[:, :], lnbf[:, :])]
                        vc = sch.chain_inc("v", fns, sVE)
                        sch.wait("t", sVE, vc)
                        fns = []
                        for ch in range(4):
                            fns.append(lambda e, ch=ch, pt=pt: e.matmul(
                                pSum[0:1, ch * 512:(ch + 1) * 512], conesc[:, :],
                                lnbf[:, ch * 512:(ch + 1) * 512],
                                start=(pt == 0), stop=(pt == NPT - 1)))
                            fns.append(lambda e, ch=ch, pt=pt: e.matmul(
                                pSq[0:1, ch * 512:(ch + 1) * 512], conesc[:, :],
                                lnsq[:, ch * 512:(ch + 1) * 512],
                                start=(pt == 0), stop=(pt == NPT - 1)))
                        vp = sch.last_inc("t", fns, sPE)
                        sch.wait("v", sPE, vp)
                    sch.wait("s", sPE, sPE.val)
                    vs = sch.last_inc("s", [
                        lambda e: e.copy(r_sy, pSum[0:1, :]),
                        lambda e: e.copy(r_sq, pSq[0:1, :])], sSC)
                sch.wait("v", sSC, vs)
                vv = sch.chain_inc("v", [
                    lambda e: e.tensor_scalar_mul(r_sy, r_sy, 1.0 / D),
                    lambda e: e.tensor_scalar_mul(r_sq, r_sq, 1.0 / D),
                    lambda e: e.tensor_mul(r_t, r_sy, r_sy),
                    lambda e: e.tensor_sub(r_t, r_sq, r_t)], sVE)
                sch.wait("s", sVE, vv)
                vs = sch.chain_inc("s", [
                    lambda e: e.activation(r_t, r_t, AF.Ln, bias=1e-5),
                    lambda e: e.activation(r_A, r_t, AF.Exp, scale=-0.5)], sSC)
                sch.wait("v", sSC, vs)
                vv = sch.last_inc("v", [lambda e: e.scalar_tensor_tensor(
                    r_B, r_sy, -1.0, r_A, ALU.mult, ALU.mult)], sVE)
                with ExitStack() as stp:
                    pBA = [ps("pBA0", [128, 512], stp), ps("pBA1", [128, 512], stp)]
                    pBB = [ps("pBB0", [128, 512], stp), ps("pBB1", [128, 512], stp)]
                    tmpf = sb("lntmp", [128, 512], F32, stp)
                    ev2 = [0, 0]
                    for ch in range(4):
                        buf = ch % 2
                        sch.wait("t", sVE, vv)
                        if ch >= 2:
                            sch.wait("t", sVE, ev2[buf])
                        fns = [lambda e, buf=buf, ch=ch: e.matmul(
                            pBA[buf][:, :], conesrf[:, :],
                            R[0:1, 3 * T + ch * 512: 3 * T + (ch + 1) * 512],
                            start=True, stop=True),
                            lambda e, buf=buf, ch=ch: e.matmul(
                            pBB[buf][:, :], conesrf[:, :],
                            R[0:1, 4 * T + ch * 512: 4 * T + (ch + 1) * 512],
                            start=True, stop=True)]
                        vp = sch.last_inc("t", fns, sPE)
                        sch.wait("v", sPE, vp)
                        fns = []
                        for pt in range(NPT):
                            fns.append(lambda e, pt=pt, ch=ch, buf=buf: e.tensor_mul(
                                tmpf[:, :], XT[pt][:, ch * 512:(ch + 1) * 512], pBA[buf][:, :]))
                            fns.append(lambda e, pt=pt, ch=ch, buf=buf: e.tensor_add(
                                XT[pt][:, ch * 512:(ch + 1) * 512], tmpf[:, :], pBB[buf][:, :]))
                        ev2[buf] = sch.chain_inc("v", fns, sVE)
                    if spill:
                        vv2 = sVE.val
                        for pt in range(NPT):
                            def sp(e, pt=pt, vv2=vv2):
                                e.wait_ge(sVE.sem, vv2)
                                return e.dma_start(out=x1s[:, pt * T:(pt + 1) * T],
                                                   in_=XT[pt][:, :]).then_inc(sLD.sem, 16)
                            sch.emit("y", sp)
                            sLD.inc()

        with ExitStack() as stA:
            XB = [sb(f"XB{i}", [128, SC], BF16, stA) for i in range(NPT)]
            qT = [sb(f"qTs{i}", [128, SC], BF16, stA) for i in range(NPT)]
            kT = [sb(f"kTs{i}", [128, SC], BF16, stA) for i in range(NPT)]
            vtokA = sb("vtokA", [128, CPS * 1024], BF16, stA)
            ktokA = sb("ktokA", [128, CPS * 1024], BF16, stA)
            aout = sb("aout", [128, CPS * 1024], BF16, stA)
            aoT = sb("aoT", [128, NPT * SC], BF16, stA)
            vdm = [sb(f"vdm{i}", [128, SC], BF16, stA) for i in range(2)]
            eqs = [sb(f"eqs{i}", [128, SC], F32, stA) for i in range(2)]
            rus = [sb(f"rus{i}", [128, SC], F32, stA) for i in range(2)]
            bqc = sb("bqc", [128, 8], F32, stA)
            bkc = sb("bkc", [128, 8], F32, stA)
            bvc = sb("bvc", [128, 8], F32, stA)
            boc = sb("boc", [128, 8], F32, stA)
            maskedS = [sb(f"mskd{i}", [128, 2048], BF16, stA) for i in range(2)]
            Ssnap = sb("Ssnap", [128, 1040], BF16, stA)
            Ssb = sb("Ssb", [64, 1024], F32, stA)
            zsb = sb("zsb", [64, 16], F32, stA)
            rdenA = sb("rdenA", [128, 16], F32, stA)

            cmaskA = sb("cmaskA", [128, 1024], BF16, stA)
            dma_phase_gate()
            for nm, dst in (("A_bq", bqc[:, :]), ("A_bk", bkc[:, :]),
                            ("A_bv", bvc[:, :]), ("A_bo", boc[:, :])):
                colload(nm, dst)
            sch.emit("y", lambda e: e.dma_start(
                out=cmaskA[:, :], in_=cons["c_maskA"]).then_inc(sLD.sem, 16))
            sLD.inc()
            v_abias = sLD.val
            sch.chain_inc("v", [lambda e: e.memset(Ssb[:, :], 0.0),
                                lambda e: e.memset(zsb[:, :], 0.0)], sVE)

            stScan = ExitStack()
            sSN = sem("sSN", 16)
            tev = [Gate(), Gate()]

            def emit_partA_sc(sc):
                c0 = sc * SC
                if sc > 0:
                    sch.wait("v", sPE, sPE.val)
                    sch.wait("s", sPE, sPE.val)
                sch.wait("v", sVE, v_xt_ready)
                fns = [lambda e, i=i, c0=c0: e.tensor_copy(XB[i][:, :], XT[i][:, c0:c0 + SC])
                       for i in range(NPT)]
                v_xb = sch.last_inc("v", fns, sVE)
                sch.wait("v", sLD, v_abias)
                if sc > 0:
                    sch.wait("t", sSC, sSC.val)

                with ExitStack() as stp:
                    pQ = [ps(f"pQ{i}", [128, SC], stp) for i in range(4)]
                    pVT = [ps("pVTa", [128, 512], stp), ps("pVTb", [128, 512], stp)]
                    slot_free = [Gate(), Gate(), Gate(), Gate()]
                    idx = 0
                    last_phi = [0, 0]
                    for which, bc, kind in (("A_Wq", bqc, "q"), ("A_Wk", bkc, "k"),
                                            ("A_Wv", bvc, "v")):
                        for mt in range(NPT):
                            wslot = idx % 4
                            slot = idx % 4
                            s2 = idx % 2
                            wc = wload(which, mt, wrA[wslot], wrA_gate[wslot], sGW[wslot])
                            sch.wait("t", wc, wc.inc())
                            sch.wait("t", sVE, v_xb)
                            slot_free[slot].require(sch, "t")
                            fns = []
                            for kt in range(NPT):
                                fns.append(lambda e, wslot=wslot, slot=slot, kt=kt: e.matmul(
                                    pQ[slot][:, :],
                                    wrA[wslot][:, kt * 128:(kt + 1) * 128],
                                    XB[kt][:, :],
                                    start=(kt == 0), stop=(kt == NPT - 1)))
                            vp = sch.last_inc("t", fns, sPE)
                            wrA_gate[wslot].set(sPE, vp)
                            sch.wait("s", sPE, vp)
                            sch.wait("s", sLD, v_abias)
                            if kind in ("q", "k"):
                                dstT = qT if kind == "q" else kT
                                sch.wait("s", sVE, last_phi[s2])
                                vs = sch.last_inc("s", [
                                    lambda e, slot=slot, s2=s2, bc=bc, mt=mt:
                                    e.activation(eqs[s2][:, :], pQ[slot][:, :], AF.Exp,
                                                 bias=bc[:, mt:mt + 1])], sSC)
                                sch.wait("v", sSC, vs)
                                fns = [lambda e, slot=slot, s2=s2, bc=bc, mt=mt:
                                       e.tensor_scalar(rus[s2][:, :], pQ[slot][:, :],
                                                       bc[:, mt:mt + 1], 0.0, ALU.add, ALU.max),
                                       lambda e, dstT=dstT, s2=s2, mt=mt:
                                       e.scalar_tensor_tensor(dstT[mt][:, :], eqs[s2][:, :],
                                                              1.0, rus[s2][:, :],
                                                              ALU.min, ALU.add)]
                                vv = sch.chain_inc("v", fns, sVE)
                                slot_free[slot].set(sVE, vv)
                                last_phi[s2] = vv
                            else:
                                sch.wait("v", sPE, vp)
                                vs = sch.last_inc("v", [lambda e, slot=slot, s2=s2, mt=mt:
                                                        e.tensor_scalar_add(vdm[s2][:, :],
                                                                            pQ[slot][:, :],
                                                                            bvc[:, mt:mt + 1])], sVE)
                                slot_free[slot].set(sVE, vs)
                                sch.wait("t", sVE, vs)
                                srcs = [vdm[s2][:, j * 128:(j + 1) * 128] for j in range(CPS)]

                                def mkv(mt, s2):
                                    def wr(e, pbuf):
                                        oap = bass.AP(
                                            tensor=vtokA, offset=mt * 128,
                                            ap=[[CPS * 1024, 128], [1024, CPS], [1, 128]])
                                        iap = pbuf[:, :].rearrange("p (a m) -> p a m", m=128)
                                        return e.copy(oap, iap)
                                    return wr
                                transpose_batch(srcs, [mkv(mt, s2)], pVT, tev)
                            idx += 1
                    for dpt in range(NPT):
                        for sl in range(4):
                            slot_free[sl].require(sch, "t")
                        srcs = [kT[dpt][:, j * 128:(j + 1) * 128] for j in range(CPS)]

                        def mkk(dpt):
                            def wr(e, pbuf):
                                oap = bass.AP(
                                    tensor=ktokA, offset=dpt * 128,
                                    ap=[[CPS * 1024, 128], [1024, CPS], [1, 128]])
                                iap = pbuf[:, :].rearrange("p (a m) -> p a m", m=128)
                                return e.copy(oap, iap)
                            return wr
                        transpose_batch(srcs, [mkk(dpt)], pVT, tev)
                v_proj_done = sSC.val

                if PH < 2:
                    return
                with ExitStack() as sts:
                    pSC0 = ps("pSCa", [128, 1024], sts)
                    pSC = [pSC0, pSC0]
                    pN = [ps("pNa", [128, 512], sts), ps("pNb", [128, 512], sts)]
                    pS = ps("pS", [64, 1024], sts)
                    pZz = ps("pZz", [64, 32], sts)
                    pZden = ps("pZden", [128, 32], sts)
                    v_evac_prev = sVE.val
                    v_state_prev = sVE.val
                    v_div_prev = sSC.val
                    pe_chunk_done = sPE.val
                    last_tt = [0, 0]
                    sch.wait("t", sSC, v_proj_done)
                    sch.wait("t", sVE, sVE.val)
                    for cc in range(CPS):
                        c = sc * CPS + cc
                        if c >= NCHL:
                            continue
                        par = c % 2
                        if c > 0:
                            sch.wait("s", sPE, pe_chunk_done)
                            sch.wait("s", sVE, v_state_prev)
                            vsnap = sch.last_inc("s", [
                                lambda e: e.copy(Ssnap[0:64, 0:1024], Ssb[:, :]),
                                lambda e: e.copy(Ssnap[0:64, 1024:1040], zsb[:, :])], sSC)
                            sch.wait("g", sSC, vsnap)
                            vdup = sch.last_inc("g", [lambda e: e.dma_start(
                                out=Ssnap[64:128, :], in_=Ssnap[0:64, :])], sSN)
                        else:
                            vsnap = sSC.val
                            vdup = sSN.val
                        v_tt = [0, 0]
                        for half in range(2):
                            sch.wait("t", sVE, last_tt[0])
                            sch.wait("t", sVE, last_tt[1])
                            fns = []
                            order = [1, 3, 5, 7, None, 0, 2, 4, 6] if half == 0 \
                                else [0, 2, 4, 6, None, 1, 3, 5, 7]
                            for hh in order:
                                if hh is None:
                                    fns.append(lambda e: e.drain())
                                    continue
                                h = half * 8 + hh
                                pp, sub = h // 2, (h % 2) * 64
                                fns.append(lambda e, half=half, hh=hh, pp=pp, sub=sub, cc=cc:
                                           e.matmul(pSC[half][:, hh * 128:(hh + 1) * 128],
                                                    kT[pp][sub:sub + 64, cc * 128:(cc + 1) * 128],
                                                    qT[pp][sub:sub + 64, cc * 128:(cc + 1) * 128],
                                                    start=True, stop=True))
                            vm1 = sch.last_inc("t", fns, sPE)
                            sch.wait("v", sPE, vm1)
                            v_tt[half] = sch.last_inc("v", [
                                lambda e, half=half, par=par: e.tensor_mul(
                                    maskedS[par][:, half * 1024:(half + 1) * 1024],
                                    pSC[half][:, :], cmaskA[:, :])], sVE)
                            last_tt[half] = v_tt[half]
                            last_tt[1 - half] = max(last_tt[1 - half], 0)
                        sch.wait("t", sSC, vsnap)
                        sch.wait("t", sSN, vdup)
                        sch.wait("t", sVE, v_evac_prev)
                        sch.wait("t", sSC, v_div_prev)
                        for half in range(2):
                            sch.wait("t", sVE, v_tt[half])
                            fns = []
                            for hh in range(8):
                                h = half * 8 + hh
                                pp, sub = h // 2, (h % 2) * 64
                                moff = half * 1024 + hh * 128
                                vcol = cc * 1024 + h * 64
                                if SOP & 1:
                                    fns.append(lambda e, par=par, half=half, hh=hh, moff=moff,
                                           vcol=vcol, st=(hh == 0),
                                           sp=(c == 0 and hh == 7):
                                           e.matmul(pN[half][:, hh * 64:(hh + 1) * 64],
                                                    maskedS[par][:, moff:moff + 128],
                                                    vtokA[:, vcol:vcol + 64],
                                                    start=st, stop=sp))
                                if SOP & 2:
                                    fns.append(lambda e, par=par, moff=moff, h=h,
                                           st=(h == 0), sp=(c == 0 and h == 15):
                                           e.matmul(pZden[:, 2 * h:2 * h + 1],
                                                    maskedS[par][:, moff:moff + 128],
                                                    conesc[:, :], start=st, stop=sp))
                                if c > 0 and (SOP & 4):
                                    fns.append(lambda e, pp=pp, sub=sub, cc=cc, half=half, hh=hh, h=h:
                                               e.matmul(pN[half][:, hh * 64:(hh + 1) * 64],
                                                        qT[pp][sub:sub + 64, cc * 128:(cc + 1) * 128],
                                                        Ssnap[sub:sub + 64, h * 64:h * 64 + 64],
                                                        start=False, stop=(hh == 7)))
                                if SOP & 8:
                                    fns.append(lambda e, vcol=vcol, h=h:
                                           e.matmul(pS[0:64, h * 64:h * 64 + 64],
                                                    ktokA[:, vcol:vcol + 64],
                                                    vtokA[:, vcol:vcol + 64],
                                                    start=True, stop=True))
                                if SOP & 16:
                                    fns.append(lambda e, vcol=vcol, h=h:
                                           e.matmul(pZz[:, 2 * h:2 * h + 1],
                                                    ktokA[:, vcol:vcol + 64],
                                                    conesc[:, :], start=True, stop=True))
                                if c > 0 and (SOP & 2):
                                    fns.append(lambda e, pp=pp, sub=sub, cc=cc, h=h:
                                               e.matmul(pZden[:, 2 * h:2 * h + 1],
                                                        qT[pp][sub:sub + 64, cc * 128:(cc + 1) * 128],
                                                        Ssnap[sub:sub + 64, 1024 + h:1025 + h],
                                                        start=False, stop=(h == 15)))
                            pe_chunk_done = sch.last_inc("t", fns, sPE)
                        sch.wait("v", sPE, pe_chunk_done)
                        if not (SOP & 32):
                            continue
                        pZdh = bass.AP(tensor=pZden, offset=0, ap=[[32, 128], [2, 16]])
                        pZzh = bass.AP(tensor=pZz, offset=0, ap=[[32, 64], [2, 16]])
                        fns = []
                        if SOP & 8:
                            fns.append(lambda e: e.tensor_add(Ssb[:, :], Ssb[:, :], pS[0:64, :]))
                        if SOP & 16:
                            fns.append(lambda e, pZzh=pZzh: e.tensor_add(zsb[:, :], zsb[:, :], pZzh))
                        if fns:
                            v_state_prev = sch.chain_inc("v", fns, sVE)
                        if SOP & 2:
                            sch.wait("v", sSC, v_div_prev)
                            v_recip = sch.chain_inc("v", [
                                lambda e, pZdh=pZdh: e.reciprocal(rdenA[:, :], pZdh)], sVE)
                            sch.wait("s", sVE, v_recip)
                        fns = []
                        for h in (range(16) if (SOP & 1) and (SOP & 2) else []):
                            half, hh = h // 8, h % 8
                            fns.append(lambda e, half=half, hh=hh, h=h, cc=cc:
                                       e.mul(
                                           aout[:, cc * 1024 + h * 64: cc * 1024 + h * 64 + 64],
                                           pN[half][:, hh * 64:(hh + 1) * 64],
                                           rdenA[:, h:h + 1]))
                        if fns:
                            v_div_prev = sch.last_inc("s", fns, sSC)
                        v_evac_prev = v_state_prev

                if PH < 3:
                    return
                with ExitStack() as stt:
                    pVT2 = [ps("pVTc", [128, 512], stt), ps("pVTd", [128, 512], stt)]
                    tev2 = [Gate(), Gate()]
                    sch.wait("t", sVE, v_evac_prev)
                    sch.wait("t", sSC, v_div_prev)
                    for tt in range(CPS):
                        srcs = [aout[:, tt * 1024 + j * 128: tt * 1024 + (j + 1) * 128]
                                for j in range(8)]
                        dstw = []
                        for b in range(2):
                            def mka(b, tt):
                                def wr(e, pbuf):
                                    oap = bass.AP(
                                        tensor=aoT, offset=(b * 4) * SC + tt * 128,
                                        ap=[[NPT * SC, 128], [SC, 4], [1, 128]])
                                    iap = pbuf[:, :].rearrange("p (a m) -> p a m", m=128)
                                    return e.copy(oap, iap)
                                return wr
                            dstw.append(mka(b, tt))
                        transpose_batch(srcs, dstw, pVT2, tev2)
                v_aoT = sSC.val

                with ExitStack() as stw:
                    pW = [ps("pWa", [128, SC], stw), ps("pWb", [128, SC], stw)]
                    slot_free = [Gate(), Gate()]
                    for mt in range(NPT):
                        wslot = mt % 4
                        slot = mt % 2
                        wc = wload("A_Wo", mt, wrA[wslot], wrA_gate[wslot], sGW[wslot])
                        sch.wait("t", wc, wc.inc())
                        sch.wait("t", sSC, v_aoT)
                        slot_free[slot].require(sch, "t")
                        fns = []
                        for kt in range(NPT):
                            fns.append(lambda e, wslot=wslot, slot=slot, kt=kt: e.matmul(
                                pW[slot][:, :],
                                wrA[wslot][:, kt * 128:(kt + 1) * 128],
                                aoT[:, kt * SC:(kt + 1) * SC],
                                start=(kt == 0), stop=(kt == NPT - 1)))
                        vp = sch.last_inc("t", fns, sPE)
                        wrA_gate[wslot].set(sPE, vp)
                        sch.wait("v", sPE, vp)
                        vv = sch.last_inc("v", [
                            lambda e, slot=slot, mt=mt, c0=c0: e.scalar_tensor_tensor(
                                XT[mt][:, c0:c0 + SC], pW[slot][:, :], boc[:, mt:mt + 1],
                                XT[mt][:, c0:c0 + SC], ALU.add, ALU.add)], sVE)
                        slot_free[slot].set(sVE, vv)
            for sc in range(NSC):
                emit_partA_sc(sc)
            stScan.close()
        if PH >= 3:
            emit_ln_plain(spill=(STAGE >= 2))
        v_spill = sLD.val

        if STAGE >= 2:

            def emit_layerB(l):
                WN = f"B{l}_"
                cc = LCON[l]
                qscol = cc["qscol"]
                kscol = cc["kscol"]
                negm0 = cc["negm0"]
                lngc = cc["lngc"]
                lnbc = cc["lnbc"]
                b1ac = cc["b1ac"]
                b1gc = cc["b1gc"]
                b2c = cc["b2c"]
                v_bias = cc["v_bias"]
                v_m0 = cc["v_m0"]
                v_tabs = v_tabs_all
                with ExitStack() as stL:
                    oTsb = sb("oTsb", [128, 4 * T], BF16, stL)
                    with ExitStack() as stAt:
                        q2sb = sb("q2sb", [128, 4 * T], BF16, stAt)
                        k2sb = sb("k2sb", [128, 4 * T], BF16, stAt)
                        vtokB = sb("vtokB", [128, NTT * 520], BF16, stAt)
                        with ExitStack() as stq:
                            hbuf = [sb(f"hbuf{i}", [128, 1024], BF16, stq)
                                    for i in range(NPT)]
                            sqb = [sb(f"sqb{i}", [128, 1024], BF16, stq) for i in range(4)]
                            qraw = [sb(f"qraw{i}", [128, 1024], BF16, stq)
                                    for i in range(4)]
                            g_sqb = [Gate() for _ in range(4)]
                            g_qraw = [Gate() for _ in range(4)]
                            rstdr = sb("rstdr", [2, 1024], F32, stq)
                            rbsb = [sb(f"rbsb{i}", [128, 512], F32, stq) for i in range(2)]
                            qqs = [sb(f"qqs{i}", [128, 512], BF16, stq) for i in range(2)]
                            t1sb = sb("t1sb", [128, 512], BF16, stq)
                            t2sb = sb("t2sb", [128, 512], BF16, stq)
                            vdmB = sb("vdmB", [128, 1024], BF16, stq)

                            def shift_cast(nb, hb):
                                c0n = nb * 1024
                                fns = []
                                for pt in range(NPT):
                                    if pt < 4:
                                        fns.append(lambda e, pt=pt, c0n=c0n: e.tensor_copy(
                                            hb[pt][:, :], XT[pt][:, c0n:c0n + 1024]))
                                    elif nb == 0:
                                        fns.append(lambda e, pt=pt: e.memset(
                                            hb[pt][:, 0:1], 0.0))
                                        fns.append(lambda e, pt=pt: e.tensor_copy(
                                            hb[pt][:, 1:1024], XT[pt][:, 0:1023]))
                                    else:
                                        fns.append(lambda e, pt=pt, c0n=c0n: e.tensor_copy(
                                            hb[pt][:, :], XT[pt][:, c0n - 1:c0n + 1023]))
                                return sch.last_inc("v", fns, sVE)

                            pQK = [ps("pQKa", [128, 1024], stq), ps("pQKb", [128, 1024], stq)]
                            pS2 = ps("pS2", [2, 512], stq)
                            pRB = ps("pRBx", [128, 512], stq)
                            pRT = [ps("pRTa", [128, 512], stq), ps("pRTb", [128, 512], stq)]
                            slot_free = [Gate(), Gate()]
                            g_pS2 = Gate()
                            g_pRB = Gate()
                            g_rbsb = [Gate(), Gate()]
                            g_qqs = [Gate(), Gate()]
                            tevB = [Gate(), Gate()]
                            for nb in range(2):
                                c0n = nb * 1024
                                sch.wait("v", sPE, sPE.val)
                                v_h = shift_cast(nb, hbuf)
                                if nb == 0 and l == 0:
                                    sch.wait("t", sSC, LCON[-1]["v_m0"])
                                for mt in range(12):
                                    wslot = mt % 4
                                    slot = mt % 2
                                    wc = wload(WN + "wqkv", mt, wrA[wslot], wrA_gate[wslot], sGW[wslot])
                                    sch.wait("t", wc, wc.inc())
                                    sch.wait("t", sVE, v_h)
                                    slot_free[slot].require(sch, "t")
                                    fns = []
                                    for kt in range(NPT):
                                        for h2 in range(2):
                                            fns.append(lambda e, wslot=wslot, slot=slot, kt=kt, h2=h2:
                                                       e.matmul(
                                                           pQK[slot][:, h2 * 512:(h2 + 1) * 512],
                                                           wrA[wslot][:, kt * 128:(kt + 1) * 128],
                                                           hbuf[kt][:, h2 * 512:(h2 + 1) * 512],
                                                           start=(kt == 0), stop=(kt == NPT - 1)))
                                    vp = sch.last_inc("t", fns, sPE)
                                    wrA_gate[wslot].set(sPE, vp)
                                    if mt < 8:
                                        isq = mt < 4
                                        pp = mt % 4
                                        s4 = mt % 4
                                        scol = qscol if isq else kscol
                                        tabs = tabq if isq else tabk
                                        dst2 = q2sb if isq else k2sb
                                        sch.wait("s", sPE, vp)
                                        sch.wait("s", cc["ctr"], v_bias)
                                        g_sqb[s4].require(sch, "s")
                                        g_qraw[s4].require(sch, "s")
                                        vs = sch.last_inc("s", [
                                            lambda e, slot=slot, s4=s4: e.activation(
                                                sqb[s4][:, :], pQK[slot][:, :],
                                                AF.Square)], sSC)
                                        v_qcp = sch.last_inc("s", [
                                            lambda e, slot=slot, s4=s4: e.copy(
                                                qraw[s4][:, :], pQK[slot][:, :])], sSC)
                                        slot_free[slot].set(sSC, v_qcp)
                                        for ch in range(2):
                                            sch.wait("t", sSC, vs)
                                            g_pS2.require(sch, "t")
                                            vpc = sch.last_inc("t", [
                                                lambda e, s4=s4, ch=ch: e.matmul(
                                                    pS2[:, :], cbd2t[:, :],
                                                    sqb[s4][:, ch * 512:(ch + 1) * 512],
                                                    start=True, stop=True)], sPE)
                                            if ch == 1:
                                                g_sqb[s4].set(sPE, vpc)
                                            sch.wait("s", sPE, vpc)
                                            vs2 = sch.chain_inc("s", [
                                                lambda e, ch=ch: e.activation(
                                                    rstdr[:, ch * 512:(ch + 1) * 512],
                                                    pS2[:, :], AF.Ln, bias=1e-24),
                                                lambda e, ch=ch: e.activation(
                                                    rstdr[:, ch * 512:(ch + 1) * 512],
                                                    rstdr[:, ch * 512:(ch + 1) * 512],
                                                    AF.Exp, scale=-0.5)], sSC)
                                            g_pS2.set(sSC, vs2)
                                            sch.wait("t", sSC, vs2)
                                            g_pRB.require(sch, "t")
                                            vpb = sch.last_inc("t", [
                                                lambda e, ch=ch: e.matmul(
                                                    pRB[:, :], cbd2f[0:2, :],
                                                    rstdr[:, ch * 512:(ch + 1) * 512],
                                                    start=True, stop=True)], sPE)
                                            sch.wait("s", sPE, vpb)
                                            g_rbsb[ch].require(sch, "s")
                                            vs3 = sch.last_inc("s", [
                                                lambda e, ch=ch: e.copy(
                                                    rbsb[ch][:, :], pRB[:, :])], sSC)
                                            g_pRB.set(sSC, vs3)
                                            sch.wait("v", sSC, vs3)
                                            g_qqs[ch].require(sch, "v")
                                            vq = sch.last_inc("v", [
                                                lambda e, s4=s4, ch=ch, scol=scol:
                                                e.scalar_tensor_tensor(
                                                    qqs[ch][:, :],
                                                    qraw[s4][:, ch * 512:(ch + 1) * 512],
                                                    scol[:, 0:1], rbsb[ch][:, :],
                                                    ALU.mult, ALU.mult)], sVE)
                                            g_rbsb[ch].set(sVE, vq)
                                            if ch == 1:
                                                g_qraw[s4].set(sVE, vq)
                                            sch.wait("t", sVE, vq)
                                            vpr = sch.last_inc("t", [
                                                lambda e, ch=ch: e.matmul(
                                                    pRT[ch][:, :], cprot[:, :],
                                                    qqs[ch][:, :], start=True, stop=True)], sPE)
                                            gcol = c0n + ch * 512
                                            sch.wait("v", sPE, vpr)
                                            sch.wait("v", sTB, v_tabs)
                                            vq2 = sch.chain_inc("v", [
                                                lambda e, ch=ch, tabs=tabs, gcol=gcol:
                                                e.tensor_mul(t1sb[:, :], qqs[ch][:, :],
                                                             tabs[0][:, gcol:gcol + 512]),
                                                lambda e, ch=ch, tabs=tabs, gcol=gcol:
                                                e.tensor_mul(t2sb[:, :], pRT[ch][:, :],
                                                             tabs[1][:, gcol:gcol + 512]),
                                                lambda e, ch=ch, pp=pp, gcol=gcol, dst2=dst2:
                                                e.tensor_add(
                                                    dst2[:, pp * T + gcol: pp * T + gcol + 512],
                                                    t1sb[:, :], t2sb[:, :])], sVE)
                                            g_qqs[ch].set(sVE, vq2)
                                    else:
                                        mtv = mt - 8
                                        if mtv == 0:
                                            tevB[0].set(sVE, sVE.val)
                                            tevB[1].set(sVE, sVE.val)
                                        sch.wait("s", sPE, vp)
                                        vs = sch.last_inc("s", [
                                            lambda e, slot=slot: e.copy(
                                                vdmB[:, :], pQK[slot][:, :])], sSC)
                                        slot_free[slot].set(sSC, vs)
                                        sch.wait("t", sSC, vs)
                                        srcs = [vdmB[:, j * 128:(j + 1) * 128]
                                                for j in range(8)]
                                        dstw = []
                                        for b in range(2):
                                            def mkb(b, mtv, nb):
                                                def wr(e, pbuf):
                                                    tt0 = nb * 8 + b * 4
                                                    oap = bass.AP(
                                                        tensor=vtokB,
                                                        offset=tt0 * 520 + (2 * mtv) * 65,
                                                        ap=[[NTT * 520, 128], [520, 4],
                                                            [65, 2], [1, 64]])
                                                    iap = pbuf[:, :].rearrange(
                                                        "p (a h m) -> p a h m", h=2, m=64)
                                                    return e.copy(oap, iap)
                                                return wr
                                            dstw.append(mkb(b, mtv, nb))
                                        transpose_batch(srcs, dstw, pRT, tevB)
                            fns = [lambda e, tt=tt: e.memset(bass.AP(
                                tensor=vtokB, offset=tt * 520 + 64,
                                ap=[[NTT * 520, 128], [65, 8]]), 1.0) for tt in range(NTT)]
                            v_vones = sch.last_inc("v", fns, sVE)
                        v_qk = sVE.val

                        DEN = sb("DENROW", [1, 3 * T], F32, stAt)
                        rdrow = DEN[0:1, 2 * T:3 * T]
                        RDENs = [sb("RDENa", [128, T], F32, stAt),
                                 sb("RDENb", [128, T], F32, stAt)]
                        with ExitStack() as sta:
                            expT = [sb(f"expT{i}", [128, 640], BF16, sta)
                                    for i in range(3)]
                            pSCb = [ps("pSCx", [128, 640], sta), ps("pSCy", [128, 640], sta)]
                            pOT = ps("pOT", [65, T], sta)
                            sc_free = [0, 0]
                            av_done = {}
                            v_evac_head = sSC.val
                            v_ev2_prev = sSC.val
                            v_ev3_box = [sSC.val]
                            v_rd = sVE.val
                            v_rd_box = [sVE.val]
                            dma_after_head = {}
                            pending_fin = [None]
                            pending_ev3 = [None]

                            def mk_fin(h, pp):
                                def fin():
                                    sch.wait("v", sSC, v_ev3_box[0])
                                    sch.wait("v", sGD, sGD.val)
                                    v_rd = sch.last_inc("v", [lambda e, h=h: e.reciprocal(
                                        rdrow, DEN[0:1, (h % 2) * T:(h % 2) * T + T])], sVE)
                                    v_rd_box[0] = v_rd

                                    def gdma(e, pp=pp, s2=(h % 2), v_rd=v_rd):
                                        e.wait_ge(sVE.sem, v_rd)
                                        base = 64 * (pp % 2) + s2
                                        return e.dma_start(
                                            out=RDENs[pp // 2][base:base + 1, :],
                                            in_=rdrow).then_inc(sGD.sem, 16)
                                    sch.emit("g", gdma)
                                    sGD.inc()
                                    dma_after_head[h] = sGD.val
                                return fin
                            for h in range(8):
                                pp, sub = h // 2, (h % 2) * 64
                                banks_touched = set()
                                reg_open = set()
                                for kt in range(NTT):
                                    g1 = min(kt + 4, NTT - 1)
                                    vq = (g1 - kt + 1) * 128
                                    q0 = kt * 128
                                    slot = kt % 2
                                    esl = (h * NTT + kt) % 3
                                    sch.wait("t", sVE, v_qk)
                                    sch.wait("t", sVE, v_vones)
                                    sch.wait("t", sSC, sc_free[slot])
                                    fns = []
                                    has_mkp = kt + 4 <= NTT - 1
                                    segs = [(0, min(128, vq), True, False)]
                                    if vq > 128:
                                        segs.append((128, min(512, vq), False, False))
                                    if vq > 512:
                                        segs.append((512, vq, True, not has_mkp))
                                    for o0, o1, st, sp in segs:
                                        fns.append(lambda e, slot=slot, pp=pp, sub=sub,
                                                   kt=kt, o0=o0, o1=o1, q0=q0, st=st, sp=sp:
                                                   e.matmul(
                                                       pSCb[slot][:, o0:o1],
                                                       k2sb[sub:sub + 64,
                                                            pp * T + kt * 128:
                                                            pp * T + (kt + 1) * 128],
                                                       q2sb[sub:sub + 64,
                                                            pp * T + q0 + o0:
                                                            pp * T + q0 + o1],
                                                       start=st, stop=sp))
                                    fns.append(lambda e, slot=slot: e.matmul(
                                        pSCb[slot][:, 0:128], cidbf[:, :], cmko[:, :],
                                        start=False, stop=True))
                                    if kt + 4 <= NTT - 1:
                                        fns.append(lambda e, slot=slot: e.matmul(
                                            pSCb[slot][:, 512:640], cidbf[:, :], cmkp[:, :],
                                            start=False, stop=True))
                                    vp = sch.last_inc("t", fns, sPE)
                                    sch.wait("s", sPE, vp)
                                    prev = (h, kt - 3) if kt >= 3 else (h - 1, NTT - 3 + kt)
                                    if prev in av_done:
                                        sch.wait("s", sPE, av_done[prev])
                                    sch.wait("s", sSC, v_m0)
                                    vs = sch.last_inc("s", [
                                        lambda e, slot=slot, esl=esl, vq=vq:
                                        e.activation(expT[esl][:, 0:vq],
                                                     pSCb[slot][:, 0:vq], AF.Exp,
                                                     bias=negm0[:, 0:1])], sSC)
                                    sc_free[slot] = vs
                                    sch.wait("t", sSC, vs)
                                    if h > 0 and kt == 0:
                                        sch.wait("t", sSC, v_ev2_prev)
                                    if h > 0 and kt == 8:
                                        sch.wait("t", sSC, v_ev3_box[0])
                                    fns = []
                                    for g in range(kt, min(kt + 4, NTT - 1) + 1):
                                        r = g // 4
                                        st_flag = r not in reg_open
                                        reg_open.add(r)
                                        sp_flag = (kt == min(4 * r + 3, NTT - 1)
                                                   and g == kt)
                                        if sp_flag:
                                            reg_open.discard(r)
                                        off = (g - kt) * 128
                                        fns.append(lambda e, kt=kt, h=h, g=g, off=off,
                                                   esl=esl, st_flag=st_flag,
                                                   sp_flag=sp_flag: e.matmul(
                                                       pOT[:, g * 128:(g + 1) * 128],
                                                       vtokB[:, kt * 520 + h * 65:
                                                             kt * 520 + h * 65 + 65],
                                                       expT[esl][:, off:off + 128],
                                                       start=st_flag, stop=sp_flag))
                                    av_done[(h, kt)] = sch.last_inc("t", fns, sPE)
                                    if kt in (3, 7, 11):
                                        r = kt // 4
                                        sch.wait("s", sPE, av_done[(h, kt)])
                                        if r == 0:
                                            sch.wait("s", sVE, v_rd_box[0])
                                        vev = sch.last_inc("s", [
                                            lambda e, pp=pp, sub=sub, r=r: e.copy(
                                                oTsb[sub:sub + 64,
                                                     pp * T + r * 512:pp * T + (r + 1) * 512],
                                                pOT[0:64, r * 512:(r + 1) * 512]),
                                            lambda e, r=r, h=h: e.copy(
                                                DEN[0:1, (h % 2) * T + r * 512:
                                                    (h % 2) * T + (r + 1) * 512],
                                                pOT[64:65, r * 512:(r + 1) * 512])], sSC)
                                        if kt == 11:
                                            v_ev2_prev = vev
                                    if kt == 4 and pending_ev3[0] is not None:
                                        pending_ev3[0]()
                                        pending_ev3[0] = None
                                    if kt == 6 and pending_fin[0] is not None:
                                        pending_fin[0]()
                                        pending_fin[0] = None

                                def mkev3(pp, sub, av_last, h=None):
                                    def emitev3():
                                        sch.wait("s", sPE, av_last)
                                        v_ev3_box[0] = sch.last_inc("s", [
                                            lambda e, pp=pp, sub=sub: e.copy(
                                                oTsb[sub:sub + 64, pp * T + 1536:(pp + 1) * T],
                                                pOT[0:64, 1536:2048]),
                                            lambda e, h=h: e.copy(
                                                DEN[0:1, (h % 2) * T + 1536:
                                                    (h % 2) * T + 2048],
                                                pOT[64:65, 1536:2048])], sSC)
                                    return emitev3
                                pending_ev3[0] = mkev3(pp, sub, av_done[(h, NTT - 1)], h)
                                pending_fin[0] = mk_fin(h, pp)
                            pending_ev3[0]()
                            pending_ev3[0] = None
                            pending_fin[0]()
                            pending_fin[0] = None
                            v_alldma = sGD.val
                        with ExitStack() as std:
                            pDV = [ps("pDVa", [128, 512], std),
                                   ps("pDVb", [128, 512], std)]
                            ev3 = [0, 0]
                            i = 0
                            for pp in range(4):
                                for ch in range(4):
                                    buf = i % 2
                                    sch.wait("t", sGD, dma_after_head[2 * pp + 1])
                                    if i >= 2:
                                        sch.wait("t", sVE, ev3[buf])
                                    vp = sch.last_inc("t", [
                                        lambda e, pp=pp, ch=ch, buf=buf: e.matmul(
                                            pDV[buf][:, :],
                                            cbd2f[64 * (pp % 2):64 * (pp % 2) + 2, :],
                                            RDENs[pp // 2][64 * (pp % 2):64 * (pp % 2) + 2,
                                                           ch * 512:(ch + 1) * 512],
                                            start=True, stop=True)], sPE)
                                    sch.wait("v", sPE, vp)
                                    ev3[buf] = sch.last_inc("v", [
                                        lambda e, pp=pp, ch=ch, buf=buf: e.tensor_mul(
                                            oTsb[:, pp * T + ch * 512:
                                                 pp * T + (ch + 1) * 512],
                                            oTsb[:, pp * T + ch * 512:
                                                 pp * T + (ch + 1) * 512],
                                            pDV[buf][:, :])], sVE)
                                    i += 1
                            v_oT = sVE.val

                    with ExitStack() as stF:
                        _ = None
                        hbufF2 = [[sb(f"hbufF{n}_{i}", [128, 1024], BF16, stF)
                                   for i in range(NPT)] for n in range(2)]
                        wrO = [sb(f"wrO{i}", [128, 512], BF16, stF) for i in range(2)]
                        RW = sb("lnrowsB", [1, 3 * 1024], F32, stF)

                        h3sb = sb("h3sb", [128, NJ * 1024], BF16, stF)
                        NJH = NJ // 2
                        wr2 = [sb(f"wr2_{i}", [128, NJH * 128], BF16, stF)
                               for i in range(2)]
                        lntmp2 = [sb("lntmpB0", [128, 512], F32, stF)]
                        lntmp2.append(lntmp2[0])
                        xedge = sb("xedge", [128, 8], F32, stF)
                        dma_phase_gate()

                        def shift_castF(nb):
                            hb = hbufF2[nb]
                            c0n = nb * 1024
                            sch.wait("v", sVE, sVE.val)
                            fns = []
                            for pt in range(NPT):
                                if pt < 4:
                                    fns.append(lambda e, hb=hb, pt=pt, c0n=c0n: e.tensor_copy(
                                        hb[pt][:, :], XT[pt][:, c0n:c0n + 1024]))
                                elif nb == 0:
                                    fns.append(lambda e, hb=hb, pt=pt: e.memset(
                                        hb[pt][:, 0:1], 0.0))
                                    fns.append(lambda e, hb=hb, pt=pt: e.tensor_copy(
                                        hb[pt][:, 1:1024], XT[pt][:, 0:1023]))
                                else:
                                    fns.append(lambda e, hb=hb, pt=pt: e.tensor_copy(
                                        hb[pt][:, 0:1], xedge[:, pt:pt + 1]))
                                    fns.append(lambda e, hb=hb, pt=pt, c0n=c0n: e.tensor_copy(
                                        hb[pt][:, 1:1024], XT[pt][:, c0n:c0n + 1023]))
                            return sch.chain_inc("v", fns, sVE)

                        W1 = 1024
                        b_sy = RW[0:1, 0:W1]
                        b_s1 = RW[0:1, W1:2 * W1]
                        b_s2 = RW[0:1, 2 * W1:3 * W1]
                        def ln2_pieces(nb):
                            hbufF = hbufF2[nb]
                            c0n = nb * 1024
                            st = {"es": ExitStack()}
                            box = {}

                            def p_start():
                                st["pSum"] = ps("pSumB", [1, 1024], st["es"])
                                st["pSq"] = ps("pSqB", [1, 1024], st["es"])
                                sch.wait("v", sPE, sPE.val)
                                shift_castF(nb)

                            sqF = hbufF2[1 - nb][0]

                            def p_pt(pt):
                                pSum, pSq2 = st["pSum"], st["pSq"]
                                vc = sch.last_inc("v", [lambda e, pt=pt: e.tensor_mul(
                                    sqF[:, :], hbufF[pt][:, :], hbufF[pt][:, :])], sVE)
                                sch.wait("t", sVE, vc)
                                fns = []
                                for ch in range(2):
                                    fns.append(lambda e, ch=ch, pt=pt: e.matmul(
                                        pSum[0:1, ch * 512:(ch + 1) * 512], conesc[:, :],
                                        hbufF[pt][:, ch * 512:(ch + 1) * 512],
                                        start=(pt == 0), stop=(pt == NPT - 1)))
                                    fns.append(lambda e, ch=ch, pt=pt: e.matmul(
                                        pSq2[0:1, ch * 512:(ch + 1) * 512], conesc[:, :],
                                        sqF[:, ch * 512:(ch + 1) * 512],
                                        start=(pt == 0), stop=(pt == NPT - 1)))
                                vp = sch.last_inc("t", fns, sPE)
                                sch.wait("v", sPE, vp)

                            def p_rows():
                                pSum, pSq2 = st["pSum"], st["pSq"]
                                sch.wait("s", sPE, sPE.val)
                                vs = sch.last_inc("s", [
                                    lambda e: e.copy(b_sy, pSum[0:1, :]),
                                    lambda e: e.copy(b_s1, pSq2[0:1, :])], sSC)
                                st["es"].close()
                                sch.wait("v", sSC, vs)
                                sch.chain_inc("v", [
                                    lambda e: e.tensor_scalar_mul(b_sy, b_sy, 1.0 / D),
                                    lambda e: e.tensor_scalar_mul(b_s1, b_s1, 1.0 / D),
                                    lambda e: e.tensor_mul(b_s2, b_sy, b_sy),
                                    lambda e: e.tensor_sub(b_s1, b_s1, b_s2)], sVE)
                                sch.wait("s", sVE, sVE.val)
                                vs = sch.chain_inc("s", [
                                    lambda e: e.activation(b_s1, b_s1, AF.Ln, bias=1e-5),
                                    lambda e: e.activation(b_s1, b_s1, AF.Exp, scale=-0.5)], sSC)
                                sch.wait("v", sSC, vs)
                                st["vv"] = sch.last_inc("v", [
                                    lambda e: e.scalar_tensor_tensor(
                                        b_s2, b_sy, -1.0, b_s1, ALU.mult, ALU.mult)], sVE)

                            def p_apply(ch):
                                if ch == 0:
                                    st["es2"] = ExitStack()
                                    st["pBA"] = [ps("pBAx", [128, 512], st["es2"]),
                                                 ps("pBAy", [128, 512], st["es2"])]
                                    st["pBB"] = [ps("pBBx", [128, 512], st["es2"]),
                                                 ps("pBBy", [128, 512], st["es2"])]
                                pBA, pBB = st["pBA"], st["pBB"]
                                buf = ch % 2
                                sch.wait("t", sVE, st["vv"])
                                fns = [lambda e, buf=buf, ch=ch: e.matmul(
                                    pBA[buf][:, :], conesrf[:, :],
                                    RW[0:1, W1 + ch * 512: W1 + (ch + 1) * 512],
                                    start=True, stop=True),
                                    lambda e, buf=buf, ch=ch: e.matmul(
                                    pBB[buf][:, :], conesrf[:, :],
                                    RW[0:1, 2 * W1 + ch * 512: 2 * W1 + (ch + 1) * 512],
                                    start=True, stop=True)]
                                vp = sch.last_inc("t", fns, sPE)
                                sch.wait("v", sPE, vp)
                                for pt in range(NPT):
                                    cg = c0n + ch * 512
                                    if pt < 4:
                                        fns = [lambda e, pt=pt, cg=cg, buf=buf:
                                               e.tensor_mul(lntmp2[0][:, :],
                                                            XT[pt][:, cg:cg + 512],
                                                            pBA[buf][:, :]),
                                               lambda e, buf=buf: e.tensor_add(
                                            lntmp2[1][:, :], lntmp2[0][:, :],
                                            pBB[buf][:, :])]
                                    elif cg == 1024:
                                        fns = [lambda e, pt=pt, buf=buf:
                                               e.tensor_mul(lntmp2[0][:, 1:512],
                                                            XT[pt][:, 1024:1535],
                                                            pBA[buf][:, 1:512]),
                                               lambda e, pt=pt, buf=buf:
                                               e.tensor_mul(lntmp2[0][:, 0:1],
                                                            xedge[:, pt:pt + 1],
                                                            pBA[buf][:, 0:1]),
                                               lambda e, buf=buf: e.tensor_add(
                                            lntmp2[1][:, :], lntmp2[0][:, :],
                                            pBB[buf][:, :])]
                                    elif cg > 0:
                                        fns = [lambda e, pt=pt, cg=cg, buf=buf:
                                               e.tensor_mul(lntmp2[0][:, :],
                                                            XT[pt][:, cg - 1:cg + 511],
                                                            pBA[buf][:, :]),
                                               lambda e, buf=buf: e.tensor_add(
                                            lntmp2[1][:, :], lntmp2[0][:, :],
                                            pBB[buf][:, :])]
                                    else:
                                        fns = [lambda e, pt=pt, buf=buf:
                                               e.tensor_mul(lntmp2[0][:, 1:512],
                                                            XT[pt][:, 0:511],
                                                            pBA[buf][:, 1:512]),
                                               lambda e: e.memset(
                                            lntmp2[0][:, 0:1], 0.0),
                                               lambda e, buf=buf: e.tensor_add(
                                            lntmp2[1][:, :], lntmp2[0][:, :],
                                            pBB[buf][:, :])]
                                    fns.append(lambda e, pt=pt, ch=ch: e.tensor_scalar(
                                        hbufF[pt][:, ch * 512:(ch + 1) * 512],
                                        lntmp2[1][:, :], lngc[:, pt:pt + 1],
                                        lnbc[:, pt:pt + 1], ALU.mult, ALU.add))
                                    sch.chain_inc("v", fns, sVE)
                                box["v_hp_c%d" % ch] = sVE.val
                                if ch == 1:
                                    box["v_hp"] = sVE.val
                                    st["es2"].close()

                            pieces = [p_start]
                            pieces += [(lambda pt: (lambda: p_pt(pt)))(pt)
                                       for pt in range(NPT)]
                            pieces += [p_rows, lambda: p_apply(0), lambda: p_apply(1)]
                            return pieces, box

                        last_h3 = [0]

                        def emit_w1(nb, v_hp_c0, v_hp):
                            hbufF = hbufF2[nb]
                            e2sb = hbufF2[1 - nb][1]
                            with ExitStack() as stw1:
                                pW1 = [ps("pW1a", [128, 1024], stw1),
                                       ps("pW1b", [128, 1024], stw1),
                                       ps("pW1c", [128, 1024], stw1),
                                       ps("pW1d", [128, 1024], stw1)]
                                pair_free = [Gate(), Gate()]
                                for j in range(NJ):
                                    prslot = j % 2
                                    pa, pg = pW1[2 * prslot], pW1[2 * prslot + 1]
                                    pair_free[prslot].require(sch, "t")
                                    vps = []
                                    for ag in range(2):
                                        wslot = (2 * j + ag) % 4
                                        wc = wload(WN + "w1", 2 * j + ag, wrA[wslot],
                                                   wrA_gate[wslot], sGW[wslot])
                                        sch.wait("t", wc, wc.inc())
                                        if j == 0 and ag == 0:
                                            sch.wait("t", sVE, v_hp_c0)
                                        pdst = pa if ag == 0 else pg
                                        for h2 in range(2):
                                            fns = []
                                            for kt in range(NPT):
                                                fns.append(lambda e, pdst=pdst, wslot=wslot,
                                                           hbufF=hbufF,
                                                           kt=kt, h2=h2: e.matmul(
                                                               pdst[:, h2 * 512:(h2 + 1) * 512],
                                                               wrA[wslot][:, kt * 128:(kt + 1) * 128],
                                                               hbufF[kt][:, h2 * 512:(h2 + 1) * 512],
                                                               start=(kt == 0),
                                                               stop=(kt == NPT - 1)))
                                            if j == 0 and ag == 0 and h2 == 1:
                                                sch.wait("t", sVE, v_hp)
                                            vp = sch.last_inc("t", fns, sPE)
                                        wrA_gate[wslot].set(sPE, vp)
                                        vps.append(vp)
                                    sch.wait("s", sPE, vps[1])
                                    sch.wait("s", sVE, last_h3[0])
                                    vs = sch.last_inc("s", [
                                        lambda e, pg=pg, j=j: e.activation(
                                            e2sb[:, :], pg[:, :], AF.Gelu,
                                            bias=b1gc[:, j:j + 1])], sSC)
                                    sch.wait("v", sSC, vs)
                                    vv4 = sch.last_inc("v", [
                                        lambda e, pa=pa, j=j: e.scalar_tensor_tensor(
                                            h3sb[:, j * 1024:(j + 1) * 1024],
                                            pa[:, :], b1ac[:, j:j + 1],
                                            e2sb[:, :], ALU.add, ALU.mult)], sVE)
                                    pair_free[prslot].set(sVE, vv4)
                                    last_h3[0] = vv4
                            return sVE.val

                        def wload_half(sub, hf, dstbuf, gate, ctr):
                            src2 = wdr[WN + "w2"][sub][hf * NJH * 128:(hf + 1) * NJH * 128]
                            src2 = src2.rearrange("(a p) m -> p a m", p=128)
                            dst = dstbuf[:, :].rearrange("p (a m) -> p a m", m=128)
                            if gate is not None:
                                gate.require(sch, "g")
                            sch.emit("g", lambda e, dst=dst, src2=src2, ctr=ctr:
                                     e.dma_start(out=dst, in_=src2).then_inc(ctr.sem, 16))
                            return ctr

                        def emit_w2(nb, v_h3, inserts=None):
                            c0n = nb * 1024
                            with ExitStack() as stw2:
                                pW2 = [ps("pW2a", [128, 1024], stw2),
                                       ps("pW2b", [128, 1024], stw2)]
                                slot_free2 = [Gate(), Gate()]
                                for mt in range(NPT):
                                    slot = mt % 2
                                    pslot = mt % 2
                                    vp = 0
                                    sch.wait("t", sVE, v_h3)
                                    slot_free2[pslot].require(sch, "t")
                                    for hf in range(2):
                                        wsl = (2 * mt + hf) % 2
                                        wc = wload_half(mt, hf, wr2[wsl],
                                                        wr2_gate[wsl], sG2[wsl])
                                        sch.wait("t", wc, wc.inc())
                                        fns = []
                                        for kj in range(NJH):
                                            kt = hf * NJH + kj
                                            for h2 in range(2):
                                                fns.append(lambda e, pslot=pslot, wsl=wsl,
                                                           kj=kj, kt=kt, h2=h2:
                                                           e.matmul(
                                                               pW2[pslot][:, h2 * 512:(h2 + 1) * 512],
                                                               wr2[wsl][:, kj * 128:(kj + 1) * 128],
                                                               h3sb[:, kt * 1024 + h2 * 512:
                                                                    kt * 1024 + (h2 + 1) * 512],
                                                               start=(kt == 0), stop=(kt == NJ - 1)))
                                        vp = sch.last_inc("t", fns, sPE)
                                        wr2_gate[wsl].set(sPE, vp)
                                    sch.wait("v", sPE, vp)
                                    vv5 = sch.last_inc("v", [
                                        lambda e, pslot=pslot, mt=mt, c0n=c0n:
                                        e.scalar_tensor_tensor(
                                            XT[mt][:, c0n:c0n + 1024], pW2[pslot][:, :],
                                            b2c[:, mt:mt + 1], XT[mt][:, c0n:c0n + 1024],
                                            ALU.add, ALU.add)], sVE)
                                    slot_free2[pslot].set(sVE, vv5)
                                    for f in (inserts or {}).get(mt, []):
                                        f()

                        def emit_wo(inserts=None):
                            with ExitStack() as stw:
                                pW = [ps("pWc", [128, 1024], stw),
                                      ps("pWd", [128, 1024], stw)]
                                slot_free = [Gate(), Gate()]
                                i = 0
                                for nb in range(2):
                                    for mt in range(NPT):
                                        wslot = i % 2
                                        slot = i % 2
                                        wc = wload(WN + "wo", mt, wrO[wslot],
                                                   wrO_gate[wslot], sGO[wslot])
                                        sch.wait("t", wc, wc.inc())
                                        sch.wait("t", sVE, v_oT)
                                        slot_free[slot].require(sch, "t")
                                        fns = []
                                        for kt in range(4):
                                            for h2 in range(2):
                                                fns.append(lambda e, slot=slot, wslot=wslot,
                                                           kt=kt, h2=h2, nb=nb: e.matmul(
                                                               pW[slot][:, h2 * 512:(h2 + 1) * 512],
                                                               wrO[wslot][:, kt * 128:(kt + 1) * 128],
                                                               oTsb[:, kt * T + nb * 1024 + h2 * 512:
                                                                    kt * T + nb * 1024 + (h2 + 1) * 512],
                                                               start=(kt == 0), stop=(kt == 3)))
                                        vp = sch.last_inc("t", fns, sPE)
                                        wrO_gate[wslot].set(sPE, vp)
                                        sch.wait("v", sPE, vp)
                                        if l == 0:
                                            sch.wait("v", sLD, v_spill)
                                        vv = sch.last_inc("v", [
                                            lambda e, slot=slot, mt=mt, nb=nb: e.tensor_add(
                                                XT[mt][:, nb * 1024:(nb + 1) * 1024],
                                                XT[mt][:, nb * 1024:(nb + 1) * 1024],
                                                pW[slot][:, :])], sVE)
                                        slot_free[slot].set(sVE, vv)
                                        for f in (inserts or {}).get((nb, mt), []):
                                            f()
                                        i += 1

                        pieces0, box0 = ln2_pieces(0)
                        ins0 = {(1, 0): [pieces0[0]],
                                (1, 1): [pieces0[1], pieces0[2]],
                                (1, 2): [pieces0[3], pieces0[4]],
                                (1, 3): [pieces0[5], pieces0[6]],
                                (1, 4): [pieces0[7], pieces0[8]],
                                (1, 5): [pieces0[9]],
                                (1, 6): [pieces0[10]],
                                (1, 7): [pieces0[11]]}
                        emit_wo(inserts=ins0)
                        sch.chain_inc("v", [
                            (lambda pt2: (lambda e: e.tensor_copy(
                                xedge[:, pt2:pt2 + 1], XT[pt2][:, 1023:1024])))(pt2)
                            for pt2 in range(NPT)], sVE)
                        v_h3_0 = emit_w1(0, box0["v_hp_c0"], box0["v_hp"])
                        pieces1, box1 = ln2_pieces(1)
                        ins = {0: [pieces1[0]],
                               1: [pieces1[1], pieces1[2]],
                               2: [pieces1[3], pieces1[4]],
                               3: [pieces1[5], pieces1[6]],
                               4: [pieces1[7], pieces1[8]],
                               5: [pieces1[9]],
                               6: [pieces1[10]],
                               7: [pieces1[11]]}
                        emit_w2(0, v_h3_0, inserts=ins)
                        v_h3_1 = emit_w1(1, box1["v_hp_c0"], box1["v_hp"])
                        emit_w2(1, v_h3_1)

            for l in range(NLAY):
                emit_layerB(l)
            stPB.close()

        if STAGE >= 3:
            with ExitStack() as stf:
                x1r = [sb(f"x1r{i}", [128, T], F32, stf) for i in range(NPT)]
                R = sb("lnrows", [1, 5 * T], F32, stf)
                lnbf = sb("lnbf", [128, T], BF16, stf)
                lnsq = sb("lnsq", [128, T], BF16, stf)
                otok = sb("otok", [128, 4 * D], F32, stf)
                r_sy = R[0:1, 0:T]
                r_sq = R[0:1, T:2 * T]
                r_t = R[0:1, 2 * T:3 * T]
                r_A = R[0:1, 3 * T:4 * T]
                r_B = R[0:1, 4 * T:5 * T]
                dma_phase_gate()
                sX1 = [sem(f"sX1{i}", 16) for i in range(NPT)]
                for pt in range(NPT):
                    sch.emit("y", lambda e, pt=pt: e.dma_start(
                        out=x1r[pt][:, :], in_=x1s[:, pt * T:(pt + 1) * T])
                        .then_inc(sX1[pt].sem, 16))
                    sX1[pt].inc()
                sch.wait("v", sPE, sPE.val)
                with ExitStack() as stp:
                    pSum = ps("pSum", [1, T], stp)
                    pSq = ps("pSq", [1, T], stp)
                    for pt in range(NPT):
                        sch.wait("v", sX1[pt], 16)
                        fns = [lambda e, pt=pt: e.tensor_add(
                                   XT[pt][:, :], XT[pt][:, :], x1r[pt][:, :]),
                               lambda e, pt=pt: e.tensor_copy(lnbf[:, :], XT[pt][:, :]),
                               lambda e: e.tensor_mul(lnsq[:, :], lnbf[:, :], lnbf[:, :])]
                        vc = sch.chain_inc("v", fns, sVE)
                        sch.wait("t", sVE, vc)
                        fns = []
                        for ch in range(4):
                            fns.append(lambda e, ch=ch, pt=pt: e.matmul(
                                pSum[0:1, ch * 512:(ch + 1) * 512], conesc[:, :],
                                lnbf[:, ch * 512:(ch + 1) * 512],
                                start=(pt == 0), stop=(pt == NPT - 1)))
                            fns.append(lambda e, ch=ch, pt=pt: e.matmul(
                                pSq[0:1, ch * 512:(ch + 1) * 512], conesc[:, :],
                                lnsq[:, ch * 512:(ch + 1) * 512],
                                start=(pt == 0), stop=(pt == NPT - 1)))
                        vp = sch.last_inc("t", fns, sPE)
                        sch.wait("v", sPE, vp)
                    sch.wait("s", sPE, sPE.val)
                    vs = sch.last_inc("s", [
                        lambda e: e.copy(r_sy, pSum[0:1, :]),
                        lambda e: e.copy(r_sq, pSq[0:1, :])], sSC)
                sch.wait("v", sSC, vs)
                vv = sch.chain_inc("v", [
                    lambda e: e.tensor_scalar_mul(r_sy, r_sy, 1.0 / D),
                    lambda e: e.tensor_scalar_mul(r_sq, r_sq, 1.0 / D),
                    lambda e: e.tensor_mul(r_t, r_sy, r_sy),
                    lambda e: e.tensor_sub(r_t, r_sq, r_t)], sVE)
                sch.wait("s", sVE, vv)
                vs = sch.chain_inc("s", [
                    lambda e: e.activation(r_t, r_t, AF.Ln, bias=1e-5),
                    lambda e: e.activation(r_A, r_t, AF.Exp, scale=-0.5)], sSC)
                sch.wait("v", sSC, vs)
                vv = sch.last_inc("v", [lambda e: e.scalar_tensor_tensor(
                    r_B, r_sy, -1.0, r_A, ALU.mult, ALU.mult)], sVE)
                v_ch = [0, 0, 0, 0]
                with ExitStack() as stp:
                    pBA = [ps("pBA0", [128, 512], stp), ps("pBA1", [128, 512], stp)]
                    pBB = [ps("pBB0", [128, 512], stp), ps("pBB1", [128, 512], stp)]
                    tmpf = sb("lntmp", [128, 512], F32, stp)
                    ev2 = [0, 0]
                    pO = [ps("pO0", [128, 512], stp), ps("pO1", [128, 512], stp)]
                    ev = [0, 0]
                    idx = 0
                    sSTs = [sem(f"sSTs{i}", 16) for i in range(4)]
                    yr = y_out.rearrange("(tt p) d -> tt p d", p=128)
                    for ch in range(4):
                        buf = ch % 2
                        sch.wait("t", sVE, vv)
                        if ch >= 2:
                            sch.wait("t", sVE, ev2[buf])
                        fns = [lambda e, buf=buf, ch=ch: e.matmul(
                            pBA[buf][:, :], conesrf[:, :],
                            R[0:1, 3 * T + ch * 512: 3 * T + (ch + 1) * 512],
                            start=True, stop=True),
                            lambda e, buf=buf, ch=ch: e.matmul(
                            pBB[buf][:, :], conesrf[:, :],
                            R[0:1, 4 * T + ch * 512: 4 * T + (ch + 1) * 512],
                            start=True, stop=True)]
                        vp = sch.last_inc("t", fns, sPE)
                        sch.wait("v", sPE, vp)
                        fns = []
                        for pt in range(NPT):
                            fns.append(lambda e, pt=pt, ch=ch, buf=buf: e.tensor_mul(
                                tmpf[:, :], XT[pt][:, ch * 512:(ch + 1) * 512], pBA[buf][:, :]))
                            fns.append(lambda e, pt=pt, ch=ch, buf=buf: e.tensor_add(
                                XT[pt][:, ch * 512:(ch + 1) * 512], tmpf[:, :], pBB[buf][:, :]))
                        ev2[buf] = sch.chain_inc("v", fns, sVE)
                        v_ch[ch] = ev2[buf]
                        for tt in range(ch * 4, (ch + 1) * 4):
                            ts = tt % 4
                            for dg in range(2):
                                tbuf = idx % 2
                                sch.wait("t", sVE, v_ch[ch])
                                if idx >= 2:
                                    sch.wait("t", sVE, ev[tbuf])
                                fns = []
                                for j in range(4):
                                    dt_i = dg * 4 + j
                                    fns.append(lambda e, tbuf=tbuf, j=j, tt=tt, dt_i=dt_i:
                                               e.matmul(
                                                   pO[tbuf][:, j * 128:(j + 1) * 128],
                                                   XT[dt_i][:, tt * 128:(tt + 1) * 128],
                                                   cid32[:, :], start=True, stop=True))
                                vp = sch.last_inc("t", fns, sPE)
                                sch.wait("v", sPE, vp)
                                if tt >= 4 and dg == 0:
                                    sch.wait("v", sSTs[ts], sSTs[ts].val)
                                fns = [lambda e, tbuf=tbuf, j=j, ts=ts, dt_i=dg * 4 + j:
                                       e.tensor_copy(
                                           otok[:, ts * D + dt_i * 128:
                                                ts * D + dt_i * 128 + 128],
                                           pO[tbuf][:, j * 128:(j + 1) * 128])
                                       for j in range(4)]
                                ev[tbuf] = sch.last_inc("v", fns, sVE)
                                idx += 1
                            vv6 = sVE.val

                            def stf_(e, tt=tt, ts=ts, vv6=vv6):
                                e.wait_ge(sVE.sem, vv6)
                                return e.dma_start(
                                    out=yr[tt],
                                    in_=otok[:, ts * D:(ts + 1) * D]).then_inc(sSTs[ts].sem, 16)
                            sch.emit("y", stf_)
                            sSTs[ts].inc()
                    for i in range(4):
                        sch.emit("y", lambda e, s=sSTs[i].sem, v=sSTs[i].val:
                                 e.wait_ge(s, v))

        with nc.Block() as blk:
            @blk.tensor
            def _(t):
                for fn in sch.ops["t"]:
                    fn(t)

            @blk.vector
            def _(v):
                for fn in sch.ops["v"]:
                    fn(v)

            @blk.scalar
            def _(s):
                for fn in sch.ops["s"]:
                    fn(s)

            @blk.gpsimd
            def _(g):
                for fn in sch.ops["g"]:
                    fn(g)

            @blk.sync
            def _(y):
                for fn in sch.ops["y"]:
                    fn(y)

    return nc


def kernel(**inputs):
    x = np.ascontiguousarray(inputs["L0_attn_x"]).astype(np.float32, copy=False)
    w = _prep_weights(inputs)
    wspecs = {nm: arr.shape for nm, arr in w.items()}
    nc = build_nc(wspecs)
    in_maps = []
    for c in range(8):
        im = {"x_in": np.ascontiguousarray(x[c])}
        for nm, arr in _CONSTS.items():
            im[nm] = arr
        for nm, arr in w.items():
            im[nm] = arr
        in_maps.append(im)
    res = run_bass_kernel_spmd(nc, in_maps, core_ids=list(range(8)))
    return np.stack([res.results[c]["y_out"] for c in range(8)], axis=0)


# version 12
# speedup vs baseline: 1.0000x; 1.0000x over previous
import os
import numpy as np
import ml_dtypes
from contextlib import ExitStack

import concourse.bass as bass
import concourse.mybir as mybir
from concourse.bass_utils import run_bass_kernel_spmd

F32 = mybir.dt.float32
BF16 = mybir.dt.bfloat16
AF = mybir.ActivationFunctionType
ALU = mybir.AluOpType
AX = mybir.AxisListType
BF = ml_dtypes.bfloat16

T = 2048
D = 1024
NPT = D // 128
NTT = T // 128
HID = 2730
NJ = 22
SC = 512
NSC = T // SC
CPS = SC // 128
STAGE = int(os.environ.get("K_STAGE", "3"))
PH = int(os.environ.get("K_PH", "99"))
NCHL = int(os.environ.get("K_NCHL", "99"))
SOP = int(os.environ.get("K_SOP", "63"))
NLAY = {1: 0, 2: 1, 3: 2}[STAGE]


def _host_consts():
    c = {}
    c["c_idf32"] = np.eye(128, dtype=np.float32)
    c["c_idbf"] = np.eye(128, dtype=BF)
    tri = (np.arange(128)[:, None] <= np.arange(128)[None, :]).astype(np.float32)
    c["c_maskA"] = np.tile(tri, (1, 8)).astype(BF)
    big = np.float32(-1e30)
    c["c_mko"] = np.where(np.arange(128)[:, None] <= np.arange(128)[None, :], 0.0, big).astype(BF)
    c["c_mkp"] = np.where(np.arange(128)[:, None] >= np.arange(128)[None, :], 0.0, big).astype(BF)
    c["c_ones_col_bf"] = np.ones((128, 1), dtype=BF)
    c["c_ones_row_f32"] = np.ones((1, 128), dtype=np.float32)
    bd2 = np.zeros((2, 128), dtype=np.float32)
    bd2[0, 0:64] = 1.0
    bd2[1, 64:128] = 1.0
    bd2r = np.zeros((128, 128), dtype=np.float32)
    bd2r[0:2] = bd2
    bd2r[64:66] = bd2
    c["c_bd2_f32"] = bd2r
    c["c_bd2T_bf"] = np.ascontiguousarray(bd2.T).astype(BF)
    P1 = np.zeros((64, 64), dtype=np.float32)
    for dp in range(32):
        P1[dp + 32, dp] = -1.0
    for dp in range(32, 64):
        P1[dp - 32, dp] = 1.0
    Pr = np.zeros((128, 128), dtype=np.float32)
    Pr[0:64, 0:64] = P1
    Pr[64:128, 64:128] = P1
    c["c_prot"] = Pr.astype(BF)
    c["c_eps"] = np.tile(np.array([[1e-5, 1e-24]], dtype=np.float32), (128, 1))
    bdw = np.zeros((128, 32), dtype=np.float32)
    bdw[0:64, 15] = 1.0
    bdw[64:128, 16] = 1.0
    c["c_bdw"] = bdw.astype(BF)
    bsel = np.zeros((16, 8 * 128), dtype=np.float32)
    for mtq in range(8):
        bsel[2 * mtq, mtq * 128:mtq * 128 + 64] = 1.0
        bsel[2 * mtq + 1, mtq * 128 + 64:mtq * 128 + 128] = 1.0
    c["c_bsel"] = bsel.astype(BF)
    DIM_HEAD, WINDOW = 64, 512
    half = np.arange(0, DIM_HEAD, 2, dtype=np.float64)
    inv_freq = 1.0 / 10000.0 ** (half / DIM_HEAD)
    t = np.arange(T, dtype=np.float64)
    freqs = np.concatenate([t[:, None] * inv_freq[None, :]] * 2, axis=-1)
    base = (half + 0.4 * DIM_HEAD) / (1.4 * DIM_HEAD)
    power = (t - T // 2) / float(WINDOW // 2)
    scale = np.concatenate([base[None, :] ** power[:, None]] * 2, axis=-1)
    cos, sin = np.cos(freqs), np.sin(freqs)
    QKS = 8.0
    for nm, tab in (("c_csq", QKS * (cos * scale).T), ("c_ssq", QKS * (sin * scale).T),
                    ("c_csk", (cos / scale).T), ("c_ssk", (sin / scale).T)):
        c[nm] = np.concatenate([tab, tab], axis=0).astype(np.float32).astype(BF)
    return c


_CONSTS = _host_consts()
DBGNAMES = {}


def _prep_weights(inp):
    w = {}
    asc = np.ascontiguousarray

    def tile_cols(M):
        K, Mf = M.shape
        nmt = -(-Mf // 128)
        P = np.zeros((nmt, K, 128), dtype=np.float32)
        for i in range(nmt):
            blk = M[:, i * 128:(i + 1) * 128]
            P[i, :, :blk.shape[1]] = blk
        return asc(P)

    def col_pt(v):
        return asc(v.reshape(-1, 128).T.astype(np.float32))

    for nm in ("Wq", "Wk", "Wv", "Wo"):
        w["A_" + nm] = tile_cols(asc(inp[f"L0_attn_{nm}"].astype(np.float32).T))
    for nm in ("bq", "bk", "bv", "bo"):
        w["A_" + nm] = col_pt(inp[f"L0_attn_{nm}"].astype(np.float32))
    for l in range(2):
        w[f"B{l}_wqkv"] = tile_cols(inp["L0_ffn_wqkv"][l].astype(np.float32))
        w[f"B{l}_wo"] = tile_cols(inp["L0_ffn_wo"][l].astype(np.float32))
        w1 = inp["L0_ffn_w1"][l].astype(np.float32)
        a_t = tile_cols(w1[:, :HID])
        g_t = tile_cols(w1[:, HID:])
        inter = np.empty((2 * NJ, D, 128), dtype=np.float32)
        inter[0::2] = a_t
        inter[1::2] = g_t
        w[f"B{l}_w1"] = asc(inter)
        w2p = np.zeros((NJ * 128, D), dtype=np.float32)
        w2p[:HID] = inp["L0_ffn_w2"][l].astype(np.float32)
        w[f"B{l}_w2"] = tile_cols(w2p)
        b1 = inp["L0_ffn_b1"][l].astype(np.float32)
        b1a = np.zeros(NJ * 128, dtype=np.float32)
        b1a[:HID] = b1[:HID]
        b1g = np.zeros(NJ * 128, dtype=np.float32)
        b1g[:HID] = b1[HID:]
        w[f"B{l}_b1a"] = col_pt(b1a)
        w[f"B{l}_b1g"] = col_pt(b1g)
        w[f"B{l}_b2"] = col_pt(inp["L0_ffn_b2"][l].astype(np.float32))
        w[f"B{l}_lng"] = col_pt(inp["L0_ffn_ln_g"][l].astype(np.float32))
        w[f"B{l}_lnb"] = col_pt(inp["L0_ffn_ln_b"][l].astype(np.float32))
        w[f"B{l}_qs"] = asc(inp["L0_ffn_q_scale"][l].astype(np.float32)[None, :])
        w[f"B{l}_ks"] = asc(inp["L0_ffn_k_scale"][l].astype(np.float32)[None, :])
    return w


class Ctr:
    def __init__(self, sem, step):
        self.sem = sem
        self.step = step
        self.val = 0

    def inc(self):
        self.val += self.step
        return self.val


class Sched:
    ENGS = ("t", "v", "s", "g", "y")

    def __init__(self):
        self.ops = {e: [] for e in self.ENGS}
        self.wmax = {e: {} for e in self.ENGS}

    def emit(self, eng, fn):
        self.ops[eng].append(fn)

    def wait(self, eng, ctr, val=None):
        v = ctr.val if val is None else val
        if v <= 0:
            return
        key = id(ctr.sem)
        if self.wmax[eng].get(key, -1) >= v:
            return
        self.wmax[eng][key] = v
        self.ops[eng].append(lambda e, s=ctr.sem, v=v: e.wait_ge(s, v))

    def chain_inc(self, eng, fns, ctr):
        self.wait(eng, ctr, ctr.val)
        v = None
        for f in fns[:-1]:
            v = self.last_inc(eng, [f], ctr)
            self.wait(eng, ctr, v)
        return self.last_inc(eng, [fns[-1]], ctr)

    def last_inc(self, eng, fns, ctr):
        assert fns
        for f in fns[:-1]:
            self.ops[eng].append(f)
        self.ops[eng].append(lambda e, f=fns[-1]: f(e).then_inc(ctr.sem, ctr.step))
        return ctr.inc()


class Gate:
    def __init__(self):
        self.ctr = None
        self.val = 0

    def set(self, ctr, val):
        self.ctr = ctr
        self.val = val

    def require(self, sch, eng):
        if self.ctr is not None and self.val > 0:
            sch.wait(eng, self.ctr, self.val)


def build_nc(wspecs):
    nc = bass.Bass("TRN2", target_bir_lowering=False)

    x_in = nc.dram_tensor("x_in", [T, D], F32, kind="ExternalInput").ap()
    y_out = nc.dram_tensor("y_out", [T, D], F32, kind="ExternalOutput").ap()
    x1s = nc.dram_tensor("x1s", [128, NPT * T], F32, kind="Internal").ap()

    cons = {}
    for nm, arr in _CONSTS.items():
        dt = BF16 if arr.dtype == BF else F32
        cons[nm] = nc.dram_tensor(nm, list(arr.shape), dt, kind="ExternalInput").ap()
    wdr = {}
    for nm, shape in wspecs.items():
        wdr[nm] = nc.dram_tensor(nm, list(shape), F32, kind="ExternalInput").ap()

    sch = Sched()
    ctx = ExitStack()

    _uid = [0]

    def _un(name):
        _uid[0] += 1
        un = f"{name}_{_uid[0]}"
        DBGNAMES.setdefault(name, []).append(un)
        return un

    def sb(name, shape, dt, st=None):
        return (st or ctx).enter_context(nc.sbuf_tensor(_un(name), shape, dt))

    def ps(name, shape, st=None):
        return (st or ctx).enter_context(nc.psum_tensor(_un(name), shape, F32))

    def sem(name, step=1):
        return Ctr(ctx.enter_context(nc.semaphore(name)), step)

    with ctx:
        XT = [sb(f"XT{i}", [128, T], F32) for i in range(NPT)]
        cid32 = sb("cid32", [128, 128], F32)
        cidbf = sb("cidbf", [128, 128], BF16)
        conesc = sb("conesc", [128, 1], BF16)
        conesrf = sb("conesrf", [1, 128], F32)
        cbd2f = sb("cbd2f", [128, 128], F32)
        cbd2t = sb("cbd2t", [128, 2], BF16)
        cprot = sb("cprot", [128, 128], BF16)
        cmko = sb("cmko", [128, 128], BF16)
        cmkp = sb("cmkp", [128, 128], BF16)
        cepsc = sb("cepsc", [128, 2], F32)
        cbdw = sb("cbdw", [128, 32], BF16)
        cbsel = sb("cbsel", [16, 1024], BF16)
        nc.const_aps.register(F32, 1e-5, cepsc[:, 0:1])
        nc.const_aps.register(F32, 1e-24, cepsc[:, 1:2])
        wrA = [sb(f"wrA{i}", [128, 1024], BF16) for i in range(4)]

        sLD = sem("sLD", 16)
        sGD = sem("sGD", 16)
        sGW = [sem(f"sGW{i}", 16) for i in range(4)]
        sGO = [sem(f"sGO{i}", 16) for i in range(4)]
        sG2 = [sem("sG20", 16), sem("sG21", 16)]
        sPE = sem("sPE", 1)
        sVE = sem("sVE", 1)
        sSC = sem("sSC", 1)
        sST = sem("sST", 16)

        for nm, dst in (("c_idf32", cid32), ("c_idbf", cidbf),
                        ("c_ones_col_bf", conesc), ("c_ones_row_f32", conesrf),
                        ("c_bd2_f32", cbd2f), ("c_bd2T_bf", cbd2t),
                        ("c_prot", cprot),
                        ("c_mko", cmko), ("c_mkp", cmkp), ("c_eps", cepsc),
                        ("c_bdw", cbdw), ("c_bsel", cbsel)):
            sch.emit("y", lambda e, nm=nm, dst=dst:
                     e.dma_start(out=dst[:, :], in_=cons[nm]).then_inc(sLD.sem, 16))
            sLD.inc()

        wrA_gate = [Gate(), Gate(), Gate(), Gate()]
        wrO_gate = [Gate(), Gate(), Gate(), Gate()]
        wr2_gate = [Gate(), Gate()]

        def wload(nm, sub, dstbuf, gate=None, ctr=None, cols=None):
            src = wdr[nm][sub].rearrange("(a p) m -> p a m", p=128)
            dst = (dstbuf[:, 0:cols] if cols else dstbuf[:, :]).rearrange(
                "p (a m) -> p a m", m=128)
            ctr = ctr or sGD
            if gate is not None:
                gate.require(sch, "g")
            sch.emit("g", lambda e, dst=dst, src=src, ctr=ctr:
                     e.dma_start(out=dst, in_=src).then_inc(ctr.sem, 16))
            return ctr


        def dma_phase_gate():
            for qq in ("y", "g"):
                sch.wait(qq, sPE, sPE.val)
                sch.wait(qq, sVE, sVE.val)
                sch.wait(qq, sSC, sSC.val)

        def colload(nm, dst, ctr=None):
            ctr = ctr or sLD
            sch.emit("y", lambda e, nm=nm, dst=dst, ctr=ctr:
                     e.dma_start(out=dst, in_=wdr[nm]).then_inc(ctr.sem, 16))
            return ctr.inc()

        def transpose_batch(srcs, dstw, pbufs, ev):
            nb = len(srcs) // 4
            for b in range(nb):
                buf = b % 2
                ev[buf].require(sch, "t")
                fns = []
                for j in range(4):
                    sap = srcs[b * 4 + j]
                    fns.append(lambda e, buf=buf, j=j, sap=sap: e.matmul(
                        pbufs[buf][:, j * 128:(j + 1) * 128], sap, cidbf[:, :],
                        start=True, stop=True))
                vp = sch.last_inc("t", fns, sPE)
                sch.wait("s", sPE, vp)
                vs = sch.last_inc("s", [lambda e, buf=buf, b=b: dstw[b](e, pbufs[buf])], sSC)
                ev[buf].set(sSC, vs)
            return sSC.val

        if STAGE >= 2:
            stPB = ExitStack()
            tabq = [sb("tcsq", [128, T], BF16, stPB), sb("tssq", [128, T], BF16, stPB)]
            tabk = [sb("tcsk", [128, T], BF16, stPB), sb("tssk", [128, T], BF16, stPB)]
            LCON = []
            sTB = sem("sTB", 16)
            sCL = [sem("sCL0", 16), sem("sCL1", 16)]
            for l in range(NLAY):
                cc = {}
                cc["qscol"] = sb("qscol", [128, 1], F32, stPB)
                cc["kscol"] = sb("kscol", [128, 1], F32, stPB)
                SROW = sb("SROW", [1, 256], F32, stPB)
                cc["qsrow"] = SROW[0:1, 0:64]
                cc["ksrow"] = SROW[0:1, 64:128]
                cc["m0r"] = SROW[0:1, 128:129]
                cc["m0r2"] = SROW[0:1, 129:130]
                cc["negm0"] = sb("negm0", [128, 1], F32, stPB)
                cc["lngc"] = sb("lngc", [128, 8], F32, stPB)
                cc["lnbc"] = sb("lnbc", [128, 8], F32, stPB)
                cc["b1ac"] = sb("b1ac", [128, NJ], F32, stPB)
                cc["b1gc"] = sb("b1gc", [128, NJ], F32, stPB)
                cc["b2c"] = sb("b2c", [128, 8], F32, stPB)
                LCON.append(cc)

        with ExitStack() as st0:
            xtok = sb("xtok", [128, NTT * D], F32, st0)
            xr = x_in.rearrange("(tt p) d -> p tt d", p=128)
            xtok3 = xtok[:, :].rearrange("p (tt d) -> p tt d", d=D)
            sch.emit("y", lambda e: e.dma_start(out=xtok3, in_=xr).then_inc(sLD.sem, 16))
            sLD.inc()
            sch.wait("t", sLD, sLD.val)
            pT = [ps("pT0", [128, 512], st0), ps("pT1", [128, 512], st0)]
            ev = [0, 0]
            idx = 0
            for tt in range(NTT):
                for dg in range(2):
                    buf = idx % 2
                    if idx >= 2:
                        sch.wait("t", sVE, ev[buf])
                    fns = []
                    for j in range(4):
                        dt_i = dg * 4 + j
                        fns.append(lambda e, buf=buf, j=j, tt=tt, dt_i=dt_i: e.matmul(
                            pT[buf][:, j * 128:(j + 1) * 128],
                            xtok[:, tt * D + dt_i * 128: tt * D + dt_i * 128 + 128],
                            cid32[:, :], start=True, stop=True))
                    vp = sch.last_inc("t", fns, sPE)
                    sch.wait("v", sPE, vp)
                    fns = [lambda e, buf=buf, j=j, tt=tt, dt_i=dg * 4 + j: e.tensor_copy(
                        XT[dt_i][:, tt * 128:(tt + 1) * 128],
                        pT[buf][:, j * 128:(j + 1) * 128]) for j in range(4)]
                    ev[buf] = sch.last_inc("v", fns, sVE)
                    idx += 1
        v_xt_ready = sVE.val

        if STAGE >= 2:
            for nm, dst in (("c_csq", tabq[0]), ("c_ssq", tabq[1]),
                            ("c_csk", tabk[0]), ("c_ssk", tabk[1])):
                sch.emit("y", lambda e, nm=nm, dst=dst:
                         e.dma_start(out=dst[:, :], in_=cons[nm]).then_inc(sTB.sem, 16))
                sTB.inc()
            v_tabs_all = sTB.val
            for l in range(NLAY):
                WN = f"B{l}_"
                cc = LCON[l]
                m0r = cc["m0r"]
                m0r2 = cc["m0r2"]
                for nm, dst in ((WN + "lng", cc["lngc"][:, :]), (WN + "lnb", cc["lnbc"][:, :]),
                                (WN + "b1a", cc["b1ac"][:, :]), (WN + "b1g", cc["b1gc"][:, :]),
                                (WN + "b2", cc["b2c"][:, :]), (WN + "qs", cc["qsrow"]),
                                (WN + "ks", cc["ksrow"])):
                    colload(nm, dst, sCL[l])
                for srcn, dst in ((WN + "qs", cc["qscol"]), (WN + "ks", cc["kscol"])):
                    for hb in range(2):
                        sch.emit("y", lambda e, srcn=srcn, dst=dst, hb=hb, l=l: e.dma_start(
                            out=dst[hb * 64:(hb + 1) * 64, 0:1],
                            in_=wdr[srcn].rearrange("a b -> b a")).then_inc(sCL[l].sem, 16))
                        sCL[l].inc()
                cc["ctr"] = sCL[l]
                cc["v_bias"] = sCL[l].val
                sch.wait("v", sCL[l], cc["v_bias"])
                vv = sch.chain_inc("v", [
                    lambda e, m0r=m0r, q=cc["qsrow"]: e.reduce_max(
                        m0r, q, AX.X, apply_absolute_value=True),
                    lambda e, m0r2=m0r2, ks=cc["ksrow"]: e.reduce_max(
                        m0r2, ks, AX.X, apply_absolute_value=True),
                    lambda e, m0r=m0r, m0r2=m0r2: e.tensor_mul(m0r, m0r, m0r2),
                    lambda e, m0r=m0r: e.tensor_scalar_mul(m0r, m0r, -8.0)], sVE)
                with ExitStack() as stm:
                    pM = ps("pM0", [128, 1], stm)
                    sch.wait("t", sVE, vv)
                    if l > 0:
                        sch.wait("t", sSC, LCON[l - 1]["v_m0"])
                    vp = sch.last_inc("t", [lambda e, m0r=m0r: e.matmul(
                        pM[:, :], conesrf[:, :], m0r, start=True, stop=True)], sPE)
                    sch.wait("s", sPE, vp)
                    cc["v_m0"] = sch.last_inc("s", [lambda e, n=cc["negm0"]: e.copy(
                        n[:, :], pM[:, :])], sSC)


        def emit_ln_plain(spill=False):
            sch.wait("v", sPE, sPE.val)
            with ExitStack() as stl:
                R = sb("lnrows", [1, 5 * T], F32, stl)
                lnbf = sb("lnbf", [128, T], BF16, stl)
                lnsq = sb("lnsq", [128, T], BF16, stl)
                r_sy = R[0:1, 0:T]
                r_sq = R[0:1, T:2 * T]
                r_t = R[0:1, 2 * T:3 * T]
                r_A = R[0:1, 3 * T:4 * T]
                r_B = R[0:1, 4 * T:5 * T]
                with ExitStack() as stp:
                    pSum = ps("pSum", [1, T], stp)
                    pSq = ps("pSq", [1, T], stp)
                    for pt in range(NPT):
                        fns = [lambda e, pt=pt: e.tensor_copy(lnbf[:, :], XT[pt][:, :]),
                               lambda e: e.tensor_mul(lnsq[:, :], lnbf[:, :], lnbf[:, :])]
                        vc = sch.chain_inc("v", fns, sVE)
                        sch.wait("t", sVE, vc)
                        fns = []
                        for ch in range(4):
                            fns.append(lambda e, ch=ch, pt=pt: e.matmul(
                                pSum[0:1, ch * 512:(ch + 1) * 512], conesc[:, :],
                                lnbf[:, ch * 512:(ch + 1) * 512],
                                start=(pt == 0), stop=(pt == NPT - 1)))
                            fns.append(lambda e, ch=ch, pt=pt: e.matmul(
                                pSq[0:1, ch * 512:(ch + 1) * 512], conesc[:, :],
                                lnsq[:, ch * 512:(ch + 1) * 512],
                                start=(pt == 0), stop=(pt == NPT - 1)))
                        vp = sch.last_inc("t", fns, sPE)
                        sch.wait("v", sPE, vp)
                    sch.wait("s", sPE, sPE.val)
                    vs = sch.last_inc("s", [
                        lambda e: e.copy(r_sy, pSum[0:1, :]),
                        lambda e: e.copy(r_sq, pSq[0:1, :])], sSC)
                sch.wait("v", sSC, vs)
                vv = sch.chain_inc("v", [
                    lambda e: e.tensor_scalar_mul(r_sy, r_sy, 1.0 / D),
                    lambda e: e.tensor_scalar_mul(r_sq, r_sq, 1.0 / D),
                    lambda e: e.tensor_mul(r_t, r_sy, r_sy),
                    lambda e: e.tensor_sub(r_t, r_sq, r_t)], sVE)
                sch.wait("s", sVE, vv)
                vs = sch.chain_inc("s", [
                    lambda e: e.activation(r_t, r_t, AF.Ln, bias=1e-5),
                    lambda e: e.activation(r_A, r_t, AF.Exp, scale=-0.5)], sSC)
                sch.wait("v", sSC, vs)
                vv = sch.last_inc("v", [lambda e: e.scalar_tensor_tensor(
                    r_B, r_sy, -1.0, r_A, ALU.mult, ALU.mult)], sVE)
                with ExitStack() as stp:
                    pBA = [ps("pBA0", [128, 512], stp), ps("pBA1", [128, 512], stp)]
                    pBB = [ps("pBB0", [128, 512], stp), ps("pBB1", [128, 512], stp)]
                    tmpf = sb("lntmp", [128, 512], F32, stp)
                    ev2 = [0, 0]
                    for ch in range(4):
                        buf = ch % 2
                        sch.wait("t", sVE, vv)
                        if ch >= 2:
                            sch.wait("t", sVE, ev2[buf])
                        fns = [lambda e, buf=buf, ch=ch: e.matmul(
                            pBA[buf][:, :], conesrf[:, :],
                            R[0:1, 3 * T + ch * 512: 3 * T + (ch + 1) * 512],
                            start=True, stop=True),
                            lambda e, buf=buf, ch=ch: e.matmul(
                            pBB[buf][:, :], conesrf[:, :],
                            R[0:1, 4 * T + ch * 512: 4 * T + (ch + 1) * 512],
                            start=True, stop=True)]
                        vp = sch.last_inc("t", fns, sPE)
                        sch.wait("v", sPE, vp)
                        fns = []
                        for pt in range(NPT):
                            fns.append(lambda e, pt=pt, ch=ch, buf=buf: e.tensor_mul(
                                tmpf[:, :], XT[pt][:, ch * 512:(ch + 1) * 512], pBA[buf][:, :]))
                            fns.append(lambda e, pt=pt, ch=ch, buf=buf: e.tensor_add(
                                XT[pt][:, ch * 512:(ch + 1) * 512], tmpf[:, :], pBB[buf][:, :]))
                        ev2[buf] = sch.chain_inc("v", fns, sVE)
                    if spill:
                        vv2 = sVE.val
                        for pt in range(NPT):
                            def sp(e, pt=pt, vv2=vv2):
                                e.wait_ge(sVE.sem, vv2)
                                return e.dma_start(out=x1s[:, pt * T:(pt + 1) * T],
                                                   in_=XT[pt][:, :]).then_inc(sLD.sem, 16)
                            sch.emit("y", sp)
                            sLD.inc()

        with ExitStack() as stA:
            XB = [sb(f"XB{i}", [128, SC], BF16, stA) for i in range(NPT)]
            qT = [sb(f"qTs{i}", [128, SC], BF16, stA) for i in range(NPT)]
            kT = [sb(f"kTs{i}", [128, SC], BF16, stA) for i in range(NPT)]
            vtokA = sb("vtokA", [128, CPS * 1024], BF16, stA)
            ktokA = sb("ktokA", [128, CPS * 1024], BF16, stA)
            aout = sb("aout", [128, CPS * 1024], BF16, stA)
            aoT = sb("aoT", [128, NPT * SC], BF16, stA)
            vdm = [sb(f"vdm{i}", [128, SC], BF16, stA) for i in range(2)]
            eqs = [sb(f"eqs{i}", [128, SC], F32, stA) for i in range(2)]
            rus = [sb(f"rus{i}", [128, SC], F32, stA) for i in range(2)]
            bqc = sb("bqc", [128, 8], F32, stA)
            bkc = sb("bkc", [128, 8], F32, stA)
            bvc = sb("bvc", [128, 8], F32, stA)
            boc = sb("boc", [128, 8], F32, stA)
            maskedS = [sb(f"mskd{i}", [128, 2048], BF16, stA) for i in range(2)]
            Ssnap = sb("Ssnap", [128, 1040], BF16, stA)
            Ssb = sb("Ssb", [64, 1024], F32, stA)
            zsb = sb("zsb", [64, 16], F32, stA)
            rdenA = sb("rdenA", [128, 16], F32, stA)

            cmaskA = sb("cmaskA", [128, 1024], BF16, stA)
            dma_phase_gate()
            for nm, dst in (("A_bq", bqc[:, :]), ("A_bk", bkc[:, :]),
                            ("A_bv", bvc[:, :]), ("A_bo", boc[:, :])):
                colload(nm, dst)
            sch.emit("y", lambda e: e.dma_start(
                out=cmaskA[:, :], in_=cons["c_maskA"]).then_inc(sLD.sem, 16))
            sLD.inc()
            v_abias = sLD.val
            sch.chain_inc("v", [lambda e: e.memset(Ssb[:, :], 0.0),
                                lambda e: e.memset(zsb[:, :], 0.0)], sVE)

            stScan = ExitStack()
            sSN = sem("sSN", 16)
            tev = [Gate(), Gate()]

            def emit_partA_sc(sc):
                c0 = sc * SC
                if sc > 0:
                    sch.wait("v", sPE, sPE.val)
                    sch.wait("s", sPE, sPE.val)
                sch.wait("v", sVE, v_xt_ready)
                fns = [lambda e, i=i, c0=c0: e.tensor_copy(XB[i][:, :], XT[i][:, c0:c0 + SC])
                       for i in range(NPT)]
                v_xb = sch.last_inc("v", fns, sVE)
                sch.wait("v", sLD, v_abias)
                if sc > 0:
                    sch.wait("t", sSC, sSC.val)

                with ExitStack() as stp:
                    pQ = [ps(f"pQ{i}", [128, SC], stp) for i in range(4)]
                    pVT = [ps("pVTa", [128, 512], stp), ps("pVTb", [128, 512], stp)]
                    slot_free = [Gate(), Gate(), Gate(), Gate()]
                    idx = 0
                    last_phi = [0, 0]
                    for which, bc, kind in (("A_Wq", bqc, "q"), ("A_Wk", bkc, "k"),
                                            ("A_Wv", bvc, "v")):
                        for mt in range(NPT):
                            wslot = idx % 4
                            slot = idx % 4
                            s2 = idx % 2
                            wc = wload(which, mt, wrA[wslot], wrA_gate[wslot], sGW[wslot])
                            sch.wait("t", wc, wc.inc())
                            sch.wait("t", sVE, v_xb)
                            slot_free[slot].require(sch, "t")
                            fns = []
                            for kt in range(NPT):
                                fns.append(lambda e, wslot=wslot, slot=slot, kt=kt: e.matmul(
                                    pQ[slot][:, :],
                                    wrA[wslot][:, kt * 128:(kt + 1) * 128],
                                    XB[kt][:, :],
                                    start=(kt == 0), stop=(kt == NPT - 1)))
                            vp = sch.last_inc("t", fns, sPE)
                            wrA_gate[wslot].set(sPE, vp)
                            sch.wait("s", sPE, vp)
                            sch.wait("s", sLD, v_abias)
                            if kind in ("q", "k"):
                                dstT = qT if kind == "q" else kT
                                sch.wait("s", sVE, last_phi[s2])
                                vs = sch.last_inc("s", [
                                    lambda e, slot=slot, s2=s2, bc=bc, mt=mt:
                                    e.activation(eqs[s2][:, :], pQ[slot][:, :], AF.Exp,
                                                 bias=bc[:, mt:mt + 1])], sSC)
                                sch.wait("v", sSC, vs)
                                fns = [lambda e, slot=slot, s2=s2, bc=bc, mt=mt:
                                       e.tensor_scalar(rus[s2][:, :], pQ[slot][:, :],
                                                       bc[:, mt:mt + 1], 0.0, ALU.add, ALU.max),
                                       lambda e, dstT=dstT, s2=s2, mt=mt:
                                       e.scalar_tensor_tensor(dstT[mt][:, :], eqs[s2][:, :],
                                                              1.0, rus[s2][:, :],
                                                              ALU.min, ALU.add)]
                                vv = sch.chain_inc("v", fns, sVE)
                                slot_free[slot].set(sVE, vv)
                                last_phi[s2] = vv
                            else:
                                sch.wait("v", sPE, vp)
                                vs = sch.last_inc("v", [lambda e, slot=slot, s2=s2, mt=mt:
                                                        e.tensor_scalar_add(vdm[s2][:, :],
                                                                            pQ[slot][:, :],
                                                                            bvc[:, mt:mt + 1])], sVE)
                                slot_free[slot].set(sVE, vs)
                                sch.wait("t", sVE, vs)
                                srcs = [vdm[s2][:, j * 128:(j + 1) * 128] for j in range(CPS)]

                                def mkv(mt, s2):
                                    def wr(e, pbuf):
                                        oap = bass.AP(
                                            tensor=vtokA, offset=mt * 128,
                                            ap=[[CPS * 1024, 128], [1024, CPS], [1, 128]])
                                        iap = pbuf[:, :].rearrange("p (a m) -> p a m", m=128)
                                        return e.copy(oap, iap)
                                    return wr
                                transpose_batch(srcs, [mkv(mt, s2)], pVT, tev)
                            idx += 1
                    for dpt in range(NPT):
                        for sl in range(4):
                            slot_free[sl].require(sch, "t")
                        srcs = [kT[dpt][:, j * 128:(j + 1) * 128] for j in range(CPS)]

                        def mkk(dpt):
                            def wr(e, pbuf):
                                oap = bass.AP(
                                    tensor=ktokA, offset=dpt * 128,
                                    ap=[[CPS * 1024, 128], [1024, CPS], [1, 128]])
                                iap = pbuf[:, :].rearrange("p (a m) -> p a m", m=128)
                                return e.copy(oap, iap)
                            return wr
                        transpose_batch(srcs, [mkk(dpt)], pVT, tev)
                v_proj_done = sSC.val

                if PH < 2:
                    return
                with ExitStack() as sts:
                    pSC0 = ps("pSCa", [128, 1024], sts)
                    pSC = [pSC0, pSC0]
                    pN = [ps("pNa", [128, 512], sts), ps("pNb", [128, 512], sts)]
                    pS = ps("pS", [64, 1024], sts)
                    pZz = ps("pZz", [64, 32], sts)
                    pZden = ps("pZden", [128, 32], sts)
                    v_evac_prev = sVE.val
                    v_state_prev = sVE.val
                    v_div_prev = sSC.val
                    pe_chunk_done = sPE.val
                    last_tt = [0, 0]
                    sch.wait("t", sSC, v_proj_done)
                    sch.wait("t", sVE, sVE.val)
                    for cc in range(CPS):
                        c = sc * CPS + cc
                        if c >= NCHL:
                            continue
                        par = c % 2
                        if c > 0:
                            sch.wait("s", sPE, pe_chunk_done)
                            sch.wait("s", sVE, v_state_prev)
                            vsnap = sch.last_inc("s", [
                                lambda e: e.copy(Ssnap[0:64, 0:1024], Ssb[:, :]),
                                lambda e: e.copy(Ssnap[0:64, 1024:1040], zsb[:, :])], sSC)
                            sch.wait("g", sSC, vsnap)
                            vdup = sch.last_inc("g", [lambda e: e.dma_start(
                                out=Ssnap[64:128, :], in_=Ssnap[0:64, :])], sSN)
                        else:
                            vsnap = sSC.val
                            vdup = sSN.val
                        v_tt = [0, 0]
                        for half in range(2):
                            sch.wait("t", sVE, last_tt[0])
                            sch.wait("t", sVE, last_tt[1])
                            fns = []
                            order = [1, 3, 5, 7, None, 0, 2, 4, 6] if half == 0 \
                                else [0, 2, 4, 6, None, 1, 3, 5, 7]
                            for hh in order:
                                if hh is None:
                                    fns.append(lambda e: e.drain())
                                    continue
                                h = half * 8 + hh
                                pp, sub = h // 2, (h % 2) * 64
                                fns.append(lambda e, half=half, hh=hh, pp=pp, sub=sub, cc=cc:
                                           e.matmul(pSC[half][:, hh * 128:(hh + 1) * 128],
                                                    kT[pp][sub:sub + 64, cc * 128:(cc + 1) * 128],
                                                    qT[pp][sub:sub + 64, cc * 128:(cc + 1) * 128],
                                                    start=True, stop=True))
                            vm1 = sch.last_inc("t", fns, sPE)
                            sch.wait("v", sPE, vm1)
                            v_tt[half] = sch.last_inc("v", [
                                lambda e, half=half, par=par: e.tensor_mul(
                                    maskedS[par][:, half * 1024:(half + 1) * 1024],
                                    pSC[half][:, :], cmaskA[:, :])], sVE)
                            last_tt[half] = v_tt[half]
                            last_tt[1 - half] = max(last_tt[1 - half], 0)
                        sch.wait("t", sSC, vsnap)
                        sch.wait("t", sSN, vdup)
                        sch.wait("t", sVE, v_evac_prev)
                        sch.wait("t", sSC, v_div_prev)
                        for half in range(2):
                            sch.wait("t", sVE, v_tt[half])
                            fns = []
                            for hh in range(8):
                                h = half * 8 + hh
                                pp, sub = h // 2, (h % 2) * 64
                                moff = half * 1024 + hh * 128
                                vcol = cc * 1024 + h * 64
                                if SOP & 1:
                                    fns.append(lambda e, par=par, half=half, hh=hh, moff=moff,
                                           vcol=vcol, st=(hh == 0),
                                           sp=(c == 0 and hh == 7):
                                           e.matmul(pN[half][:, hh * 64:(hh + 1) * 64],
                                                    maskedS[par][:, moff:moff + 128],
                                                    vtokA[:, vcol:vcol + 64],
                                                    start=st, stop=sp))
                                if SOP & 2:
                                    fns.append(lambda e, par=par, moff=moff, h=h,
                                           st=(h == 0), sp=(c == 0 and h == 15):
                                           e.matmul(pZden[:, 2 * h:2 * h + 1],
                                                    maskedS[par][:, moff:moff + 128],
                                                    conesc[:, :], start=st, stop=sp))
                                if c > 0 and (SOP & 4):
                                    fns.append(lambda e, pp=pp, sub=sub, cc=cc, half=half, hh=hh, h=h:
                                               e.matmul(pN[half][:, hh * 64:(hh + 1) * 64],
                                                        qT[pp][sub:sub + 64, cc * 128:(cc + 1) * 128],
                                                        Ssnap[sub:sub + 64, h * 64:h * 64 + 64],
                                                        start=False, stop=(hh == 7)))
                                if SOP & 8:
                                    fns.append(lambda e, vcol=vcol, h=h:
                                           e.matmul(pS[0:64, h * 64:h * 64 + 64],
                                                    ktokA[:, vcol:vcol + 64],
                                                    vtokA[:, vcol:vcol + 64],
                                                    start=True, stop=True))
                                if SOP & 16:
                                    fns.append(lambda e, vcol=vcol, h=h:
                                           e.matmul(pZz[:, 2 * h:2 * h + 1],
                                                    ktokA[:, vcol:vcol + 64],
                                                    conesc[:, :], start=True, stop=True))
                                if c > 0 and (SOP & 2):
                                    fns.append(lambda e, pp=pp, sub=sub, cc=cc, h=h:
                                               e.matmul(pZden[:, 2 * h:2 * h + 1],
                                                        qT[pp][sub:sub + 64, cc * 128:(cc + 1) * 128],
                                                        Ssnap[sub:sub + 64, 1024 + h:1025 + h],
                                                        start=False, stop=(h == 15)))
                            pe_chunk_done = sch.last_inc("t", fns, sPE)
                        sch.wait("v", sPE, pe_chunk_done)
                        if not (SOP & 32):
                            continue
                        pZdh = bass.AP(tensor=pZden, offset=0, ap=[[32, 128], [2, 16]])
                        pZzh = bass.AP(tensor=pZz, offset=0, ap=[[32, 64], [2, 16]])
                        fns = []
                        if SOP & 8:
                            fns.append(lambda e: e.tensor_add(Ssb[:, :], Ssb[:, :], pS[0:64, :]))
                        if SOP & 16:
                            fns.append(lambda e, pZzh=pZzh: e.tensor_add(zsb[:, :], zsb[:, :], pZzh))
                        if fns:
                            v_state_prev = sch.chain_inc("v", fns, sVE)
                        if SOP & 2:
                            sch.wait("v", sSC, v_div_prev)
                            v_recip = sch.chain_inc("v", [
                                lambda e, pZdh=pZdh: e.reciprocal(rdenA[:, :], pZdh)], sVE)
                            sch.wait("s", sVE, v_recip)
                        fns = []
                        for h in (range(16) if (SOP & 1) and (SOP & 2) else []):
                            half, hh = h // 8, h % 8
                            fns.append(lambda e, half=half, hh=hh, h=h, cc=cc:
                                       e.mul(
                                           aout[:, cc * 1024 + h * 64: cc * 1024 + h * 64 + 64],
                                           pN[half][:, hh * 64:(hh + 1) * 64],
                                           rdenA[:, h:h + 1]))
                        if fns:
                            v_div_prev = sch.last_inc("s", fns, sSC)
                        v_evac_prev = v_state_prev

                if PH < 3:
                    return
                with ExitStack() as stt:
                    pVT2 = [ps("pVTc", [128, 512], stt), ps("pVTd", [128, 512], stt)]
                    tev2 = [Gate(), Gate()]
                    sch.wait("t", sVE, v_evac_prev)
                    sch.wait("t", sSC, v_div_prev)
                    for tt in range(CPS):
                        srcs = [aout[:, tt * 1024 + j * 128: tt * 1024 + (j + 1) * 128]
                                for j in range(8)]
                        dstw = []
                        for b in range(2):
                            def mka(b, tt):
                                def wr(e, pbuf):
                                    oap = bass.AP(
                                        tensor=aoT, offset=(b * 4) * SC + tt * 128,
                                        ap=[[NPT * SC, 128], [SC, 4], [1, 128]])
                                    iap = pbuf[:, :].rearrange("p (a m) -> p a m", m=128)
                                    return e.copy(oap, iap)
                                return wr
                            dstw.append(mka(b, tt))
                        transpose_batch(srcs, dstw, pVT2, tev2)
                v_aoT = sSC.val

                with ExitStack() as stw:
                    pW = [ps("pWa", [128, SC], stw), ps("pWb", [128, SC], stw)]
                    slot_free = [Gate(), Gate()]
                    for mt in range(NPT):
                        wslot = mt % 4
                        slot = mt % 2
                        wc = wload("A_Wo", mt, wrA[wslot], wrA_gate[wslot], sGW[wslot])
                        sch.wait("t", wc, wc.inc())
                        sch.wait("t", sSC, v_aoT)
                        slot_free[slot].require(sch, "t")
                        fns = []
                        for kt in range(NPT):
                            fns.append(lambda e, wslot=wslot, slot=slot, kt=kt: e.matmul(
                                pW[slot][:, :],
                                wrA[wslot][:, kt * 128:(kt + 1) * 128],
                                aoT[:, kt * SC:(kt + 1) * SC],
                                start=(kt == 0), stop=(kt == NPT - 1)))
                        vp = sch.last_inc("t", fns, sPE)
                        wrA_gate[wslot].set(sPE, vp)
                        sch.wait("v", sPE, vp)
                        vv = sch.last_inc("v", [
                            lambda e, slot=slot, mt=mt, c0=c0: e.scalar_tensor_tensor(
                                XT[mt][:, c0:c0 + SC], pW[slot][:, :], boc[:, mt:mt + 1],
                                XT[mt][:, c0:c0 + SC], ALU.add, ALU.add)], sVE)
                        slot_free[slot].set(sVE, vv)
            for sc in range(NSC):
                emit_partA_sc(sc)
            stScan.close()
        if PH >= 3:
            emit_ln_plain(spill=(STAGE >= 2))
        v_spill = sLD.val

        if STAGE >= 2:

            def emit_layerB(l):
                WN = f"B{l}_"
                cc = LCON[l]
                qscol = cc["qscol"]
                kscol = cc["kscol"]
                negm0 = cc["negm0"]
                lngc = cc["lngc"]
                lnbc = cc["lnbc"]
                b1ac = cc["b1ac"]
                b1gc = cc["b1gc"]
                b2c = cc["b2c"]
                v_bias = cc["v_bias"]
                v_m0 = cc["v_m0"]
                v_tabs = v_tabs_all
                with ExitStack() as stL:
                    oTsb = sb("oTsb", [128, 4 * T], BF16, stL)
                    with ExitStack() as stAt:
                        q2sb = sb("q2sb", [128, 4 * T], BF16, stAt)
                        k2sb = sb("k2sb", [128, 4 * T], BF16, stAt)
                        vtokB = sb("vtokB", [128, NTT * 520], BF16, stAt)
                        with ExitStack() as stq:
                            hbuf = [sb(f"hbuf{i}", [128, 1024], BF16, stq)
                                    for i in range(NPT)]
                            sqb = [sb(f"sqb{i}", [128, 1024], BF16, stq) for i in range(2)]
                            qraw = [sb(f"qraw{i}", [128, 1024], BF16, stq)
                                    for i in range(8)]
                            g_sqb = [Gate() for _ in range(2)]
                            g_qraw = [Gate() for _ in range(8)]
                            rstd16 = sb("rstd16", [16, 1024], BF16, stq)
                            qqs = [sb(f"qqs{i}", [128, 512], BF16, stq) for i in range(2)]
                            t1sb = sb("t1sb", [128, 512], BF16, stq)
                            t2sb = sb("t2sb", [128, 512], BF16, stq)
                            vdmB = sb("vdmB", [128, 1024], BF16, stq)

                            def shift_cast(nb, hb):
                                c0n = nb * 1024
                                fns = []
                                for pt in range(NPT):
                                    if pt < 4:
                                        fns.append(lambda e, pt=pt, c0n=c0n: e.tensor_copy(
                                            hb[pt][:, :], XT[pt][:, c0n:c0n + 1024]))
                                    elif nb == 0:
                                        fns.append(lambda e, pt=pt: e.memset(
                                            hb[pt][:, 0:1], 0.0))
                                        fns.append(lambda e, pt=pt: e.tensor_copy(
                                            hb[pt][:, 1:1024], XT[pt][:, 0:1023]))
                                    else:
                                        fns.append(lambda e, pt=pt, c0n=c0n: e.tensor_copy(
                                            hb[pt][:, :], XT[pt][:, c0n - 1:c0n + 1023]))
                                return sch.last_inc("v", fns, sVE)

                            pQK = [ps("pQKa", [128, 1024], stq), ps("pQKb", [128, 1024], stq)]
                            pS16 = ps("pS16", [16, 1024], stq)
                            pRT = [ps("pRTa", [128, 512], stq), ps("pRTb", [128, 512], stq)]
                            slot_free = [Gate(), Gate()]
                            g_bRB = Gate()
                            g_rot = Gate()
                            g_qqs = [Gate(), Gate()]
                            tevB = [Gate(), Gate()]
                            v_bc_end = 0
                            v_rstd_prev = 0
                            for nb in range(2):
                                c0n = nb * 1024
                                sch.wait("v", sPE, sPE.val)
                                v_h = shift_cast(nb, hbuf)
                                if nb == 0 and l == 0:
                                    sch.wait("t", sSC, LCON[-1]["v_m0"])
                                v_sum_done = 0
                                for mt in range(12):
                                    wslot = mt % 4
                                    slot = mt % 2
                                    wc = wload(WN + "wqkv", mt, wrA[wslot], wrA_gate[wslot], sGW[wslot])
                                    sch.wait("t", wc, wc.inc())
                                    sch.wait("t", sVE, v_h)
                                    slot_free[slot].require(sch, "t")
                                    fns = []
                                    for kt in range(NPT):
                                        for h2 in range(2):
                                            fns.append(lambda e, wslot=wslot, slot=slot, kt=kt, h2=h2:
                                                       e.matmul(
                                                           pQK[slot][:, h2 * 512:(h2 + 1) * 512],
                                                           wrA[wslot][:, kt * 128:(kt + 1) * 128],
                                                           hbuf[kt][:, h2 * 512:(h2 + 1) * 512],
                                                           start=(kt == 0), stop=(kt == NPT - 1)))
                                    vp = sch.last_inc("t", fns, sPE)
                                    wrA_gate[wslot].set(sPE, vp)
                                    if mt < 8:
                                        s4 = mt % 2
                                        sch.wait("s", sPE, vp)
                                        sch.wait("s", cc["ctr"], v_bias)
                                        g_sqb[s4].require(sch, "s")
                                        g_qraw[mt].require(sch, "s")
                                        vs = sch.last_inc("s", [
                                            lambda e, slot=slot, s4=s4: e.activation(
                                                sqb[s4][:, :], pQK[slot][:, :],
                                                AF.Square)], sSC)
                                        v_qcp = sch.last_inc("s", [
                                            lambda e, slot=slot, mt=mt: e.copy(
                                                qraw[mt][:, :], pQK[slot][:, :])], sSC)
                                        slot_free[slot].set(sSC, v_qcp)
                                        sch.wait("t", sSC, vs)
                                        if mt == 0 and nb == 1:
                                            sch.wait("t", sSC, v_rstd_prev)
                                        fns = []
                                        for ch in range(2):
                                            fns.append(lambda e, s4=s4, ch=ch, mt=mt: e.matmul(
                                                pS16[:, ch * 512:(ch + 1) * 512],
                                                cbdw[:, 15 - 2 * mt:31 - 2 * mt],
                                                sqb[s4][:, ch * 512:(ch + 1) * 512],
                                                start=(mt == 0), stop=(mt == 7)))
                                        v_sum_done = sch.last_inc("t", fns, sPE)
                                        g_sqb[s4].set(sPE, v_sum_done)
                                    else:
                                        mtv = mt - 8
                                        if mtv == 0:
                                            tevB[0].set(sVE, sVE.val)
                                            tevB[1].set(sVE, sVE.val)
                                        sch.wait("s", sPE, vp)
                                        vs = sch.last_inc("s", [
                                            lambda e, slot=slot: e.copy(
                                                vdmB[:, :], pQK[slot][:, :])], sSC)
                                        slot_free[slot].set(sSC, vs)
                                        sch.wait("t", sSC, vs)
                                        srcs = [vdmB[:, j * 128:(j + 1) * 128]
                                                for j in range(8)]
                                        dstw = []
                                        for b in range(2):
                                            def mkb(b, mtv, nb):
                                                def wr(e, pbuf):
                                                    tt0 = nb * 8 + b * 4
                                                    oap = bass.AP(
                                                        tensor=vtokB,
                                                        offset=tt0 * 520 + (2 * mtv) * 65,
                                                        ap=[[NTT * 520, 128], [520, 4],
                                                            [65, 2], [1, 64]])
                                                    iap = pbuf[:, :].rearrange(
                                                        "p (a h m) -> p a h m", h=2, m=64)
                                                    return e.copy(oap, iap)
                                                return wr
                                            dstw.append(mkb(b, mtv, nb))
                                        transpose_batch(srcs, dstw, pRT, tevB)
                                sch.wait("s", sPE, v_sum_done)
                                sch.wait("s", sPE, v_bc_end)
                                v_rstd = sch.chain_inc("s", [
                                    lambda e: e.activation(rstd16[:, :], pS16[:, :],
                                                           AF.Ln, bias=1e-24),
                                    lambda e: e.activation(rstd16[:, :], rstd16[:, :],
                                                           AF.Exp, scale=-0.5)], sSC)
                                v_rstd_prev = v_rstd
                                sch.wait("v", cc["ctr"], v_bias)
                                g_bRB.set(sSC, sSC.val)
                                g_rot.set(sSC, sSC.val)
                                for mt in range(8):
                                    isq = mt < 4
                                    pp = mt % 4
                                    scol = qscol if isq else kscol
                                    tabs = tabq if isq else tabk
                                    dst2 = q2sb if isq else k2sb
                                    for ch in range(2):
                                        sch.wait("t", sSC, v_rstd)
                                        g_bRB.require(sch, "t")
                                        vpb = sch.last_inc("t", [
                                            lambda e, mt=mt, ch=ch: e.matmul(
                                                pRT[0][:, :],
                                                cbsel[:, mt * 128:(mt + 1) * 128],
                                                rstd16[:, ch * 512:(ch + 1) * 512],
                                                start=True, stop=True)], sPE)
                                        v_bc_end = vpb
                                        sch.wait("v", sPE, vpb)
                                        g_qqs[ch].require(sch, "v")
                                        vq = sch.last_inc("v", [
                                            lambda e, mt=mt, ch=ch, scol=scol:
                                            e.scalar_tensor_tensor(
                                                qqs[ch][:, :],
                                                qraw[mt][:, ch * 512:(ch + 1) * 512],
                                                scol[:, 0:1], pRT[0][:, :],
                                                ALU.mult, ALU.mult)], sVE)
                                        g_bRB.set(sVE, vq)
                                        if ch == 1:
                                            g_qraw[mt].set(sVE, vq)
                                        sch.wait("t", sVE, vq)
                                        g_rot.require(sch, "t")
                                        vpr = sch.last_inc("t", [
                                            lambda e, ch=ch: e.matmul(
                                                pRT[1][:, :], cprot[:, :],
                                                qqs[ch][:, :], start=True, stop=True)], sPE)
                                        gcol = c0n + ch * 512
                                        sch.wait("v", sPE, vpr)
                                        sch.wait("v", sTB, v_tabs)
                                        vq2 = sch.chain_inc("v", [
                                            lambda e, ch=ch, tabs=tabs, gcol=gcol:
                                            e.tensor_mul(t1sb[:, :], qqs[ch][:, :],
                                                         tabs[0][:, gcol:gcol + 512]),
                                            lambda e, ch=ch, tabs=tabs, gcol=gcol:
                                            e.tensor_mul(t2sb[:, :], pRT[1][:, :],
                                                         tabs[1][:, gcol:gcol + 512]),
                                            lambda e, ch=ch, pp=pp, gcol=gcol, dst2=dst2:
                                            e.tensor_add(
                                                dst2[:, pp * T + gcol: pp * T + gcol + 512],
                                                t1sb[:, :], t2sb[:, :])], sVE)
                                        g_rot.set(sVE, vq2)
                                        g_qqs[ch].set(sVE, vq2)
                            fns = [lambda e, tt=tt: e.memset(bass.AP(
                                tensor=vtokB, offset=tt * 520 + 64,
                                ap=[[NTT * 520, 128], [65, 8]]), 1.0) for tt in range(NTT)]
                            v_vones = sch.last_inc("v", fns, sVE)
                        v_qk = sVE.val

                        DEN = sb("DENROW", [1, 3 * T], F32, stAt)
                        rdrow = DEN[0:1, 2 * T:3 * T]
                        RDENs = [sb("RDENa", [128, T], F32, stAt),
                                 sb("RDENb", [128, T], F32, stAt)]
                        with ExitStack() as sta:
                            expT = [sb(f"expT{i}", [128, 640], BF16, sta)
                                    for i in range(3)]
                            pSCb = [ps("pSCx", [128, 640], sta), ps("pSCy", [128, 640], sta)]
                            pOT = ps("pOT", [65, T], sta)
                            sc_free = [0, 0]
                            av_done = {}
                            v_evac_head = sSC.val
                            v_ev2_prev = sSC.val
                            v_ev3_box = [sSC.val]
                            v_rd = sVE.val
                            v_rd_box = [sVE.val]
                            dma_after_head = {}
                            pending_fin = [None]
                            pending_ev3 = [None]

                            def mk_fin(h, pp):
                                def fin():
                                    sch.wait("v", sSC, v_ev3_box[0])
                                    sch.wait("v", sGD, sGD.val)
                                    v_rd = sch.last_inc("v", [lambda e, h=h: e.reciprocal(
                                        rdrow, DEN[0:1, (h % 2) * T:(h % 2) * T + T])], sVE)
                                    v_rd_box[0] = v_rd

                                    def gdma(e, pp=pp, s2=(h % 2), v_rd=v_rd):
                                        e.wait_ge(sVE.sem, v_rd)
                                        base = 64 * (pp % 2) + s2
                                        return e.dma_start(
                                            out=RDENs[pp // 2][base:base + 1, :],
                                            in_=rdrow).then_inc(sGD.sem, 16)
                                    sch.emit("g", gdma)
                                    sGD.inc()
                                    dma_after_head[h] = sGD.val
                                return fin
                            for h in range(8):
                                pp, sub = h // 2, (h % 2) * 64
                                banks_touched = set()
                                reg_open = set()
                                for kt in range(NTT):
                                    g1 = min(kt + 4, NTT - 1)
                                    vq = (g1 - kt + 1) * 128
                                    q0 = kt * 128
                                    slot = kt % 2
                                    esl = (h * NTT + kt) % 3
                                    sch.wait("t", sVE, v_qk)
                                    sch.wait("t", sVE, v_vones)
                                    sch.wait("t", sSC, sc_free[slot])
                                    fns = []
                                    has_mkp = kt + 4 <= NTT - 1
                                    segs = [(0, min(128, vq), True, False)]
                                    if vq > 128:
                                        segs.append((128, min(512, vq), False, False))
                                    if vq > 512:
                                        segs.append((512, vq, True, not has_mkp))
                                    for o0, o1, st, sp in segs:
                                        fns.append(lambda e, slot=slot, pp=pp, sub=sub,
                                                   kt=kt, o0=o0, o1=o1, q0=q0, st=st, sp=sp:
                                                   e.matmul(
                                                       pSCb[slot][:, o0:o1],
                                                       k2sb[sub:sub + 64,
                                                            pp * T + kt * 128:
                                                            pp * T + (kt + 1) * 128],
                                                       q2sb[sub:sub + 64,
                                                            pp * T + q0 + o0:
                                                            pp * T + q0 + o1],
                                                       start=st, stop=sp))
                                    fns.append(lambda e, slot=slot: e.matmul(
                                        pSCb[slot][:, 0:128], cidbf[:, :], cmko[:, :],
                                        start=False, stop=True))
                                    if kt + 4 <= NTT - 1:
                                        fns.append(lambda e, slot=slot: e.matmul(
                                            pSCb[slot][:, 512:640], cidbf[:, :], cmkp[:, :],
                                            start=False, stop=True))
                                    vp = sch.last_inc("t", fns, sPE)
                                    sch.wait("s", sPE, vp)
                                    prev = (h, kt - 3) if kt >= 3 else (h - 1, NTT - 3 + kt)
                                    if prev in av_done:
                                        sch.wait("s", sPE, av_done[prev])
                                    sch.wait("s", sSC, v_m0)
                                    vs = sch.last_inc("s", [
                                        lambda e, slot=slot, esl=esl, vq=vq:
                                        e.activation(expT[esl][:, 0:vq],
                                                     pSCb[slot][:, 0:vq], AF.Exp,
                                                     bias=negm0[:, 0:1])], sSC)
                                    sc_free[slot] = vs
                                    sch.wait("t", sSC, vs)
                                    if h > 0 and kt == 0:
                                        sch.wait("t", sSC, v_ev2_prev)
                                    if h > 0 and kt == 8:
                                        sch.wait("t", sSC, v_ev3_box[0])
                                    fns = []
                                    for g in range(kt, min(kt + 4, NTT - 1) + 1):
                                        r = g // 4
                                        st_flag = r not in reg_open
                                        reg_open.add(r)
                                        sp_flag = (kt == min(4 * r + 3, NTT - 1)
                                                   and g == kt)
                                        if sp_flag:
                                            reg_open.discard(r)
                                        off = (g - kt) * 128
                                        fns.append(lambda e, kt=kt, h=h, g=g, off=off,
                                                   esl=esl, st_flag=st_flag,
                                                   sp_flag=sp_flag: e.matmul(
                                                       pOT[:, g * 128:(g + 1) * 128],
                                                       vtokB[:, kt * 520 + h * 65:
                                                             kt * 520 + h * 65 + 65],
                                                       expT[esl][:, off:off + 128],
                                                       start=st_flag, stop=sp_flag))
                                    av_done[(h, kt)] = sch.last_inc("t", fns, sPE)
                                    if kt in (3, 7, 11):
                                        r = kt // 4
                                        sch.wait("s", sPE, av_done[(h, kt)])
                                        if r == 0:
                                            sch.wait("s", sVE, v_rd_box[0])
                                        vev = sch.last_inc("s", [
                                            lambda e, pp=pp, sub=sub, r=r: e.copy(
                                                oTsb[sub:sub + 64,
                                                     pp * T + r * 512:pp * T + (r + 1) * 512],
                                                pOT[0:64, r * 512:(r + 1) * 512]),
                                            lambda e, r=r, h=h: e.copy(
                                                DEN[0:1, (h % 2) * T + r * 512:
                                                    (h % 2) * T + (r + 1) * 512],
                                                pOT[64:65, r * 512:(r + 1) * 512])], sSC)
                                        if kt == 11:
                                            v_ev2_prev = vev
                                    if kt == 4 and pending_ev3[0] is not None:
                                        pending_ev3[0]()
                                        pending_ev3[0] = None
                                    if kt == 6 and pending_fin[0] is not None:
                                        pending_fin[0]()
                                        pending_fin[0] = None

                                def mkev3(pp, sub, av_last, h=None):
                                    def emitev3():
                                        sch.wait("s", sPE, av_last)
                                        v_ev3_box[0] = sch.last_inc("s", [
                                            lambda e, pp=pp, sub=sub: e.copy(
                                                oTsb[sub:sub + 64, pp * T + 1536:(pp + 1) * T],
                                                pOT[0:64, 1536:2048]),
                                            lambda e, h=h: e.copy(
                                                DEN[0:1, (h % 2) * T + 1536:
                                                    (h % 2) * T + 2048],
                                                pOT[64:65, 1536:2048])], sSC)
                                    return emitev3
                                pending_ev3[0] = mkev3(pp, sub, av_done[(h, NTT - 1)], h)
                                pending_fin[0] = mk_fin(h, pp)
                            pending_ev3[0]()
                            pending_ev3[0] = None
                            pending_fin[0]()
                            pending_fin[0] = None
                            v_alldma = sGD.val
                        with ExitStack() as std:
                            pDV = [ps("pDVa", [128, 512], std),
                                   ps("pDVb", [128, 512], std)]
                            ev3 = [0, 0]
                            i = 0
                            for pp in range(4):
                                for ch in range(4):
                                    buf = i % 2
                                    sch.wait("t", sGD, dma_after_head[2 * pp + 1])
                                    if i >= 2:
                                        sch.wait("t", sVE, ev3[buf])
                                    vp = sch.last_inc("t", [
                                        lambda e, pp=pp, ch=ch, buf=buf: e.matmul(
                                            pDV[buf][:, :],
                                            cbd2f[64 * (pp % 2):64 * (pp % 2) + 2, :],
                                            RDENs[pp // 2][64 * (pp % 2):64 * (pp % 2) + 2,
                                                           ch * 512:(ch + 1) * 512],
                                            start=True, stop=True)], sPE)
                                    sch.wait("v", sPE, vp)
                                    ev3[buf] = sch.last_inc("v", [
                                        lambda e, pp=pp, ch=ch, buf=buf: e.tensor_mul(
                                            oTsb[:, pp * T + ch * 512:
                                                 pp * T + (ch + 1) * 512],
                                            oTsb[:, pp * T + ch * 512:
                                                 pp * T + (ch + 1) * 512],
                                            pDV[buf][:, :])], sVE)
                                    i += 1
                            v_oT = sVE.val

                    with ExitStack() as stF:
                        _ = None
                        hbufF2 = [[sb(f"hbufF{n}_{i}", [128, 1024], BF16, stF)
                                   for i in range(NPT)] for n in range(2)]

                        RW = sb("lnrowsB", [1, 3 * 1024], F32, stF)

                        h3sb = sb("h3sb", [128, NJ * 1024], BF16, stF)
                        NJH = NJ // 2
                        wr2 = [sb(f"wr2_{i}", [128, NJH * 128], BF16, stF)
                               for i in range(2)]
                        lntmp2 = [sb("lntmpB0", [128, 512], BF16, stF)]
                        lntmp2.append(lntmp2[0])
                        xedge = sb("xedge", [128, 8], F32, stF)
                        dma_phase_gate()

                        def shift_castF(nb):
                            hb = hbufF2[nb]
                            c0n = nb * 1024
                            sch.wait("v", sVE, sVE.val)
                            fns = []
                            for pt in range(NPT):
                                if pt < 4:
                                    fns.append(lambda e, hb=hb, pt=pt, c0n=c0n: e.tensor_copy(
                                        hb[pt][:, :], XT[pt][:, c0n:c0n + 1024]))
                                elif nb == 0:
                                    fns.append(lambda e, hb=hb, pt=pt: e.memset(
                                        hb[pt][:, 0:1], 0.0))
                                    fns.append(lambda e, hb=hb, pt=pt: e.tensor_copy(
                                        hb[pt][:, 1:1024], XT[pt][:, 0:1023]))
                                else:
                                    fns.append(lambda e, hb=hb, pt=pt: e.tensor_copy(
                                        hb[pt][:, 0:1], xedge[:, pt:pt + 1]))
                                    fns.append(lambda e, hb=hb, pt=pt, c0n=c0n: e.tensor_copy(
                                        hb[pt][:, 1:1024], XT[pt][:, c0n:c0n + 1023]))
                            return sch.chain_inc("v", fns, sVE)

                        W1 = 1024
                        b_sy = RW[0:1, 0:W1]
                        b_s1 = RW[0:1, W1:2 * W1]
                        b_s2 = RW[0:1, 2 * W1:3 * W1]
                        def ln2_pieces(nb):
                            hbufF = hbufF2[nb]
                            c0n = nb * 1024
                            st = {"es": ExitStack()}
                            box = {}

                            def p_start():
                                st["pSum"] = ps("pSumB", [1, 1024], st["es"])
                                st["pSq"] = ps("pSqB", [1, 1024], st["es"])
                                sch.wait("v", sPE, sPE.val)
                                shift_castF(nb)

                            sqF = hbufF2[1 - nb][0]

                            def p_pt(pt):
                                pSum, pSq2 = st["pSum"], st["pSq"]
                                vc = sch.last_inc("v", [lambda e, pt=pt: e.tensor_mul(
                                    sqF[:, :], hbufF[pt][:, :], hbufF[pt][:, :])], sVE)
                                sch.wait("t", sVE, vc)
                                fns = []
                                for ch in range(2):
                                    fns.append(lambda e, ch=ch, pt=pt: e.matmul(
                                        pSum[0:1, ch * 512:(ch + 1) * 512], conesc[:, :],
                                        hbufF[pt][:, ch * 512:(ch + 1) * 512],
                                        start=(pt == 0), stop=(pt == NPT - 1)))
                                    fns.append(lambda e, ch=ch, pt=pt: e.matmul(
                                        pSq2[0:1, ch * 512:(ch + 1) * 512], conesc[:, :],
                                        sqF[:, ch * 512:(ch + 1) * 512],
                                        start=(pt == 0), stop=(pt == NPT - 1)))
                                vp = sch.last_inc("t", fns, sPE)
                                sch.wait("v", sPE, vp)

                            def p_rows():
                                pSum, pSq2 = st["pSum"], st["pSq"]
                                sch.wait("s", sPE, sPE.val)
                                vs = sch.last_inc("s", [
                                    lambda e: e.copy(b_sy, pSum[0:1, :]),
                                    lambda e: e.copy(b_s1, pSq2[0:1, :])], sSC)
                                st["es"].close()
                                sch.wait("v", sSC, vs)
                                sch.chain_inc("v", [
                                    lambda e: e.tensor_scalar_mul(b_sy, b_sy, 1.0 / D),
                                    lambda e: e.tensor_scalar_mul(b_s1, b_s1, 1.0 / D),
                                    lambda e: e.tensor_mul(b_s2, b_sy, b_sy),
                                    lambda e: e.tensor_sub(b_s1, b_s1, b_s2)], sVE)
                                sch.wait("s", sVE, sVE.val)
                                vs = sch.chain_inc("s", [
                                    lambda e: e.activation(b_s1, b_s1, AF.Ln, bias=1e-5),
                                    lambda e: e.activation(b_s1, b_s1, AF.Exp, scale=-0.5)], sSC)
                                sch.wait("v", sSC, vs)
                                st["vv"] = sch.last_inc("v", [
                                    lambda e: e.scalar_tensor_tensor(
                                        b_s2, b_sy, -1.0, b_s1, ALU.mult, ALU.mult)], sVE)

                            def p_apply(ch):
                                if ch == 0:
                                    st["es2"] = ExitStack()
                                    st["pBA"] = [ps("pBAx", [128, 512], st["es2"]),
                                                 ps("pBAy", [128, 512], st["es2"])]
                                    st["pBB"] = [ps("pBBx", [128, 512], st["es2"]),
                                                 ps("pBBy", [128, 512], st["es2"])]
                                pBA, pBB = st["pBA"], st["pBB"]
                                buf = ch % 2
                                sch.wait("t", sVE, st["vv"])
                                fns = [lambda e, buf=buf, ch=ch: e.matmul(
                                    pBA[buf][:, :], conesrf[:, :],
                                    RW[0:1, W1 + ch * 512: W1 + (ch + 1) * 512],
                                    start=True, stop=True),
                                    lambda e, buf=buf, ch=ch: e.matmul(
                                    pBB[buf][:, :], conesrf[:, :],
                                    RW[0:1, 2 * W1 + ch * 512: 2 * W1 + (ch + 1) * 512],
                                    start=True, stop=True)]
                                vp = sch.last_inc("t", fns, sPE)
                                sch.wait("v", sPE, vp)
                                for pt in range(NPT):
                                    cg = c0n + ch * 512
                                    if pt < 4:
                                        fns = [lambda e, pt=pt, cg=cg, buf=buf:
                                               e.tensor_mul(lntmp2[0][:, :],
                                                            XT[pt][:, cg:cg + 512],
                                                            pBA[buf][:, :]),
                                               lambda e, buf=buf: e.tensor_add(
                                            lntmp2[1][:, :], lntmp2[0][:, :],
                                            pBB[buf][:, :])]
                                    elif cg == 1024:
                                        fns = [lambda e, pt=pt, buf=buf:
                                               e.tensor_mul(lntmp2[0][:, 1:512],
                                                            XT[pt][:, 1024:1535],
                                                            pBA[buf][:, 1:512]),
                                               lambda e, pt=pt, buf=buf:
                                               e.tensor_mul(lntmp2[0][:, 0:1],
                                                            xedge[:, pt:pt + 1],
                                                            pBA[buf][:, 0:1]),
                                               lambda e, buf=buf: e.tensor_add(
                                            lntmp2[1][:, :], lntmp2[0][:, :],
                                            pBB[buf][:, :])]
                                    elif cg > 0:
                                        fns = [lambda e, pt=pt, cg=cg, buf=buf:
                                               e.tensor_mul(lntmp2[0][:, :],
                                                            XT[pt][:, cg - 1:cg + 511],
                                                            pBA[buf][:, :]),
                                               lambda e, buf=buf: e.tensor_add(
                                            lntmp2[1][:, :], lntmp2[0][:, :],
                                            pBB[buf][:, :])]
                                    else:
                                        fns = [lambda e, pt=pt, buf=buf:
                                               e.tensor_mul(lntmp2[0][:, 1:512],
                                                            XT[pt][:, 0:511],
                                                            pBA[buf][:, 1:512]),
                                               lambda e: e.memset(
                                            lntmp2[0][:, 0:1], 0.0),
                                               lambda e, buf=buf: e.tensor_add(
                                            lntmp2[1][:, :], lntmp2[0][:, :],
                                            pBB[buf][:, :])]
                                    fns.append(lambda e, pt=pt, ch=ch: e.tensor_scalar(
                                        hbufF[pt][:, ch * 512:(ch + 1) * 512],
                                        lntmp2[1][:, :], lngc[:, pt:pt + 1],
                                        lnbc[:, pt:pt + 1], ALU.mult, ALU.add))
                                    sch.chain_inc("v", fns, sVE)
                                box["v_hp_c%d" % ch] = sVE.val
                                if ch == 1:
                                    box["v_hp"] = sVE.val
                                    st["es2"].close()

                            pieces = [p_start]
                            pieces += [(lambda pt: (lambda: p_pt(pt)))(pt)
                                       for pt in range(NPT)]
                            pieces += [p_rows, lambda: p_apply(0), lambda: p_apply(1)]
                            return pieces, box

                        last_h3 = [0]

                        def emit_w1(nb, v_hp_c0, v_hp):
                            hbufF = hbufF2[nb]
                            e2sb = hbufF2[1 - nb][1]
                            with ExitStack() as stw1:
                                pW1 = [ps("pW1a", [128, 1024], stw1),
                                       ps("pW1b", [128, 1024], stw1),
                                       ps("pW1c", [128, 1024], stw1),
                                       ps("pW1d", [128, 1024], stw1)]
                                pair_free = [Gate(), Gate()]
                                for j in range(NJ):
                                    prslot = j % 2
                                    pa, pg = pW1[2 * prslot], pW1[2 * prslot + 1]
                                    pair_free[prslot].require(sch, "t")
                                    vps = []
                                    for ag in range(2):
                                        wslot = (2 * j + ag) % 4
                                        wc = wload(WN + "w1", 2 * j + ag, wrA[wslot],
                                                   wrA_gate[wslot], sGW[wslot])
                                        sch.wait("t", wc, wc.inc())
                                        if j == 0 and ag == 0:
                                            sch.wait("t", sVE, v_hp_c0)
                                        pdst = pa if ag == 0 else pg
                                        for h2 in range(2):
                                            fns = []
                                            for kt in range(NPT):
                                                fns.append(lambda e, pdst=pdst, wslot=wslot,
                                                           hbufF=hbufF,
                                                           kt=kt, h2=h2: e.matmul(
                                                               pdst[:, h2 * 512:(h2 + 1) * 512],
                                                               wrA[wslot][:, kt * 128:(kt + 1) * 128],
                                                               hbufF[kt][:, h2 * 512:(h2 + 1) * 512],
                                                               start=(kt == 0),
                                                               stop=(kt == NPT - 1)))
                                            if j == 0 and ag == 0 and h2 == 1:
                                                sch.wait("t", sVE, v_hp)
                                            vp = sch.last_inc("t", fns, sPE)
                                        wrA_gate[wslot].set(sPE, vp)
                                        vps.append(vp)
                                    sch.wait("s", sPE, vps[1])
                                    sch.wait("s", sVE, last_h3[0])
                                    vs = sch.last_inc("s", [
                                        lambda e, pg=pg, j=j: e.activation(
                                            e2sb[:, :], pg[:, :], AF.Gelu,
                                            bias=b1gc[:, j:j + 1])], sSC)
                                    sch.wait("v", sSC, vs)
                                    vv4 = sch.last_inc("v", [
                                        lambda e, pa=pa, j=j: e.scalar_tensor_tensor(
                                            h3sb[:, j * 1024:(j + 1) * 1024],
                                            pa[:, :], b1ac[:, j:j + 1],
                                            e2sb[:, :], ALU.add, ALU.mult)], sVE)
                                    pair_free[prslot].set(sVE, vv4)
                                    last_h3[0] = vv4
                            return sVE.val

                        def wload_half(sub, hf, dstbuf, gate, ctr):
                            src2 = wdr[WN + "w2"][sub][hf * NJH * 128:(hf + 1) * NJH * 128]
                            src2 = src2.rearrange("(a p) m -> p a m", p=128)
                            dst = dstbuf[:, :].rearrange("p (a m) -> p a m", m=128)
                            if gate is not None:
                                gate.require(sch, "g")
                            sch.emit("g", lambda e, dst=dst, src2=src2, ctr=ctr:
                                     e.dma_start(out=dst, in_=src2).then_inc(ctr.sem, 16))
                            return ctr

                        def emit_w2(nb, v_h3, inserts=None):
                            c0n = nb * 1024
                            with ExitStack() as stw2:
                                pW2 = [ps("pW2a", [128, 1024], stw2),
                                       ps("pW2b", [128, 1024], stw2)]
                                slot_free2 = [Gate(), Gate()]
                                for mt in range(NPT):
                                    slot = mt % 2
                                    pslot = mt % 2
                                    vp = 0
                                    sch.wait("t", sVE, v_h3)
                                    slot_free2[pslot].require(sch, "t")
                                    for hf in range(2):
                                        wsl = (2 * mt + hf) % 2
                                        wc = wload_half(mt, hf, wr2[wsl],
                                                        wr2_gate[wsl], sG2[wsl])
                                        sch.wait("t", wc, wc.inc())
                                        fns = []
                                        for kj in range(NJH):
                                            kt = hf * NJH + kj
                                            for h2 in range(2):
                                                fns.append(lambda e, pslot=pslot, wsl=wsl,
                                                           kj=kj, kt=kt, h2=h2:
                                                           e.matmul(
                                                               pW2[pslot][:, h2 * 512:(h2 + 1) * 512],
                                                               wr2[wsl][:, kj * 128:(kj + 1) * 128],
                                                               h3sb[:, kt * 1024 + h2 * 512:
                                                                    kt * 1024 + (h2 + 1) * 512],
                                                               start=(kt == 0), stop=(kt == NJ - 1)))
                                        vp = sch.last_inc("t", fns, sPE)
                                        wr2_gate[wsl].set(sPE, vp)
                                    sch.wait("v", sPE, vp)
                                    vv5 = sch.last_inc("v", [
                                        lambda e, pslot=pslot, mt=mt, c0n=c0n:
                                        e.scalar_tensor_tensor(
                                            XT[mt][:, c0n:c0n + 1024], pW2[pslot][:, :],
                                            b2c[:, mt:mt + 1], XT[mt][:, c0n:c0n + 1024],
                                            ALU.add, ALU.add)], sVE)
                                    slot_free2[pslot].set(sVE, vv5)
                                    for f in (inserts or {}).get(mt, []):
                                        f()

                        def emit_wo(inserts=None):
                            with ExitStack() as stw:
                                pW = [ps("pWc", [128, 1024], stw),
                                      ps("pWd", [128, 1024], stw)]
                                slot_free = [Gate(), Gate()]
                                i = 0
                                for nb in range(2):
                                    for mt in range(NPT):
                                        wslot = i % 4
                                        slot = i % 2
                                        wc = wload(WN + "wo", mt, wrA[wslot],
                                                   wrA_gate[wslot], sGW[wslot], cols=512)
                                        sch.wait("t", wc, wc.inc())
                                        sch.wait("t", sVE, v_oT)
                                        slot_free[slot].require(sch, "t")
                                        fns = []
                                        for kt in range(4):
                                            for h2 in range(2):
                                                fns.append(lambda e, slot=slot, wslot=wslot,
                                                           kt=kt, h2=h2, nb=nb: e.matmul(
                                                               pW[slot][:, h2 * 512:(h2 + 1) * 512],
                                                               wrA[wslot][:, kt * 128:(kt + 1) * 128],
                                                               oTsb[:, kt * T + nb * 1024 + h2 * 512:
                                                                    kt * T + nb * 1024 + (h2 + 1) * 512],
                                                               start=(kt == 0), stop=(kt == 3)))
                                        vp = sch.last_inc("t", fns, sPE)
                                        wrA_gate[wslot].set(sPE, vp)
                                        sch.wait("v", sPE, vp)
                                        if l == 0:
                                            sch.wait("v", sLD, v_spill)
                                        vv = sch.last_inc("v", [
                                            lambda e, slot=slot, mt=mt, nb=nb: e.tensor_add(
                                                XT[mt][:, nb * 1024:(nb + 1) * 1024],
                                                XT[mt][:, nb * 1024:(nb + 1) * 1024],
                                                pW[slot][:, :])], sVE)
                                        slot_free[slot].set(sVE, vv)
                                        for f in (inserts or {}).get((nb, mt), []):
                                            f()
                                        i += 1

                        pieces0, box0 = ln2_pieces(0)
                        ins0 = {(1, 0): [pieces0[0]],
                                (1, 1): [pieces0[1], pieces0[2]],
                                (1, 2): [pieces0[3], pieces0[4]],
                                (1, 3): [pieces0[5], pieces0[6]],
                                (1, 4): [pieces0[7], pieces0[8]],
                                (1, 5): [pieces0[9]],
                                (1, 6): [pieces0[10]],
                                (1, 7): [pieces0[11]]}
                        emit_wo(inserts=ins0)
                        sch.chain_inc("v", [
                            (lambda pt2: (lambda e: e.tensor_copy(
                                xedge[:, pt2:pt2 + 1], XT[pt2][:, 1023:1024])))(pt2)
                            for pt2 in range(NPT)], sVE)
                        v_h3_0 = emit_w1(0, box0["v_hp_c0"], box0["v_hp"])
                        pieces1, box1 = ln2_pieces(1)
                        ins = {0: [pieces1[0]],
                               1: [pieces1[1], pieces1[2]],
                               2: [pieces1[3], pieces1[4]],
                               3: [pieces1[5], pieces1[6]],
                               4: [pieces1[7], pieces1[8]],
                               5: [pieces1[9]],
                               6: [pieces1[10]],
                               7: [pieces1[11]]}
                        emit_w2(0, v_h3_0, inserts=ins)
                        v_h3_1 = emit_w1(1, box1["v_hp_c0"], box1["v_hp"])
                        emit_w2(1, v_h3_1)

            for l in range(NLAY):
                emit_layerB(l)
            stPB.close()

        if STAGE >= 3:
            with ExitStack() as stf:
                x1r = [sb(f"x1r{i}", [128, T], F32, stf) for i in range(NPT)]
                R = sb("lnrows", [1, 5 * T], F32, stf)
                lnbf = sb("lnbf", [128, T], BF16, stf)
                lnsq = sb("lnsq", [128, T], BF16, stf)
                otok = sb("otok", [128, 4 * D], F32, stf)
                r_sy = R[0:1, 0:T]
                r_sq = R[0:1, T:2 * T]
                r_t = R[0:1, 2 * T:3 * T]
                r_A = R[0:1, 3 * T:4 * T]
                r_B = R[0:1, 4 * T:5 * T]
                dma_phase_gate()
                sX1 = [sem(f"sX1{i}", 16) for i in range(NPT)]
                for pt in range(NPT):
                    sch.emit("y", lambda e, pt=pt: e.dma_start(
                        out=x1r[pt][:, :], in_=x1s[:, pt * T:(pt + 1) * T])
                        .then_inc(sX1[pt].sem, 16))
                    sX1[pt].inc()
                sch.wait("v", sPE, sPE.val)
                with ExitStack() as stp:
                    pSum = ps("pSum", [1, T], stp)
                    pSq = ps("pSq", [1, T], stp)
                    for pt in range(NPT):
                        sch.wait("v", sX1[pt], 16)
                        fns = [lambda e, pt=pt: e.tensor_add(
                                   XT[pt][:, :], XT[pt][:, :], x1r[pt][:, :]),
                               lambda e, pt=pt: e.tensor_copy(lnbf[:, :], XT[pt][:, :]),
                               lambda e: e.tensor_mul(lnsq[:, :], lnbf[:, :], lnbf[:, :])]
                        vc = sch.chain_inc("v", fns, sVE)
                        sch.wait("t", sVE, vc)
                        fns = []
                        for ch in range(4):
                            fns.append(lambda e, ch=ch, pt=pt: e.matmul(
                                pSum[0:1, ch * 512:(ch + 1) * 512], conesc[:, :],
                                lnbf[:, ch * 512:(ch + 1) * 512],
                                start=(pt == 0), stop=(pt == NPT - 1)))
                            fns.append(lambda e, ch=ch, pt=pt: e.matmul(
                                pSq[0:1, ch * 512:(ch + 1) * 512], conesc[:, :],
                                lnsq[:, ch * 512:(ch + 1) * 512],
                                start=(pt == 0), stop=(pt == NPT - 1)))
                        vp = sch.last_inc("t", fns, sPE)
                        sch.wait("v", sPE, vp)
                    sch.wait("s", sPE, sPE.val)
                    vs = sch.last_inc("s", [
                        lambda e: e.copy(r_sy, pSum[0:1, :]),
                        lambda e: e.copy(r_sq, pSq[0:1, :])], sSC)
                sch.wait("v", sSC, vs)
                vv = sch.chain_inc("v", [
                    lambda e: e.tensor_scalar_mul(r_sy, r_sy, 1.0 / D),
                    lambda e: e.tensor_scalar_mul(r_sq, r_sq, 1.0 / D),
                    lambda e: e.tensor_mul(r_t, r_sy, r_sy),
                    lambda e: e.tensor_sub(r_t, r_sq, r_t)], sVE)
                sch.wait("s", sVE, vv)
                vs = sch.chain_inc("s", [
                    lambda e: e.activation(r_t, r_t, AF.Ln, bias=1e-5),
                    lambda e: e.activation(r_A, r_t, AF.Exp, scale=-0.5)], sSC)
                sch.wait("v", sSC, vs)
                vv = sch.last_inc("v", [lambda e: e.scalar_tensor_tensor(
                    r_B, r_sy, -1.0, r_A, ALU.mult, ALU.mult)], sVE)
                v_ch = [0, 0, 0, 0]
                with ExitStack() as stp:
                    pBA = [ps("pBA0", [128, 512], stp), ps("pBA1", [128, 512], stp)]
                    pBB = [ps("pBB0", [128, 512], stp), ps("pBB1", [128, 512], stp)]
                    tmpf = sb("lntmp", [128, 512], F32, stp)
                    ev2 = [0, 0]
                    pO = [ps("pO0", [128, 512], stp), ps("pO1", [128, 512], stp)]
                    ev = [0, 0]
                    idx = 0
                    sSTs = [sem(f"sSTs{i}", 16) for i in range(4)]
                    yr = y_out.rearrange("(tt p) d -> tt p d", p=128)
                    for ch in range(4):
                        buf = ch % 2
                        sch.wait("t", sVE, vv)
                        if ch >= 2:
                            sch.wait("t", sVE, ev2[buf])
                        fns = [lambda e, buf=buf, ch=ch: e.matmul(
                            pBA[buf][:, :], conesrf[:, :],
                            R[0:1, 3 * T + ch * 512: 3 * T + (ch + 1) * 512],
                            start=True, stop=True),
                            lambda e, buf=buf, ch=ch: e.matmul(
                            pBB[buf][:, :], conesrf[:, :],
                            R[0:1, 4 * T + ch * 512: 4 * T + (ch + 1) * 512],
                            start=True, stop=True)]
                        vp = sch.last_inc("t", fns, sPE)
                        sch.wait("v", sPE, vp)
                        fns = []
                        for pt in range(NPT):
                            fns.append(lambda e, pt=pt, ch=ch, buf=buf: e.tensor_mul(
                                tmpf[:, :], XT[pt][:, ch * 512:(ch + 1) * 512], pBA[buf][:, :]))
                            fns.append(lambda e, pt=pt, ch=ch, buf=buf: e.tensor_add(
                                XT[pt][:, ch * 512:(ch + 1) * 512], tmpf[:, :], pBB[buf][:, :]))
                        ev2[buf] = sch.chain_inc("v", fns, sVE)
                        v_ch[ch] = ev2[buf]
                        for tt in range(ch * 4, (ch + 1) * 4):
                            ts = tt % 4
                            for dg in range(2):
                                tbuf = idx % 2
                                sch.wait("t", sVE, v_ch[ch])
                                if idx >= 2:
                                    sch.wait("t", sVE, ev[tbuf])
                                fns = []
                                for j in range(4):
                                    dt_i = dg * 4 + j
                                    fns.append(lambda e, tbuf=tbuf, j=j, tt=tt, dt_i=dt_i:
                                               e.matmul(
                                                   pO[tbuf][:, j * 128:(j + 1) * 128],
                                                   XT[dt_i][:, tt * 128:(tt + 1) * 128],
                                                   cid32[:, :], start=True, stop=True))
                                vp = sch.last_inc("t", fns, sPE)
                                sch.wait("v", sPE, vp)
                                if tt >= 4 and dg == 0:
                                    sch.wait("v", sSTs[ts], sSTs[ts].val)
                                fns = [lambda e, tbuf=tbuf, j=j, ts=ts, dt_i=dg * 4 + j:
                                       e.tensor_copy(
                                           otok[:, ts * D + dt_i * 128:
                                                ts * D + dt_i * 128 + 128],
                                           pO[tbuf][:, j * 128:(j + 1) * 128])
                                       for j in range(4)]
                                ev[tbuf] = sch.last_inc("v", fns, sVE)
                                idx += 1
                            vv6 = sVE.val

                            def stf_(e, tt=tt, ts=ts, vv6=vv6):
                                e.wait_ge(sVE.sem, vv6)
                                return e.dma_start(
                                    out=yr[tt],
                                    in_=otok[:, ts * D:(ts + 1) * D]).then_inc(sSTs[ts].sem, 16)
                            sch.emit("y", stf_)
                            sSTs[ts].inc()
                    for i in range(4):
                        sch.emit("y", lambda e, s=sSTs[i].sem, v=sSTs[i].val:
                                 e.wait_ge(s, v))

        with nc.Block() as blk:
            @blk.tensor
            def _(t):
                for fn in sch.ops["t"]:
                    fn(t)

            @blk.vector
            def _(v):
                for fn in sch.ops["v"]:
                    fn(v)

            @blk.scalar
            def _(s):
                for fn in sch.ops["s"]:
                    fn(s)

            @blk.gpsimd
            def _(g):
                for fn in sch.ops["g"]:
                    fn(g)

            @blk.sync
            def _(y):
                for fn in sch.ops["y"]:
                    fn(y)

    return nc


def kernel(**inputs):
    x = np.ascontiguousarray(inputs["L0_attn_x"]).astype(np.float32, copy=False)
    w = _prep_weights(inputs)
    wspecs = {nm: arr.shape for nm, arr in w.items()}
    nc = build_nc(wspecs)
    in_maps = []
    for c in range(8):
        im = {"x_in": np.ascontiguousarray(x[c])}
        for nm, arr in _CONSTS.items():
            im[nm] = arr
        for nm, arr in w.items():
            im[nm] = arr
        in_maps.append(im)
    res = run_bass_kernel_spmd(nc, in_maps, core_ids=list(range(8)))
    return np.stack([res.results[c]["y_out"] for c in range(8)], axis=0)


# version 13
# speedup vs baseline: 1.1577x; 1.1576x over previous
import os
import numpy as np
import ml_dtypes
from contextlib import ExitStack

import concourse.bass as bass
import concourse.mybir as mybir
from concourse.bass_utils import run_bass_kernel_spmd

F32 = mybir.dt.float32
BF16 = mybir.dt.bfloat16
AF = mybir.ActivationFunctionType
ALU = mybir.AluOpType
AX = mybir.AxisListType
BF = ml_dtypes.bfloat16

T = 2048
D = 1024
NPT = D // 128
NTT = T // 128
HID = 2730
NJ = 22
SC = 512
NSC = T // SC
CPS = SC // 128
STAGE = int(os.environ.get("K_STAGE", "3"))
PH = int(os.environ.get("K_PH", "99"))
NCHL = int(os.environ.get("K_NCHL", "99"))
SOP = int(os.environ.get("K_SOP", "63"))
NLAY = {1: 0, 2: 1, 3: 2}[STAGE]


def _host_consts():
    c = {}
    c["c_idf32"] = np.eye(128, dtype=np.float32)
    c["c_idbf"] = np.eye(128, dtype=BF)
    tri = (np.arange(128)[:, None] <= np.arange(128)[None, :]).astype(np.float32)
    c["c_maskA"] = np.tile(tri, (1, 8)).astype(BF)
    big = np.float32(-1e30)
    c["c_mko"] = np.where(np.arange(128)[:, None] <= np.arange(128)[None, :], 0.0, big).astype(BF)
    c["c_mkp"] = np.where(np.arange(128)[:, None] >= np.arange(128)[None, :], 0.0, big).astype(BF)
    c["c_ones_col_bf"] = np.ones((128, 1), dtype=BF)
    c["c_ones_row_f32"] = np.ones((1, 128), dtype=np.float32)
    bd2 = np.zeros((2, 128), dtype=np.float32)
    bd2[0, 0:64] = 1.0
    bd2[1, 64:128] = 1.0
    bd2r = np.zeros((128, 128), dtype=np.float32)
    bd2r[0:2] = bd2
    bd2r[64:66] = bd2
    c["c_bd2_f32"] = bd2r
    c["c_bd2T_bf"] = np.ascontiguousarray(bd2.T).astype(BF)
    P1 = np.zeros((64, 64), dtype=np.float32)
    for dp in range(32):
        P1[dp + 32, dp] = -1.0
    for dp in range(32, 64):
        P1[dp - 32, dp] = 1.0
    Pr = np.zeros((128, 128), dtype=np.float32)
    Pr[0:64, 0:64] = P1
    Pr[64:128, 64:128] = P1
    c["c_prot"] = Pr.astype(BF)
    c["c_eps"] = np.tile(np.array([[1e-5, 1e-24]], dtype=np.float32), (128, 1))
    bdw = np.zeros((128, 32), dtype=np.float32)
    bdw[0:64, 15] = 1.0
    bdw[64:128, 16] = 1.0
    c["c_bdw"] = bdw.astype(BF)
    bsel = np.zeros((16, 8 * 128), dtype=np.float32)
    for mtq in range(8):
        bsel[2 * mtq, mtq * 128:mtq * 128 + 64] = 1.0
        bsel[2 * mtq + 1, mtq * 128 + 64:mtq * 128 + 128] = 1.0
    c["c_bsel"] = bsel.astype(BF)
    DIM_HEAD, WINDOW = 64, 512
    half = np.arange(0, DIM_HEAD, 2, dtype=np.float64)
    inv_freq = 1.0 / 10000.0 ** (half / DIM_HEAD)
    t = np.arange(T, dtype=np.float64)
    freqs = np.concatenate([t[:, None] * inv_freq[None, :]] * 2, axis=-1)
    base = (half + 0.4 * DIM_HEAD) / (1.4 * DIM_HEAD)
    power = (t - T // 2) / float(WINDOW // 2)
    scale = np.concatenate([base[None, :] ** power[:, None]] * 2, axis=-1)
    cos, sin = np.cos(freqs), np.sin(freqs)
    QKS = 8.0
    for nm, tab in (("c_csq", QKS * (cos * scale).T), ("c_ssq", QKS * (sin * scale).T),
                    ("c_csk", (cos / scale).T), ("c_ssk", (sin / scale).T)):
        c[nm] = np.concatenate([tab, tab], axis=0).astype(np.float32).astype(BF)
    return c


_CONSTS = _host_consts()
DBGNAMES = {}


def _prep_weights(inp):
    w = {}
    asc = np.ascontiguousarray

    def tile_cols(M):
        K, Mf = M.shape
        nmt = -(-Mf // 128)
        P = np.zeros((nmt, K, 128), dtype=np.float32)
        for i in range(nmt):
            blk = M[:, i * 128:(i + 1) * 128]
            P[i, :, :blk.shape[1]] = blk
        return asc(P)

    def col_pt(v):
        return asc(v.reshape(-1, 128).T.astype(np.float32))

    for nm in ("Wq", "Wk", "Wv", "Wo"):
        w["A_" + nm] = tile_cols(asc(inp[f"L0_attn_{nm}"].astype(np.float32).T))
    for nm in ("bq", "bk", "bv", "bo"):
        w["A_" + nm] = col_pt(inp[f"L0_attn_{nm}"].astype(np.float32))
    for l in range(2):
        w[f"B{l}_wqkv"] = tile_cols(inp["L0_ffn_wqkv"][l].astype(np.float32))
        w[f"B{l}_wo"] = tile_cols(inp["L0_ffn_wo"][l].astype(np.float32))
        w1 = inp["L0_ffn_w1"][l].astype(np.float32)
        a_t = tile_cols(w1[:, :HID])
        g_t = tile_cols(w1[:, HID:])
        inter = np.empty((2 * NJ, D, 128), dtype=np.float32)
        inter[0::2] = a_t
        inter[1::2] = g_t
        w[f"B{l}_w1"] = asc(inter)
        w2p = np.zeros((NJ * 128, D), dtype=np.float32)
        w2p[:HID] = inp["L0_ffn_w2"][l].astype(np.float32)
        w[f"B{l}_w2"] = tile_cols(w2p)
        b1 = inp["L0_ffn_b1"][l].astype(np.float32)
        b1a = np.zeros(NJ * 128, dtype=np.float32)
        b1a[:HID] = b1[:HID]
        b1g = np.zeros(NJ * 128, dtype=np.float32)
        b1g[:HID] = b1[HID:]
        w[f"B{l}_b1a"] = col_pt(b1a)
        w[f"B{l}_b1g"] = col_pt(b1g)
        w[f"B{l}_b2"] = col_pt(inp["L0_ffn_b2"][l].astype(np.float32))
        w[f"B{l}_lng"] = col_pt(inp["L0_ffn_ln_g"][l].astype(np.float32))
        w[f"B{l}_lnb"] = col_pt(inp["L0_ffn_ln_b"][l].astype(np.float32))
        w[f"B{l}_qs"] = asc(inp["L0_ffn_q_scale"][l].astype(np.float32)[None, :])
        w[f"B{l}_ks"] = asc(inp["L0_ffn_k_scale"][l].astype(np.float32)[None, :])
    return w


class Ctr:
    def __init__(self, sem, step):
        self.sem = sem
        self.step = step
        self.val = 0

    def inc(self):
        self.val += self.step
        return self.val


class Sched:
    ENGS = ("t", "v", "s", "g", "y")

    def __init__(self):
        self.ops = {e: [] for e in self.ENGS}
        self.wmax = {e: {} for e in self.ENGS}

    def emit(self, eng, fn):
        self.ops[eng].append(fn)

    def wait(self, eng, ctr, val=None):
        v = ctr.val if val is None else val
        if v <= 0:
            return
        key = id(ctr.sem)
        if self.wmax[eng].get(key, -1) >= v:
            return
        self.wmax[eng][key] = v
        self.ops[eng].append(lambda e, s=ctr.sem, v=v: e.wait_ge(s, v))

    def chain_inc(self, eng, fns, ctr):
        self.wait(eng, ctr, ctr.val)
        v = None
        for f in fns[:-1]:
            v = self.last_inc(eng, [f], ctr)
            self.wait(eng, ctr, v)
        return self.last_inc(eng, [fns[-1]], ctr)

    def last_inc(self, eng, fns, ctr):
        assert fns
        for f in fns[:-1]:
            self.ops[eng].append(f)
        self.ops[eng].append(lambda e, f=fns[-1]: f(e).then_inc(ctr.sem, ctr.step))
        return ctr.inc()


class Gate:
    def __init__(self):
        self.ctr = None
        self.val = 0

    def set(self, ctr, val):
        self.ctr = ctr
        self.val = val

    def require(self, sch, eng):
        if self.ctr is not None and self.val > 0:
            sch.wait(eng, self.ctr, self.val)


def build_nc(wspecs):
    nc = bass.Bass("TRN2", target_bir_lowering=False)

    x_in = nc.dram_tensor("x_in", [T, D], F32, kind="ExternalInput").ap()
    y_out = nc.dram_tensor("y_out", [T, D], F32, kind="ExternalOutput").ap()
    x1s = nc.dram_tensor("x1s", [128, NPT * T], F32, kind="Internal").ap()

    cons = {}
    for nm, arr in _CONSTS.items():
        dt = BF16 if arr.dtype == BF else F32
        cons[nm] = nc.dram_tensor(nm, list(arr.shape), dt, kind="ExternalInput").ap()
    wdr = {}
    for nm, shape in wspecs.items():
        wdr[nm] = nc.dram_tensor(nm, list(shape), F32, kind="ExternalInput").ap()

    sch = Sched()
    ctx = ExitStack()

    _uid = [0]

    def _un(name):
        _uid[0] += 1
        un = f"{name}_{_uid[0]}"
        DBGNAMES.setdefault(name, []).append(un)
        return un

    def sb(name, shape, dt, st=None):
        return (st or ctx).enter_context(nc.sbuf_tensor(_un(name), shape, dt))

    def ps(name, shape, st=None):
        return (st or ctx).enter_context(nc.psum_tensor(_un(name), shape, F32))

    def sem(name, step=1):
        return Ctr(ctx.enter_context(nc.semaphore(name)), step)

    with ctx:
        XT = [sb(f"XT{i}", [128, T], F32) for i in range(NPT)]
        cid32 = sb("cid32", [128, 128], F32)
        cidbf = sb("cidbf", [128, 128], BF16)
        conesc = sb("conesc", [128, 1], BF16)
        conesrf = sb("conesrf", [1, 128], F32)
        cbd2f = sb("cbd2f", [128, 128], F32)
        cbd2t = sb("cbd2t", [128, 2], BF16)
        cprot = sb("cprot", [128, 128], BF16)
        cmko = sb("cmko", [128, 128], BF16)
        cmkp = sb("cmkp", [128, 128], BF16)
        cepsc = sb("cepsc", [128, 2], F32)
        cbdw = sb("cbdw", [128, 32], BF16)
        cbsel = sb("cbsel", [16, 1024], BF16)
        nc.const_aps.register(F32, 1e-5, cepsc[:, 0:1])
        nc.const_aps.register(F32, 1e-24, cepsc[:, 1:2])
        wrA = [sb(f"wrA{i}", [128, 1024], BF16) for i in range(4)]

        sLD = sem("sLD", 16)
        sGD = sem("sGD", 16)
        sGW = [sem(f"sGW{i}", 16) for i in range(4)]
        sGO = [sem(f"sGO{i}", 16) for i in range(4)]
        sG2 = [sem("sG20", 16), sem("sG21", 16)]
        sPE = sem("sPE", 1)
        sVE = sem("sVE", 1)
        sSC = sem("sSC", 1)
        sST = sem("sST", 16)

        for nm, dst in (("c_idf32", cid32), ("c_idbf", cidbf),
                        ("c_ones_col_bf", conesc), ("c_ones_row_f32", conesrf),
                        ("c_bd2_f32", cbd2f), ("c_bd2T_bf", cbd2t),
                        ("c_prot", cprot),
                        ("c_mko", cmko), ("c_mkp", cmkp), ("c_eps", cepsc),
                        ("c_bdw", cbdw), ("c_bsel", cbsel)):
            sch.emit("y", lambda e, nm=nm, dst=dst:
                     e.dma_start(out=dst[:, :], in_=cons[nm]).then_inc(sLD.sem, 16))
            sLD.inc()

        wrA_gate = [Gate(), Gate(), Gate(), Gate()]
        wrO_gate = [Gate(), Gate(), Gate(), Gate()]
        wr2_gate = [Gate(), Gate()]

        def wload(nm, sub, dstbuf, gate=None, ctr=None, cols=None):
            src = wdr[nm][sub].rearrange("(a p) m -> p a m", p=128)
            dst = (dstbuf[:, 0:cols] if cols else dstbuf[:, :]).rearrange(
                "p (a m) -> p a m", m=128)
            ctr = ctr or sGD
            if gate is not None:
                gate.require(sch, "g")
            sch.emit("g", lambda e, dst=dst, src=src, ctr=ctr:
                     e.dma_start(out=dst, in_=src).then_inc(ctr.sem, 16))
            return ctr


        def dma_phase_gate():
            for qq in ("y", "g"):
                sch.wait(qq, sPE, sPE.val)
                sch.wait(qq, sVE, sVE.val)
                sch.wait(qq, sSC, sSC.val)

        def colload(nm, dst, ctr=None):
            ctr = ctr or sLD
            sch.emit("y", lambda e, nm=nm, dst=dst, ctr=ctr:
                     e.dma_start(out=dst, in_=wdr[nm]).then_inc(ctr.sem, 16))
            return ctr.inc()

        def transpose_batch(srcs, dstw, pbufs, ev):
            nb = len(srcs) // 4
            for b in range(nb):
                buf = b % 2
                ev[buf].require(sch, "t")
                fns = []
                for j in range(4):
                    sap = srcs[b * 4 + j]
                    fns.append(lambda e, buf=buf, j=j, sap=sap: e.matmul(
                        pbufs[buf][:, j * 128:(j + 1) * 128], sap, cidbf[:, :],
                        start=True, stop=True))
                vp = sch.last_inc("t", fns, sPE)
                sch.wait("s", sPE, vp)
                vs = sch.last_inc("s", [lambda e, buf=buf, b=b: dstw[b](e, pbufs[buf])], sSC)
                ev[buf].set(sSC, vs)
            return sSC.val

        if STAGE >= 2:
            stPB = ExitStack()
            tabq = [sb("tcsq", [128, T], BF16, stPB), sb("tssq", [128, T], BF16, stPB)]
            tabk = [sb("tcsk", [128, T], BF16, stPB), sb("tssk", [128, T], BF16, stPB)]
            LCON = []
            sTB = sem("sTB", 16)
            sCL = [sem("sCL0", 16), sem("sCL1", 16)]
            for l in range(NLAY):
                cc = {}
                cc["qscol"] = sb("qscol", [128, 1], F32, stPB)
                cc["kscol"] = sb("kscol", [128, 1], F32, stPB)
                SROW = sb("SROW", [1, 256], F32, stPB)
                cc["qsrow"] = SROW[0:1, 0:64]
                cc["ksrow"] = SROW[0:1, 64:128]
                cc["m0r"] = SROW[0:1, 128:129]
                cc["m0r2"] = SROW[0:1, 129:130]
                cc["negm0"] = sb("negm0", [128, 1], F32, stPB)
                cc["lngc"] = sb("lngc", [128, 8], F32, stPB)
                cc["lnbc"] = sb("lnbc", [128, 8], F32, stPB)
                cc["b1ac"] = sb("b1ac", [128, NJ], F32, stPB)
                cc["b1gc"] = sb("b1gc", [128, NJ], F32, stPB)
                cc["b2c"] = sb("b2c", [128, 8], F32, stPB)
                LCON.append(cc)

        with ExitStack() as st0:
            xtok = sb("xtok", [128, NTT * D], F32, st0)
            xr = x_in.rearrange("(tt p) d -> p tt d", p=128)
            xtok3 = xtok[:, :].rearrange("p (tt d) -> p tt d", d=D)
            sch.emit("y", lambda e: e.dma_start(out=xtok3, in_=xr).then_inc(sLD.sem, 16))
            sLD.inc()
            sch.wait("t", sLD, sLD.val)
            pT = [ps("pT0", [128, 512], st0), ps("pT1", [128, 512], st0)]
            ev = [0, 0]
            idx = 0
            for tt in range(NTT):
                for dg in range(2):
                    buf = idx % 2
                    if idx >= 2:
                        sch.wait("t", sVE, ev[buf])
                    fns = []
                    for j in range(4):
                        dt_i = dg * 4 + j
                        fns.append(lambda e, buf=buf, j=j, tt=tt, dt_i=dt_i: e.matmul(
                            pT[buf][:, j * 128:(j + 1) * 128],
                            xtok[:, tt * D + dt_i * 128: tt * D + dt_i * 128 + 128],
                            cid32[:, :], start=True, stop=True))
                    vp = sch.last_inc("t", fns, sPE)
                    sch.wait("v", sPE, vp)
                    fns = [lambda e, buf=buf, j=j, tt=tt, dt_i=dg * 4 + j: e.tensor_copy(
                        XT[dt_i][:, tt * 128:(tt + 1) * 128],
                        pT[buf][:, j * 128:(j + 1) * 128]) for j in range(4)]
                    ev[buf] = sch.last_inc("v", fns, sVE)
                    idx += 1
        v_xt_ready = sVE.val

        if STAGE >= 2:
            for nm, dst in (("c_csq", tabq[0]), ("c_ssq", tabq[1]),
                            ("c_csk", tabk[0]), ("c_ssk", tabk[1])):
                sch.emit("y", lambda e, nm=nm, dst=dst:
                         e.dma_start(out=dst[:, :], in_=cons[nm]).then_inc(sTB.sem, 16))
                sTB.inc()
            v_tabs_all = sTB.val
            for l in range(NLAY):
                WN = f"B{l}_"
                cc = LCON[l]
                m0r = cc["m0r"]
                m0r2 = cc["m0r2"]
                for nm, dst in ((WN + "lng", cc["lngc"][:, :]), (WN + "lnb", cc["lnbc"][:, :]),
                                (WN + "b1a", cc["b1ac"][:, :]), (WN + "b1g", cc["b1gc"][:, :]),
                                (WN + "b2", cc["b2c"][:, :]), (WN + "qs", cc["qsrow"]),
                                (WN + "ks", cc["ksrow"])):
                    colload(nm, dst, sCL[l])
                for srcn, dst in ((WN + "qs", cc["qscol"]), (WN + "ks", cc["kscol"])):
                    for hb in range(2):
                        sch.emit("y", lambda e, srcn=srcn, dst=dst, hb=hb, l=l: e.dma_start(
                            out=dst[hb * 64:(hb + 1) * 64, 0:1],
                            in_=wdr[srcn].rearrange("a b -> b a")).then_inc(sCL[l].sem, 16))
                        sCL[l].inc()
                cc["ctr"] = sCL[l]
                cc["v_bias"] = sCL[l].val
                sch.wait("v", sCL[l], cc["v_bias"])
                vv = sch.chain_inc("v", [
                    lambda e, m0r=m0r, q=cc["qsrow"]: e.reduce_max(
                        m0r, q, AX.X, apply_absolute_value=True),
                    lambda e, m0r2=m0r2, ks=cc["ksrow"]: e.reduce_max(
                        m0r2, ks, AX.X, apply_absolute_value=True),
                    lambda e, m0r=m0r, m0r2=m0r2: e.tensor_mul(m0r, m0r, m0r2),
                    lambda e, m0r=m0r: e.tensor_scalar_mul(m0r, m0r, -8.0)], sVE)
                with ExitStack() as stm:
                    pM = ps("pM0", [128, 1], stm)
                    sch.wait("t", sVE, vv)
                    if l > 0:
                        sch.wait("t", sSC, LCON[l - 1]["v_m0"])
                    vp = sch.last_inc("t", [lambda e, m0r=m0r: e.matmul(
                        pM[:, :], conesrf[:, :], m0r, start=True, stop=True)], sPE)
                    sch.wait("s", sPE, vp)
                    cc["v_m0"] = sch.last_inc("s", [lambda e, n=cc["negm0"]: e.copy(
                        n[:, :], pM[:, :])], sSC)


        def emit_ln_plain(spill=False):
            sch.wait("v", sPE, sPE.val)
            with ExitStack() as stl:
                R = sb("lnrows", [1, 5 * T], F32, stl)
                lnbf = sb("lnbf", [128, T], BF16, stl)
                lnsq = sb("lnsq", [128, T], BF16, stl)
                r_sy = R[0:1, 0:T]
                r_sq = R[0:1, T:2 * T]
                r_t = R[0:1, 2 * T:3 * T]
                r_A = R[0:1, 3 * T:4 * T]
                r_B = R[0:1, 4 * T:5 * T]
                with ExitStack() as stp:
                    pSum = ps("pSum", [1, T], stp)
                    pSq = ps("pSq", [1, T], stp)
                    for pt in range(NPT):
                        fns = [lambda e, pt=pt: e.tensor_copy(lnbf[:, :], XT[pt][:, :]),
                               lambda e: e.tensor_mul(lnsq[:, :], lnbf[:, :], lnbf[:, :])]
                        vc = sch.chain_inc("v", fns, sVE)
                        sch.wait("t", sVE, vc)
                        fns = []
                        for ch in range(4):
                            fns.append(lambda e, ch=ch, pt=pt: e.matmul(
                                pSum[0:1, ch * 512:(ch + 1) * 512], conesc[:, :],
                                lnbf[:, ch * 512:(ch + 1) * 512],
                                start=(pt == 0), stop=(pt == NPT - 1)))
                            fns.append(lambda e, ch=ch, pt=pt: e.matmul(
                                pSq[0:1, ch * 512:(ch + 1) * 512], conesc[:, :],
                                lnsq[:, ch * 512:(ch + 1) * 512],
                                start=(pt == 0), stop=(pt == NPT - 1)))
                        vp = sch.last_inc("t", fns, sPE)
                        sch.wait("v", sPE, vp)
                    sch.wait("s", sPE, sPE.val)
                    vs = sch.last_inc("s", [
                        lambda e: e.copy(r_sy, pSum[0:1, :]),
                        lambda e: e.copy(r_sq, pSq[0:1, :])], sSC)
                sch.wait("v", sSC, vs)
                vv = sch.chain_inc("v", [
                    lambda e: e.tensor_scalar_mul(r_sy, r_sy, 1.0 / D),
                    lambda e: e.tensor_scalar_mul(r_sq, r_sq, 1.0 / D),
                    lambda e: e.tensor_mul(r_t, r_sy, r_sy),
                    lambda e: e.tensor_sub(r_t, r_sq, r_t)], sVE)
                sch.wait("s", sVE, vv)
                vs = sch.chain_inc("s", [
                    lambda e: e.activation(r_t, r_t, AF.Ln, bias=1e-5),
                    lambda e: e.activation(r_A, r_t, AF.Exp, scale=-0.5)], sSC)
                sch.wait("v", sSC, vs)
                vv = sch.last_inc("v", [lambda e: e.scalar_tensor_tensor(
                    r_B, r_sy, -1.0, r_A, ALU.mult, ALU.mult)], sVE)
                with ExitStack() as stp:
                    pBA = [ps("pBA0", [128, 512], stp), ps("pBA1", [128, 512], stp)]
                    pBB = [ps("pBB0", [128, 512], stp), ps("pBB1", [128, 512], stp)]
                    tmpf = sb("lntmp", [128, 512], F32, stp)
                    ev2 = [0, 0]
                    for ch in range(4):
                        buf = ch % 2
                        sch.wait("t", sVE, vv)
                        if ch >= 2:
                            sch.wait("t", sVE, ev2[buf])
                        fns = [lambda e, buf=buf, ch=ch: e.matmul(
                            pBA[buf][:, :], conesrf[:, :],
                            R[0:1, 3 * T + ch * 512: 3 * T + (ch + 1) * 512],
                            start=True, stop=True),
                            lambda e, buf=buf, ch=ch: e.matmul(
                            pBB[buf][:, :], conesrf[:, :],
                            R[0:1, 4 * T + ch * 512: 4 * T + (ch + 1) * 512],
                            start=True, stop=True)]
                        vp = sch.last_inc("t", fns, sPE)
                        sch.wait("v", sPE, vp)
                        fns = []
                        for pt in range(NPT):
                            fns.append(lambda e, pt=pt, ch=ch, buf=buf: e.tensor_mul(
                                tmpf[:, :], XT[pt][:, ch * 512:(ch + 1) * 512], pBA[buf][:, :]))
                            fns.append(lambda e, pt=pt, ch=ch, buf=buf: e.tensor_add(
                                XT[pt][:, ch * 512:(ch + 1) * 512], tmpf[:, :], pBB[buf][:, :]))
                        ev2[buf] = sch.chain_inc("v", fns, sVE)
                    if spill:
                        vv2 = sVE.val
                        for pt in range(NPT):
                            def sp(e, pt=pt, vv2=vv2):
                                e.wait_ge(sVE.sem, vv2)
                                return e.dma_start(out=x1s[:, pt * T:(pt + 1) * T],
                                                   in_=XT[pt][:, :]).then_inc(sLD.sem, 16)
                            sch.emit("y", sp)
                            sLD.inc()

        with ExitStack() as stA:
            XB = [sb(f"XB{i}", [128, SC], BF16, stA) for i in range(NPT)]
            qT = [sb(f"qTs{i}", [128, SC], BF16, stA) for i in range(NPT)]
            kT = [sb(f"kTs{i}", [128, SC], BF16, stA) for i in range(NPT)]
            vtokA = sb("vtokA", [128, CPS * 1024], BF16, stA)
            ktokA = sb("ktokA", [128, CPS * 1024], BF16, stA)
            aout = sb("aout", [128, CPS * 1024], BF16, stA)
            aoT = sb("aoT", [128, NPT * SC], BF16, stA)
            vdm = [sb(f"vdm{i}", [128, SC], BF16, stA) for i in range(2)]
            eqs = [sb(f"eqs{i}", [128, SC], F32, stA) for i in range(2)]
            rus = [sb(f"rus{i}", [128, SC], F32, stA) for i in range(2)]
            bqc = sb("bqc", [128, 8], F32, stA)
            bkc = sb("bkc", [128, 8], F32, stA)
            bvc = sb("bvc", [128, 8], F32, stA)
            boc = sb("boc", [128, 8], F32, stA)
            maskedS = [sb(f"mskd{i}", [128, 2048], BF16, stA) for i in range(2)]
            Ssnap = sb("Ssnap", [128, 1040], BF16, stA)
            Ssb = sb("Ssb", [64, 1024], F32, stA)
            zsb = sb("zsb", [64, 16], F32, stA)
            rdenA = sb("rdenA", [128, 16], F32, stA)

            cmaskA = sb("cmaskA", [128, 1024], BF16, stA)
            dma_phase_gate()
            for nm, dst in (("A_bq", bqc[:, :]), ("A_bk", bkc[:, :]),
                            ("A_bv", bvc[:, :]), ("A_bo", boc[:, :])):
                colload(nm, dst)
            sch.emit("y", lambda e: e.dma_start(
                out=cmaskA[:, :], in_=cons["c_maskA"]).then_inc(sLD.sem, 16))
            sLD.inc()
            v_abias = sLD.val
            sch.chain_inc("v", [lambda e: e.memset(Ssb[:, :], 0.0),
                                lambda e: e.memset(zsb[:, :], 0.0)], sVE)

            stScan = ExitStack()
            sSN = sem("sSN", 16)
            tev = [Gate(), Gate()]

            def emit_partA_sc(sc):
                c0 = sc * SC
                if sc > 0:
                    sch.wait("v", sPE, sPE.val)
                    sch.wait("s", sPE, sPE.val)
                sch.wait("v", sVE, v_xt_ready)
                fns = [lambda e, i=i, c0=c0: e.tensor_copy(XB[i][:, :], XT[i][:, c0:c0 + SC])
                       for i in range(NPT)]
                v_xb = sch.last_inc("v", fns, sVE)
                sch.wait("v", sLD, v_abias)
                if sc > 0:
                    sch.wait("t", sSC, sSC.val)

                with ExitStack() as stp:
                    pQ = [ps(f"pQ{i}", [128, SC], stp) for i in range(4)]
                    pVT = [ps("pVTa", [128, 512], stp), ps("pVTb", [128, 512], stp)]
                    slot_free = [Gate(), Gate(), Gate(), Gate()]
                    idx = 0
                    last_phi = [0, 0]
                    for which, bc, kind in (("A_Wq", bqc, "q"), ("A_Wk", bkc, "k"),
                                            ("A_Wv", bvc, "v")):
                        for mt in range(NPT):
                            wslot = idx % 4
                            slot = idx % 4
                            s2 = idx % 2
                            wc = wload(which, mt, wrA[wslot], wrA_gate[wslot], sGW[wslot])
                            sch.wait("t", wc, wc.inc())
                            sch.wait("t", sVE, v_xb)
                            slot_free[slot].require(sch, "t")
                            fns = []
                            for kt in range(NPT):
                                fns.append(lambda e, wslot=wslot, slot=slot, kt=kt: e.matmul(
                                    pQ[slot][:, :],
                                    wrA[wslot][:, kt * 128:(kt + 1) * 128],
                                    XB[kt][:, :],
                                    start=(kt == 0), stop=(kt == NPT - 1)))
                            vp = sch.last_inc("t", fns, sPE)
                            wrA_gate[wslot].set(sPE, vp)
                            sch.wait("s", sPE, vp)
                            sch.wait("s", sLD, v_abias)
                            if kind in ("q", "k"):
                                dstT = qT if kind == "q" else kT
                                sch.wait("s", sVE, last_phi[s2])
                                vs = sch.last_inc("s", [
                                    lambda e, slot=slot, s2=s2, bc=bc, mt=mt:
                                    e.activation(eqs[s2][:, :], pQ[slot][:, :], AF.Exp,
                                                 bias=bc[:, mt:mt + 1])], sSC)
                                sch.wait("v", sSC, vs)
                                fns = [lambda e, slot=slot, s2=s2, bc=bc, mt=mt:
                                       e.tensor_scalar(rus[s2][:, :], pQ[slot][:, :],
                                                       bc[:, mt:mt + 1], 0.0, ALU.add, ALU.max),
                                       lambda e, dstT=dstT, s2=s2, mt=mt:
                                       e.scalar_tensor_tensor(dstT[mt][:, :], eqs[s2][:, :],
                                                              1.0, rus[s2][:, :],
                                                              ALU.min, ALU.add)]
                                vv = sch.chain_inc("v", fns, sVE)
                                slot_free[slot].set(sVE, vv)
                                last_phi[s2] = vv
                            else:
                                sch.wait("v", sPE, vp)
                                vs = sch.last_inc("v", [lambda e, slot=slot, s2=s2, mt=mt:
                                                        e.tensor_scalar_add(vdm[s2][:, :],
                                                                            pQ[slot][:, :],
                                                                            bvc[:, mt:mt + 1])], sVE)
                                slot_free[slot].set(sVE, vs)
                                sch.wait("t", sVE, vs)
                                srcs = [vdm[s2][:, j * 128:(j + 1) * 128] for j in range(CPS)]

                                def mkv(mt, s2):
                                    def wr(e, pbuf):
                                        oap = bass.AP(
                                            tensor=vtokA, offset=mt * 128,
                                            ap=[[CPS * 1024, 128], [1024, CPS], [1, 128]])
                                        iap = pbuf[:, :].rearrange("p (a m) -> p a m", m=128)
                                        return e.copy(oap, iap)
                                    return wr
                                transpose_batch(srcs, [mkv(mt, s2)], pVT, tev)
                            idx += 1
                    for dpt in range(NPT):
                        for sl in range(4):
                            slot_free[sl].require(sch, "t")
                        srcs = [kT[dpt][:, j * 128:(j + 1) * 128] for j in range(CPS)]

                        def mkk(dpt):
                            def wr(e, pbuf):
                                oap = bass.AP(
                                    tensor=ktokA, offset=dpt * 128,
                                    ap=[[CPS * 1024, 128], [1024, CPS], [1, 128]])
                                iap = pbuf[:, :].rearrange("p (a m) -> p a m", m=128)
                                return e.copy(oap, iap)
                            return wr
                        transpose_batch(srcs, [mkk(dpt)], pVT, tev)
                v_proj_done = sSC.val

                if PH < 2:
                    return
                with ExitStack() as sts:
                    pSC0 = ps("pSCa", [128, 1024], sts)
                    pSC = [pSC0, pSC0]
                    pN = [ps("pNa", [128, 512], sts), ps("pNb", [128, 512], sts)]
                    pS = ps("pS", [64, 1024], sts)
                    pZz = ps("pZz", [64, 32], sts)
                    pZden = ps("pZden", [128, 32], sts)
                    v_evac_prev = sVE.val
                    v_state_prev = sVE.val
                    v_div_prev = sSC.val
                    v_div_half = [sSC.val, sSC.val]
                    pe_chunk_done = sPE.val
                    last_tt = [0, 0]
                    sch.wait("t", sSC, v_proj_done)
                    sch.wait("t", sVE, sVE.val)
                    for cc in range(CPS):
                        c = sc * CPS + cc
                        if c >= NCHL:
                            continue
                        par = c % 2
                        if c > 0:
                            sch.wait("s", sPE, pe_chunk_done)
                            sch.wait("s", sVE, v_state_prev)
                            vsnap = sch.last_inc("s", [
                                lambda e: e.copy(Ssnap[0:64, 0:1024], Ssb[:, :]),
                                lambda e: e.copy(Ssnap[0:64, 1024:1040], zsb[:, :])], sSC)
                            sch.wait("g", sSC, vsnap)
                            vdup = sch.last_inc("g", [lambda e: e.dma_start(
                                out=Ssnap[64:128, :], in_=Ssnap[0:64, :])], sSN)
                        else:
                            vsnap = sSC.val
                            vdup = sSN.val
                        v_tt = [0, 0]
                        for half in range(2):
                            sch.wait("t", sVE, last_tt[0])
                            sch.wait("t", sVE, last_tt[1])
                            fns = []
                            order = [1, 3, 5, 7, None, 0, 2, 4, 6] if half == 0 \
                                else [0, 2, 4, 6, None, 1, 3, 5, 7]
                            for hh in order:
                                if hh is None:
                                    fns.append(lambda e: e.drain())
                                    continue
                                h = half * 8 + hh
                                pp, sub = h // 2, (h % 2) * 64
                                fns.append(lambda e, half=half, hh=hh, pp=pp, sub=sub, cc=cc:
                                           e.matmul(pSC[half][:, hh * 128:(hh + 1) * 128],
                                                    kT[pp][sub:sub + 64, cc * 128:(cc + 1) * 128],
                                                    qT[pp][sub:sub + 64, cc * 128:(cc + 1) * 128],
                                                    start=True, stop=True))
                            vm1 = sch.last_inc("t", fns, sPE)
                            sch.wait("v", sPE, vm1)
                            v_tt[half] = sch.last_inc("v", [
                                lambda e, half=half, par=par: e.tensor_mul(
                                    maskedS[par][:, half * 1024:(half + 1) * 1024],
                                    pSC[half][:, :], cmaskA[:, :])], sVE)
                            last_tt[half] = v_tt[half]
                            last_tt[1 - half] = max(last_tt[1 - half], 0)
                        sch.wait("t", sSC, vsnap)
                        sch.wait("t", sSN, vdup)
                        sch.wait("t", sVE, v_evac_prev)
                        for half in range(2):
                            sch.wait("t", sSC, v_div_half[half])
                            sch.wait("t", sVE, v_tt[half])
                            fns = []
                            for hh in range(8):
                                h = half * 8 + hh
                                pp, sub = h // 2, (h % 2) * 64
                                moff = half * 1024 + hh * 128
                                vcol = cc * 1024 + h * 64
                                if SOP & 1:
                                    fns.append(lambda e, par=par, half=half, hh=hh, moff=moff,
                                           vcol=vcol, st=(hh == 0),
                                           sp=(c == 0 and hh == 7):
                                           e.matmul(pN[half][:, hh * 64:(hh + 1) * 64],
                                                    maskedS[par][:, moff:moff + 128],
                                                    vtokA[:, vcol:vcol + 64],
                                                    start=st, stop=sp))
                                if SOP & 2:
                                    fns.append(lambda e, par=par, moff=moff, h=h,
                                           st=(h == 0), sp=(c == 0 and h == 15):
                                           e.matmul(pZden[:, 2 * h:2 * h + 1],
                                                    maskedS[par][:, moff:moff + 128],
                                                    conesc[:, :], start=st, stop=sp))
                                if c > 0 and (SOP & 4):
                                    fns.append(lambda e, pp=pp, sub=sub, cc=cc, half=half, hh=hh, h=h:
                                               e.matmul(pN[half][:, hh * 64:(hh + 1) * 64],
                                                        qT[pp][sub:sub + 64, cc * 128:(cc + 1) * 128],
                                                        Ssnap[sub:sub + 64, h * 64:h * 64 + 64],
                                                        start=False, stop=(hh == 7)))
                                if SOP & 8:
                                    fns.append(lambda e, vcol=vcol, h=h:
                                           e.matmul(pS[0:64, h * 64:h * 64 + 64],
                                                    ktokA[:, vcol:vcol + 64],
                                                    vtokA[:, vcol:vcol + 64],
                                                    start=True, stop=True))
                                if SOP & 16:
                                    fns.append(lambda e, vcol=vcol, h=h:
                                           e.matmul(pZz[:, 2 * h:2 * h + 1],
                                                    ktokA[:, vcol:vcol + 64],
                                                    conesc[:, :], start=True, stop=True))
                                if c > 0 and (SOP & 2):
                                    fns.append(lambda e, pp=pp, sub=sub, cc=cc, h=h:
                                               e.matmul(pZden[:, 2 * h:2 * h + 1],
                                                        qT[pp][sub:sub + 64, cc * 128:(cc + 1) * 128],
                                                        Ssnap[sub:sub + 64, 1024 + h:1025 + h],
                                                        start=False, stop=(h == 15)))
                            pe_chunk_done = sch.last_inc("t", fns, sPE)
                        sch.wait("v", sPE, pe_chunk_done)
                        if not (SOP & 32):
                            continue
                        pZdh = bass.AP(tensor=pZden, offset=0, ap=[[32, 128], [2, 16]])
                        pZzh = bass.AP(tensor=pZz, offset=0, ap=[[32, 64], [2, 16]])
                        fns = []
                        if SOP & 8:
                            fns.append(lambda e: e.tensor_add(Ssb[:, :], Ssb[:, :], pS[0:64, :]))
                        if SOP & 16:
                            fns.append(lambda e, pZzh=pZzh: e.tensor_add(zsb[:, :], zsb[:, :], pZzh))
                        if fns:
                            v_state_prev = sch.chain_inc("v", fns, sVE)
                        if SOP & 2:
                            sch.wait("v", sSC, v_div_prev)
                            v_recip = sch.chain_inc("v", [
                                lambda e, pZdh=pZdh: e.reciprocal(rdenA[:, :], pZdh)], sVE)
                            sch.wait("s", sVE, v_recip)
                        for half in (range(2) if (SOP & 1) and (SOP & 2) else []):
                            fns = []
                            for hh in range(8):
                                h = half * 8 + hh
                                fns.append(lambda e, half=half, hh=hh, h=h, cc=cc:
                                           e.mul(
                                               aout[:, cc * 1024 + h * 64: cc * 1024 + h * 64 + 64],
                                               pN[half][:, hh * 64:(hh + 1) * 64],
                                               rdenA[:, h:h + 1]))
                            v_div_half[half] = sch.last_inc("s", fns, sSC)
                        v_div_prev = v_div_half[1]
                        v_evac_prev = v_state_prev

                if PH < 3:
                    return
                with ExitStack() as stt:
                    pVT2 = [ps("pVTc", [128, 512], stt), ps("pVTd", [128, 512], stt)]
                    tev2 = [Gate(), Gate()]
                    sch.wait("t", sVE, v_evac_prev)
                    sch.wait("t", sSC, v_div_prev)
                    for tt in range(CPS):
                        srcs = [aout[:, tt * 1024 + j * 128: tt * 1024 + (j + 1) * 128]
                                for j in range(8)]
                        dstw = []
                        for b in range(2):
                            def mka(b, tt):
                                def wr(e, pbuf):
                                    oap = bass.AP(
                                        tensor=aoT, offset=(b * 4) * SC + tt * 128,
                                        ap=[[NPT * SC, 128], [SC, 4], [1, 128]])
                                    iap = pbuf[:, :].rearrange("p (a m) -> p a m", m=128)
                                    return e.copy(oap, iap)
                                return wr
                            dstw.append(mka(b, tt))
                        transpose_batch(srcs, dstw, pVT2, tev2)
                v_aoT = sSC.val

                with ExitStack() as stw:
                    pW = [ps("pWa", [128, SC], stw), ps("pWb", [128, SC], stw)]
                    slot_free = [Gate(), Gate()]
                    for mt in range(NPT):
                        wslot = mt % 4
                        slot = mt % 2
                        wc = wload("A_Wo", mt, wrA[wslot], wrA_gate[wslot], sGW[wslot])
                        sch.wait("t", wc, wc.inc())
                        sch.wait("t", sSC, v_aoT)
                        slot_free[slot].require(sch, "t")
                        fns = []
                        for kt in range(NPT):
                            fns.append(lambda e, wslot=wslot, slot=slot, kt=kt: e.matmul(
                                pW[slot][:, :],
                                wrA[wslot][:, kt * 128:(kt + 1) * 128],
                                aoT[:, kt * SC:(kt + 1) * SC],
                                start=(kt == 0), stop=(kt == NPT - 1)))
                        vp = sch.last_inc("t", fns, sPE)
                        wrA_gate[wslot].set(sPE, vp)
                        sch.wait("v", sPE, vp)
                        vv = sch.last_inc("v", [
                            lambda e, slot=slot, mt=mt, c0=c0: e.scalar_tensor_tensor(
                                XT[mt][:, c0:c0 + SC], pW[slot][:, :], boc[:, mt:mt + 1],
                                XT[mt][:, c0:c0 + SC], ALU.add, ALU.add)], sVE)
                        slot_free[slot].set(sVE, vv)
            for sc in range(NSC):
                emit_partA_sc(sc)
            stScan.close()
        if PH >= 3:
            emit_ln_plain(spill=(STAGE >= 2))
        v_spill = sLD.val

        if STAGE >= 2:

            def emit_layerB(l):
                WN = f"B{l}_"
                cc = LCON[l]
                qscol = cc["qscol"]
                kscol = cc["kscol"]
                negm0 = cc["negm0"]
                lngc = cc["lngc"]
                lnbc = cc["lnbc"]
                b1ac = cc["b1ac"]
                b1gc = cc["b1gc"]
                b2c = cc["b2c"]
                v_bias = cc["v_bias"]
                v_m0 = cc["v_m0"]
                v_tabs = v_tabs_all
                with ExitStack() as stL:
                    oTsb = sb("oTsb", [128, 4 * T], BF16, stL)
                    with ExitStack() as stAt:
                        q2sb = sb("q2sb", [128, 4 * T], BF16, stAt)
                        k2sb = sb("k2sb", [128, 4 * T], BF16, stAt)
                        vtokB = sb("vtokB", [128, NTT * 520], BF16, stAt)
                        with ExitStack() as stq:
                            hbuf = [sb(f"hbuf{i}", [128, 1024], BF16, stq)
                                    for i in range(NPT)]
                            sqb = [sb(f"sqb{i}", [128, 1024], BF16, stq) for i in range(2)]
                            qraw = [sb(f"qraw{i}", [128, 1024], BF16, stq)
                                    for i in range(8)]
                            g_sqb = [Gate() for _ in range(2)]
                            g_qraw = [Gate() for _ in range(8)]
                            rstd16 = sb("rstd16", [16, 1024], BF16, stq)
                            qqs = [sb(f"qqs{i}", [128, 512], BF16, stq) for i in range(2)]
                            t1sb = sb("t1sb", [128, 512], BF16, stq)
                            t2sb = sb("t2sb", [128, 512], BF16, stq)
                            vdmB = sb("vdmB", [128, 1024], BF16, stq)

                            def shift_cast(nb, hb):
                                c0n = nb * 1024
                                fns = []
                                for pt in range(NPT):
                                    if pt < 4:
                                        fns.append(lambda e, pt=pt, c0n=c0n: e.tensor_copy(
                                            hb[pt][:, :], XT[pt][:, c0n:c0n + 1024]))
                                    elif nb == 0:
                                        fns.append(lambda e, pt=pt: e.memset(
                                            hb[pt][:, 0:1], 0.0))
                                        fns.append(lambda e, pt=pt: e.tensor_copy(
                                            hb[pt][:, 1:1024], XT[pt][:, 0:1023]))
                                    else:
                                        fns.append(lambda e, pt=pt, c0n=c0n: e.tensor_copy(
                                            hb[pt][:, :], XT[pt][:, c0n - 1:c0n + 1023]))
                                return sch.last_inc("v", fns, sVE)

                            pQK = [ps("pQKa", [128, 1024], stq), ps("pQKb", [128, 1024], stq)]
                            pS16 = ps("pS16", [16, 1024], stq)
                            pRT = [ps("pRTa", [128, 512], stq), ps("pRTb", [128, 512], stq)]
                            slot_free = [Gate(), Gate()]
                            g_bRB = Gate()
                            g_rot = Gate()
                            g_qqs = [Gate(), Gate()]
                            tevB = [Gate(), Gate()]
                            v_bc_end = 0
                            v_rstd_prev = 0
                            for nb in range(2):
                                c0n = nb * 1024
                                sch.wait("v", sPE, sPE.val)
                                v_h = shift_cast(nb, hbuf)
                                if nb == 0 and l == 0:
                                    sch.wait("t", sSC, LCON[-1]["v_m0"])
                                v_sum_done = 0
                                for mt in range(12):
                                    wslot = mt % 4
                                    slot = mt % 2
                                    wc = wload(WN + "wqkv", mt, wrA[wslot], wrA_gate[wslot], sGW[wslot])
                                    sch.wait("t", wc, wc.inc())
                                    sch.wait("t", sVE, v_h)
                                    slot_free[slot].require(sch, "t")
                                    fns = []
                                    for kt in range(NPT):
                                        for h2 in range(2):
                                            fns.append(lambda e, wslot=wslot, slot=slot, kt=kt, h2=h2:
                                                       e.matmul(
                                                           pQK[slot][:, h2 * 512:(h2 + 1) * 512],
                                                           wrA[wslot][:, kt * 128:(kt + 1) * 128],
                                                           hbuf[kt][:, h2 * 512:(h2 + 1) * 512],
                                                           start=(kt == 0), stop=(kt == NPT - 1)))
                                    vp = sch.last_inc("t", fns, sPE)
                                    wrA_gate[wslot].set(sPE, vp)
                                    if mt < 8:
                                        s4 = mt % 2
                                        sch.wait("s", sPE, vp)
                                        sch.wait("s", cc["ctr"], v_bias)
                                        g_sqb[s4].require(sch, "s")
                                        g_qraw[mt].require(sch, "s")
                                        vs = sch.last_inc("s", [
                                            lambda e, slot=slot, s4=s4: e.activation(
                                                sqb[s4][:, :], pQK[slot][:, :],
                                                AF.Square)], sSC)
                                        v_qcp = sch.last_inc("s", [
                                            lambda e, slot=slot, mt=mt: e.copy(
                                                qraw[mt][:, :], pQK[slot][:, :])], sSC)
                                        slot_free[slot].set(sSC, v_qcp)
                                        sch.wait("t", sSC, vs)
                                        if mt == 0 and nb == 1:
                                            sch.wait("t", sSC, v_rstd_prev)
                                        fns = []
                                        for ch in range(2):
                                            fns.append(lambda e, s4=s4, ch=ch, mt=mt: e.matmul(
                                                pS16[:, ch * 512:(ch + 1) * 512],
                                                cbdw[:, 15 - 2 * mt:31 - 2 * mt],
                                                sqb[s4][:, ch * 512:(ch + 1) * 512],
                                                start=(mt == 0), stop=(mt == 7)))
                                        v_sum_done = sch.last_inc("t", fns, sPE)
                                        g_sqb[s4].set(sPE, v_sum_done)
                                    else:
                                        mtv = mt - 8
                                        if mtv == 0:
                                            tevB[0].set(sVE, sVE.val)
                                            tevB[1].set(sVE, sVE.val)
                                        sch.wait("s", sPE, vp)
                                        vs = sch.last_inc("s", [
                                            lambda e, slot=slot: e.copy(
                                                vdmB[:, :], pQK[slot][:, :])], sSC)
                                        slot_free[slot].set(sSC, vs)
                                        sch.wait("t", sSC, vs)
                                        srcs = [vdmB[:, j * 128:(j + 1) * 128]
                                                for j in range(8)]
                                        dstw = []
                                        for b in range(2):
                                            def mkb(b, mtv, nb):
                                                def wr(e, pbuf):
                                                    tt0 = nb * 8 + b * 4
                                                    oap = bass.AP(
                                                        tensor=vtokB,
                                                        offset=tt0 * 520 + (2 * mtv) * 65,
                                                        ap=[[NTT * 520, 128], [520, 4],
                                                            [65, 2], [1, 64]])
                                                    iap = pbuf[:, :].rearrange(
                                                        "p (a h m) -> p a h m", h=2, m=64)
                                                    return e.copy(oap, iap)
                                                return wr
                                            dstw.append(mkb(b, mtv, nb))
                                        transpose_batch(srcs, dstw, pRT, tevB)
                                sch.wait("s", sPE, v_sum_done)
                                sch.wait("s", sPE, v_bc_end)
                                v_rstd = sch.chain_inc("s", [
                                    lambda e: e.activation(rstd16[:, :], pS16[:, :],
                                                           AF.Ln, bias=1e-24),
                                    lambda e: e.activation(rstd16[:, :], rstd16[:, :],
                                                           AF.Exp, scale=-0.5)], sSC)
                                v_rstd_prev = v_rstd
                                sch.wait("v", cc["ctr"], v_bias)
                                g_bRB.set(sSC, sSC.val)
                                g_rot.set(sSC, sSC.val)
                                for mt in range(8):
                                    isq = mt < 4
                                    pp = mt % 4
                                    scol = qscol if isq else kscol
                                    tabs = tabq if isq else tabk
                                    dst2 = q2sb if isq else k2sb
                                    for ch in range(2):
                                        sch.wait("t", sSC, v_rstd)
                                        g_bRB.require(sch, "t")
                                        vpb = sch.last_inc("t", [
                                            lambda e, mt=mt, ch=ch: e.matmul(
                                                pRT[0][:, :],
                                                cbsel[:, mt * 128:(mt + 1) * 128],
                                                rstd16[:, ch * 512:(ch + 1) * 512],
                                                start=True, stop=True)], sPE)
                                        v_bc_end = vpb
                                        sch.wait("v", sPE, vpb)
                                        g_qqs[ch].require(sch, "v")
                                        vq = sch.last_inc("v", [
                                            lambda e, mt=mt, ch=ch, scol=scol:
                                            e.scalar_tensor_tensor(
                                                qqs[ch][:, :],
                                                qraw[mt][:, ch * 512:(ch + 1) * 512],
                                                scol[:, 0:1], pRT[0][:, :],
                                                ALU.mult, ALU.mult)], sVE)
                                        g_bRB.set(sVE, vq)
                                        if ch == 1:
                                            g_qraw[mt].set(sVE, vq)
                                        sch.wait("t", sVE, vq)
                                        g_rot.require(sch, "t")
                                        vpr = sch.last_inc("t", [
                                            lambda e, ch=ch: e.matmul(
                                                pRT[1][:, :], cprot[:, :],
                                                qqs[ch][:, :], start=True, stop=True)], sPE)
                                        gcol = c0n + ch * 512
                                        sch.wait("v", sPE, vpr)
                                        sch.wait("v", sTB, v_tabs)
                                        vq2 = sch.chain_inc("v", [
                                            lambda e, ch=ch, tabs=tabs, gcol=gcol:
                                            e.tensor_mul(t1sb[:, :], qqs[ch][:, :],
                                                         tabs[0][:, gcol:gcol + 512]),
                                            lambda e, ch=ch, tabs=tabs, gcol=gcol:
                                            e.tensor_mul(t2sb[:, :], pRT[1][:, :],
                                                         tabs[1][:, gcol:gcol + 512]),
                                            lambda e, ch=ch, pp=pp, gcol=gcol, dst2=dst2:
                                            e.tensor_add(
                                                dst2[:, pp * T + gcol: pp * T + gcol + 512],
                                                t1sb[:, :], t2sb[:, :])], sVE)
                                        g_rot.set(sVE, vq2)
                                        g_qqs[ch].set(sVE, vq2)
                            fns = [lambda e, tt=tt: e.memset(bass.AP(
                                tensor=vtokB, offset=tt * 520 + 64,
                                ap=[[NTT * 520, 128], [65, 8]]), 1.0) for tt in range(NTT)]
                            v_vones = sch.last_inc("v", fns, sVE)
                        v_qk = sVE.val

                        DEN = sb("DENROW", [1, 3 * T], F32, stAt)
                        rdrow = DEN[0:1, 2 * T:3 * T]
                        RDENs = [sb("RDENa", [128, T], F32, stAt),
                                 sb("RDENb", [128, T], F32, stAt)]
                        with ExitStack() as sta:
                            expT = [sb(f"expT{i}", [128, 640], BF16, sta)
                                    for i in range(3)]
                            pSCb = [ps("pSCx", [128, 640], sta), ps("pSCy", [128, 640], sta)]
                            pOT = ps("pOT", [65, T], sta)
                            sc_free = [0, 0]
                            av_done = {}
                            v_evac_head = sSC.val
                            v_ev2_prev = sSC.val
                            v_ev3_box = [sSC.val]
                            v_rd = sVE.val
                            v_rd_box = [sVE.val]
                            dma_after_head = {}
                            pending_fin = [None]
                            pending_ev3 = [None]
                            pending_av = [None]

                            def mk_fin(h, pp):
                                def fin():
                                    sch.wait("v", sVE, v_ev3_box[0])
                                    sch.wait("v", sGD, sGD.val)
                                    v_rd = sch.last_inc("v", [lambda e, h=h: e.reciprocal(
                                        rdrow, DEN[0:1, (h % 2) * T:(h % 2) * T + T])], sVE)
                                    v_rd_box[0] = v_rd

                                    def gdma(e, pp=pp, s2=(h % 2), v_rd=v_rd):
                                        e.wait_ge(sVE.sem, v_rd)
                                        base = 64 * (pp % 2) + s2
                                        return e.dma_start(
                                            out=RDENs[pp // 2][base:base + 1, :],
                                            in_=rdrow).then_inc(sGD.sem, 16)
                                    sch.emit("g", gdma)
                                    sGD.inc()
                                    dma_after_head[h] = sGD.val
                                return fin
                            for h in range(8):
                                pp, sub = h // 2, (h % 2) * 64
                                banks_touched = set()
                                reg_open = set()
                                for kt in range(NTT):
                                    g1 = min(kt + 4, NTT - 1)
                                    vq = (g1 - kt + 1) * 128
                                    q0 = kt * 128
                                    slot = kt % 2
                                    esl = (h * NTT + kt) % 3
                                    sch.wait("t", sVE, v_qk)
                                    sch.wait("t", sVE, v_vones)
                                    sch.wait("t", sSC, sc_free[slot])
                                    fns = []
                                    has_mkp = kt + 4 <= NTT - 1
                                    segs = [(0, min(128, vq), True, False)]
                                    if vq > 128:
                                        segs.append((128, min(512, vq), False, False))
                                    if vq > 512:
                                        segs.append((512, vq, True, not has_mkp))
                                    for o0, o1, st, sp in segs:
                                        fns.append(lambda e, slot=slot, pp=pp, sub=sub,
                                                   kt=kt, o0=o0, o1=o1, q0=q0, st=st, sp=sp:
                                                   e.matmul(
                                                       pSCb[slot][:, o0:o1],
                                                       k2sb[sub:sub + 64,
                                                            pp * T + kt * 128:
                                                            pp * T + (kt + 1) * 128],
                                                       q2sb[sub:sub + 64,
                                                            pp * T + q0 + o0:
                                                            pp * T + q0 + o1],
                                                       start=st, stop=sp))
                                    fns.append(lambda e, slot=slot: e.matmul(
                                        pSCb[slot][:, 0:128], cidbf[:, :], cmko[:, :],
                                        start=False, stop=True))
                                    if kt + 4 <= NTT - 1:
                                        fns.append(lambda e, slot=slot: e.matmul(
                                            pSCb[slot][:, 512:640], cidbf[:, :], cmkp[:, :],
                                            start=False, stop=True))
                                    vp = sch.last_inc("t", fns, sPE)
                                    if pending_av[0] is not None:
                                        pending_av[0]()
                                        pending_av[0] = None
                                    sch.wait("s", sPE, vp)
                                    prev = (h, kt - 3) if kt >= 3 else (h - 1, NTT - 3 + kt)
                                    if prev in av_done:
                                        sch.wait("s", sPE, av_done[prev])
                                    sch.wait("s", sSC, v_m0)
                                    vs = sch.last_inc("s", [
                                        lambda e, slot=slot, esl=esl, vq=vq:
                                        e.activation(expT[esl][:, 0:vq],
                                                     pSCb[slot][:, 0:vq], AF.Exp,
                                                     bias=negm0[:, 0:1])], sSC)
                                    sc_free[slot] = vs

                                    def do_av(kt=kt, h=h, pp=pp, sub=sub, esl=esl, vs=vs,
                                              reg_st=None):
                                        nonlocal v_ev2_prev
                                        sch.wait("t", sSC, vs)
                                        if h > 0 and kt == 0:
                                            sch.wait("t", sVE, v_ev2_prev)
                                        if h > 0 and kt == 8:
                                            sch.wait("t", sVE, v_ev3_box[0])
                                        fns = []
                                        for g, st_flag, sp_flag, off in reg_st:
                                            fns.append(lambda e, kt=kt, h=h, g=g, off=off,
                                                       esl=esl, st_flag=st_flag,
                                                       sp_flag=sp_flag: e.matmul(
                                                           pOT[:, g * 128:(g + 1) * 128],
                                                           vtokB[:, kt * 520 + h * 65:
                                                                 kt * 520 + h * 65 + 65],
                                                           expT[esl][:, off:off + 128],
                                                           start=st_flag, stop=sp_flag))
                                        av_done[(h, kt)] = sch.last_inc("t", fns, sPE)
                                        if kt in (3, 7, 11):
                                            r = kt // 4
                                            sch.wait("v", sPE, av_done[(h, kt)])
                                            vev = sch.last_inc("v", [
                                                lambda e, pp=pp, sub=sub, r=r: e.tensor_copy(
                                                    oTsb[sub:sub + 64,
                                                         pp * T + r * 512:pp * T + (r + 1) * 512],
                                                    pOT[0:64, r * 512:(r + 1) * 512]),
                                                lambda e, r=r, h=h: e.tensor_copy(
                                                    DEN[0:1, (h % 2) * T + r * 512:
                                                        (h % 2) * T + (r + 1) * 512],
                                                    pOT[64:65, r * 512:(r + 1) * 512])], sVE)
                                            if kt == 11:
                                                v_ev2_prev = vev
                                    reg_st = []
                                    for g in range(kt, min(kt + 4, NTT - 1) + 1):
                                        r = g // 4
                                        st_flag = r not in reg_open
                                        reg_open.add(r)
                                        sp_flag = (kt == min(4 * r + 3, NTT - 1)
                                                   and g == kt)
                                        if sp_flag:
                                            reg_open.discard(r)
                                        reg_st.append((g, st_flag, sp_flag, (g - kt) * 128))
                                    pending_av[0] = (lambda d=do_av, rs=reg_st:
                                                     d(reg_st=rs))
                                    if kt == 4 and pending_ev3[0] is not None:
                                        pending_ev3[0]()
                                        pending_ev3[0] = None
                                    if kt == 6 and pending_fin[0] is not None:
                                        pending_fin[0]()
                                        pending_fin[0] = None
                                pending_av[0]()
                                pending_av[0] = None

                                def mkev3(pp, sub, h=None):
                                    def emitev3():
                                        sch.wait("v", sPE, av_done[(h, NTT - 1)])
                                        v_ev3_box[0] = sch.last_inc("v", [
                                            lambda e, pp=pp, sub=sub: e.tensor_copy(
                                                oTsb[sub:sub + 64, pp * T + 1536:(pp + 1) * T],
                                                pOT[0:64, 1536:2048]),
                                            lambda e, h=h: e.tensor_copy(
                                                DEN[0:1, (h % 2) * T + 1536:
                                                    (h % 2) * T + 2048],
                                                pOT[64:65, 1536:2048])], sVE)
                                    return emitev3
                                pending_ev3[0] = mkev3(pp, sub, h)
                                pending_fin[0] = mk_fin(h, pp)
                            pending_ev3[0]()
                            pending_ev3[0] = None
                            pending_fin[0]()
                            pending_fin[0] = None
                            v_alldma = sGD.val
                        with ExitStack() as std:
                            pDV = [ps("pDVa", [128, 512], std),
                                   ps("pDVb", [128, 512], std)]
                            ev3 = [0, 0]
                            i = 0
                            for pp in range(4):
                                for ch in range(4):
                                    buf = i % 2
                                    sch.wait("t", sGD, dma_after_head[2 * pp + 1])
                                    if i >= 2:
                                        sch.wait("t", sVE, ev3[buf])
                                    vp = sch.last_inc("t", [
                                        lambda e, pp=pp, ch=ch, buf=buf: e.matmul(
                                            pDV[buf][:, :],
                                            cbd2f[64 * (pp % 2):64 * (pp % 2) + 2, :],
                                            RDENs[pp // 2][64 * (pp % 2):64 * (pp % 2) + 2,
                                                           ch * 512:(ch + 1) * 512],
                                            start=True, stop=True)], sPE)
                                    sch.wait("v", sPE, vp)
                                    ev3[buf] = sch.last_inc("v", [
                                        lambda e, pp=pp, ch=ch, buf=buf: e.tensor_mul(
                                            oTsb[:, pp * T + ch * 512:
                                                 pp * T + (ch + 1) * 512],
                                            oTsb[:, pp * T + ch * 512:
                                                 pp * T + (ch + 1) * 512],
                                            pDV[buf][:, :])], sVE)
                                    i += 1
                            v_oT = sVE.val

                    with ExitStack() as stF:
                        _ = None
                        hbufF2 = [[sb(f"hbufF{n}_{i}", [128, 1024], BF16, stF)
                                   for i in range(NPT)] for n in range(2)]

                        RW = sb("lnrowsB", [1, 3 * 1024], F32, stF)

                        h3sb = sb("h3sb", [128, NJ * 1024], BF16, stF)
                        NJH = NJ // 2
                        wr2 = [sb(f"wr2_{i}", [128, NJH * 128], BF16, stF)
                               for i in range(2)]
                        lntmp2 = [sb("lntmpB0", [128, 512], BF16, stF)]
                        lntmp2.append(lntmp2[0])
                        xedge = sb("xedge", [128, 8], F32, stF)
                        dma_phase_gate()

                        def shift_castF(nb):
                            hb = hbufF2[nb]
                            c0n = nb * 1024
                            sch.wait("v", sVE, sVE.val)
                            fns = []
                            for pt in range(NPT):
                                if pt < 4:
                                    fns.append(lambda e, hb=hb, pt=pt, c0n=c0n: e.tensor_copy(
                                        hb[pt][:, :], XT[pt][:, c0n:c0n + 1024]))
                                elif nb == 0:
                                    fns.append(lambda e, hb=hb, pt=pt: e.memset(
                                        hb[pt][:, 0:1], 0.0))
                                    fns.append(lambda e, hb=hb, pt=pt: e.tensor_copy(
                                        hb[pt][:, 1:1024], XT[pt][:, 0:1023]))
                                else:
                                    fns.append(lambda e, hb=hb, pt=pt: e.tensor_copy(
                                        hb[pt][:, 0:1], xedge[:, pt:pt + 1]))
                                    fns.append(lambda e, hb=hb, pt=pt, c0n=c0n: e.tensor_copy(
                                        hb[pt][:, 1:1024], XT[pt][:, c0n:c0n + 1023]))
                            return sch.chain_inc("v", fns, sVE)

                        W1 = 1024
                        b_sy = RW[0:1, 0:W1]
                        b_s1 = RW[0:1, W1:2 * W1]
                        b_s2 = RW[0:1, 2 * W1:3 * W1]
                        def ln2_pieces(nb):
                            hbufF = hbufF2[nb]
                            c0n = nb * 1024
                            st = {"es": ExitStack()}
                            box = {}

                            def p_start():
                                st["pSum"] = ps("pSumB", [1, 1024], st["es"])
                                st["pSq"] = ps("pSqB", [1, 1024], st["es"])
                                sch.wait("v", sPE, sPE.val)
                                shift_castF(nb)

                            sqF = hbufF2[1 - nb][0]

                            def p_pt(pt):
                                pSum, pSq2 = st["pSum"], st["pSq"]
                                vc = sch.last_inc("v", [lambda e, pt=pt: e.tensor_mul(
                                    sqF[:, :], hbufF[pt][:, :], hbufF[pt][:, :])], sVE)
                                sch.wait("t", sVE, vc)
                                fns = []
                                for ch in range(2):
                                    fns.append(lambda e, ch=ch, pt=pt: e.matmul(
                                        pSum[0:1, ch * 512:(ch + 1) * 512], conesc[:, :],
                                        hbufF[pt][:, ch * 512:(ch + 1) * 512],
                                        start=(pt == 0), stop=(pt == NPT - 1)))
                                    fns.append(lambda e, ch=ch, pt=pt: e.matmul(
                                        pSq2[0:1, ch * 512:(ch + 1) * 512], conesc[:, :],
                                        sqF[:, ch * 512:(ch + 1) * 512],
                                        start=(pt == 0), stop=(pt == NPT - 1)))
                                vp = sch.last_inc("t", fns, sPE)
                                sch.wait("v", sPE, vp)

                            def p_rows():
                                pSum, pSq2 = st["pSum"], st["pSq"]
                                sch.wait("s", sPE, sPE.val)
                                vs = sch.last_inc("s", [
                                    lambda e: e.copy(b_sy, pSum[0:1, :]),
                                    lambda e: e.copy(b_s1, pSq2[0:1, :])], sSC)
                                st["es"].close()
                                sch.wait("v", sSC, vs)
                                sch.chain_inc("v", [
                                    lambda e: e.tensor_scalar_mul(b_sy, b_sy, 1.0 / D),
                                    lambda e: e.tensor_scalar_mul(b_s1, b_s1, 1.0 / D),
                                    lambda e: e.tensor_mul(b_s2, b_sy, b_sy),
                                    lambda e: e.tensor_sub(b_s1, b_s1, b_s2)], sVE)
                                sch.wait("s", sVE, sVE.val)
                                vs = sch.chain_inc("s", [
                                    lambda e: e.activation(b_s1, b_s1, AF.Ln, bias=1e-5),
                                    lambda e: e.activation(b_s1, b_s1, AF.Exp, scale=-0.5)], sSC)
                                sch.wait("v", sSC, vs)
                                st["vv"] = sch.last_inc("v", [
                                    lambda e: e.scalar_tensor_tensor(
                                        b_s2, b_sy, -1.0, b_s1, ALU.mult, ALU.mult)], sVE)

                            def p_apply(ch):
                                if ch == 0:
                                    st["es2"] = ExitStack()
                                    st["pBA"] = [ps("pBAx", [128, 512], st["es2"]),
                                                 ps("pBAy", [128, 512], st["es2"])]
                                    st["pBB"] = [ps("pBBx", [128, 512], st["es2"]),
                                                 ps("pBBy", [128, 512], st["es2"])]
                                pBA, pBB = st["pBA"], st["pBB"]
                                buf = ch % 2
                                sch.wait("t", sVE, st["vv"])
                                fns = [lambda e, buf=buf, ch=ch: e.matmul(
                                    pBA[buf][:, :], conesrf[:, :],
                                    RW[0:1, W1 + ch * 512: W1 + (ch + 1) * 512],
                                    start=True, stop=True),
                                    lambda e, buf=buf, ch=ch: e.matmul(
                                    pBB[buf][:, :], conesrf[:, :],
                                    RW[0:1, 2 * W1 + ch * 512: 2 * W1 + (ch + 1) * 512],
                                    start=True, stop=True)]
                                vp = sch.last_inc("t", fns, sPE)
                                sch.wait("v", sPE, vp)
                                for pt in range(NPT):
                                    cg = c0n + ch * 512
                                    if pt < 4:
                                        fns = [lambda e, pt=pt, cg=cg, buf=buf:
                                               e.tensor_mul(lntmp2[0][:, :],
                                                            XT[pt][:, cg:cg + 512],
                                                            pBA[buf][:, :]),
                                               lambda e, buf=buf: e.tensor_add(
                                            lntmp2[1][:, :], lntmp2[0][:, :],
                                            pBB[buf][:, :])]
                                    elif cg == 1024:
                                        fns = [lambda e, pt=pt, buf=buf:
                                               e.tensor_mul(lntmp2[0][:, 1:512],
                                                            XT[pt][:, 1024:1535],
                                                            pBA[buf][:, 1:512]),
                                               lambda e, pt=pt, buf=buf:
                                               e.tensor_mul(lntmp2[0][:, 0:1],
                                                            xedge[:, pt:pt + 1],
                                                            pBA[buf][:, 0:1]),
                                               lambda e, buf=buf: e.tensor_add(
                                            lntmp2[1][:, :], lntmp2[0][:, :],
                                            pBB[buf][:, :])]
                                    elif cg > 0:
                                        fns = [lambda e, pt=pt, cg=cg, buf=buf:
                                               e.tensor_mul(lntmp2[0][:, :],
                                                            XT[pt][:, cg - 1:cg + 511],
                                                            pBA[buf][:, :]),
                                               lambda e, buf=buf: e.tensor_add(
                                            lntmp2[1][:, :], lntmp2[0][:, :],
                                            pBB[buf][:, :])]
                                    else:
                                        fns = [lambda e, pt=pt, buf=buf:
                                               e.tensor_mul(lntmp2[0][:, 1:512],
                                                            XT[pt][:, 0:511],
                                                            pBA[buf][:, 1:512]),
                                               lambda e: e.memset(
                                            lntmp2[0][:, 0:1], 0.0),
                                               lambda e, buf=buf: e.tensor_add(
                                            lntmp2[1][:, :], lntmp2[0][:, :],
                                            pBB[buf][:, :])]
                                    fns.append(lambda e, pt=pt, ch=ch: e.tensor_scalar(
                                        hbufF[pt][:, ch * 512:(ch + 1) * 512],
                                        lntmp2[1][:, :], lngc[:, pt:pt + 1],
                                        lnbc[:, pt:pt + 1], ALU.mult, ALU.add))
                                    sch.chain_inc("v", fns, sVE)
                                box["v_hp_c%d" % ch] = sVE.val
                                if ch == 1:
                                    box["v_hp"] = sVE.val
                                    st["es2"].close()

                            pieces = [p_start]
                            pieces += [(lambda pt: (lambda: p_pt(pt)))(pt)
                                       for pt in range(NPT)]
                            pieces += [p_rows, lambda: p_apply(0), lambda: p_apply(1)]
                            return pieces, box

                        last_h3 = [0]

                        def emit_w1(nb, v_hp_c0, v_hp):
                            hbufF = hbufF2[nb]
                            e2sb = hbufF2[1 - nb][1]
                            with ExitStack() as stw1:
                                pW1 = [ps("pW1a", [128, 1024], stw1),
                                       ps("pW1b", [128, 1024], stw1),
                                       ps("pW1c", [128, 1024], stw1),
                                       ps("pW1d", [128, 1024], stw1)]
                                pair_free = [Gate(), Gate()]
                                for j in range(NJ):
                                    prslot = j % 2
                                    pa, pg = pW1[2 * prslot], pW1[2 * prslot + 1]
                                    pair_free[prslot].require(sch, "t")
                                    vps = []
                                    for ag in range(2):
                                        wslot = (2 * j + ag) % 4
                                        wc = wload(WN + "w1", 2 * j + ag, wrA[wslot],
                                                   wrA_gate[wslot], sGW[wslot])
                                        sch.wait("t", wc, wc.inc())
                                        if j == 0 and ag == 0:
                                            sch.wait("t", sVE, v_hp_c0)
                                        pdst = pa if ag == 0 else pg
                                        for h2 in range(2):
                                            fns = []
                                            for kt in range(NPT):
                                                fns.append(lambda e, pdst=pdst, wslot=wslot,
                                                           hbufF=hbufF,
                                                           kt=kt, h2=h2: e.matmul(
                                                               pdst[:, h2 * 512:(h2 + 1) * 512],
                                                               wrA[wslot][:, kt * 128:(kt + 1) * 128],
                                                               hbufF[kt][:, h2 * 512:(h2 + 1) * 512],
                                                               start=(kt == 0),
                                                               stop=(kt == NPT - 1)))
                                            if j == 0 and ag == 0 and h2 == 1:
                                                sch.wait("t", sVE, v_hp)
                                            vp = sch.last_inc("t", fns, sPE)
                                        wrA_gate[wslot].set(sPE, vp)
                                        vps.append(vp)
                                    sch.wait("s", sPE, vps[1])
                                    sch.wait("s", sVE, last_h3[0])
                                    vs = sch.last_inc("s", [
                                        lambda e, pg=pg, j=j: e.activation(
                                            e2sb[:, :], pg[:, :], AF.Gelu,
                                            bias=b1gc[:, j:j + 1])], sSC)
                                    sch.wait("v", sSC, vs)
                                    vv4 = sch.last_inc("v", [
                                        lambda e, pa=pa, j=j: e.scalar_tensor_tensor(
                                            h3sb[:, j * 1024:(j + 1) * 1024],
                                            pa[:, :], b1ac[:, j:j + 1],
                                            e2sb[:, :], ALU.add, ALU.mult)], sVE)
                                    pair_free[prslot].set(sVE, vv4)
                                    last_h3[0] = vv4
                            return sVE.val

                        def wload_half(sub, hf, dstbuf, gate, ctr):
                            src2 = wdr[WN + "w2"][sub][hf * NJH * 128:(hf + 1) * NJH * 128]
                            src2 = src2.rearrange("(a p) m -> p a m", p=128)
                            dst = dstbuf[:, :].rearrange("p (a m) -> p a m", m=128)
                            if gate is not None:
                                gate.require(sch, "g")
                            sch.emit("g", lambda e, dst=dst, src2=src2, ctr=ctr:
                                     e.dma_start(out=dst, in_=src2).then_inc(ctr.sem, 16))
                            return ctr

                        def emit_w2(nb, v_h3, inserts=None):
                            c0n = nb * 1024
                            with ExitStack() as stw2:
                                pW2 = [ps("pW2a", [128, 1024], stw2),
                                       ps("pW2b", [128, 1024], stw2)]
                                slot_free2 = [Gate(), Gate()]
                                for mt in range(NPT):
                                    slot = mt % 2
                                    pslot = mt % 2
                                    vp = 0
                                    sch.wait("t", sVE, v_h3)
                                    slot_free2[pslot].require(sch, "t")
                                    for hf in range(2):
                                        wsl = (2 * mt + hf) % 2
                                        wc = wload_half(mt, hf, wr2[wsl],
                                                        wr2_gate[wsl], sG2[wsl])
                                        sch.wait("t", wc, wc.inc())
                                        fns = []
                                        for kj in range(NJH):
                                            kt = hf * NJH + kj
                                            for h2 in range(2):
                                                fns.append(lambda e, pslot=pslot, wsl=wsl,
                                                           kj=kj, kt=kt, h2=h2:
                                                           e.matmul(
                                                               pW2[pslot][:, h2 * 512:(h2 + 1) * 512],
                                                               wr2[wsl][:, kj * 128:(kj + 1) * 128],
                                                               h3sb[:, kt * 1024 + h2 * 512:
                                                                    kt * 1024 + (h2 + 1) * 512],
                                                               start=(kt == 0), stop=(kt == NJ - 1)))
                                        vp = sch.last_inc("t", fns, sPE)
                                        wr2_gate[wsl].set(sPE, vp)
                                    sch.wait("v", sPE, vp)
                                    vv5 = sch.last_inc("v", [
                                        lambda e, pslot=pslot, mt=mt, c0n=c0n:
                                        e.scalar_tensor_tensor(
                                            XT[mt][:, c0n:c0n + 1024], pW2[pslot][:, :],
                                            b2c[:, mt:mt + 1], XT[mt][:, c0n:c0n + 1024],
                                            ALU.add, ALU.add)], sVE)
                                    slot_free2[pslot].set(sVE, vv5)
                                    for f in (inserts or {}).get(mt, []):
                                        f()

                        def emit_wo(inserts=None):
                            with ExitStack() as stw:
                                pW = [ps("pWc", [128, 1024], stw),
                                      ps("pWd", [128, 1024], stw)]
                                slot_free = [Gate(), Gate()]
                                i = 0
                                for nb in range(2):
                                    for mt in range(NPT):
                                        wslot = i % 4
                                        slot = i % 2
                                        wc = wload(WN + "wo", mt, wrA[wslot],
                                                   wrA_gate[wslot], sGW[wslot], cols=512)
                                        sch.wait("t", wc, wc.inc())
                                        sch.wait("t", sVE, v_oT)
                                        slot_free[slot].require(sch, "t")
                                        fns = []
                                        for kt in range(4):
                                            for h2 in range(2):
                                                fns.append(lambda e, slot=slot, wslot=wslot,
                                                           kt=kt, h2=h2, nb=nb: e.matmul(
                                                               pW[slot][:, h2 * 512:(h2 + 1) * 512],
                                                               wrA[wslot][:, kt * 128:(kt + 1) * 128],
                                                               oTsb[:, kt * T + nb * 1024 + h2 * 512:
                                                                    kt * T + nb * 1024 + (h2 + 1) * 512],
                                                               start=(kt == 0), stop=(kt == 3)))
                                        vp = sch.last_inc("t", fns, sPE)
                                        wrA_gate[wslot].set(sPE, vp)
                                        sch.wait("v", sPE, vp)
                                        if l == 0:
                                            sch.wait("v", sLD, v_spill)
                                        vv = sch.last_inc("v", [
                                            lambda e, slot=slot, mt=mt, nb=nb: e.tensor_add(
                                                XT[mt][:, nb * 1024:(nb + 1) * 1024],
                                                XT[mt][:, nb * 1024:(nb + 1) * 1024],
                                                pW[slot][:, :])], sVE)
                                        slot_free[slot].set(sVE, vv)
                                        for f in (inserts or {}).get((nb, mt), []):
                                            f()
                                        i += 1

                        pieces0, box0 = ln2_pieces(0)
                        ins0 = {(1, 0): [pieces0[0]],
                                (1, 1): [pieces0[1], pieces0[2]],
                                (1, 2): [pieces0[3], pieces0[4]],
                                (1, 3): [pieces0[5], pieces0[6]],
                                (1, 4): [pieces0[7], pieces0[8]],
                                (1, 5): [pieces0[9]],
                                (1, 6): [pieces0[10]],
                                (1, 7): [pieces0[11]]}
                        emit_wo(inserts=ins0)
                        sch.chain_inc("v", [
                            (lambda pt2: (lambda e: e.tensor_copy(
                                xedge[:, pt2:pt2 + 1], XT[pt2][:, 1023:1024])))(pt2)
                            for pt2 in range(NPT)], sVE)
                        v_h3_0 = emit_w1(0, box0["v_hp_c0"], box0["v_hp"])
                        pieces1, box1 = ln2_pieces(1)
                        ins = {0: [pieces1[0]],
                               1: [pieces1[1], pieces1[2]],
                               2: [pieces1[3], pieces1[4]],
                               3: [pieces1[5], pieces1[6]],
                               4: [pieces1[7], pieces1[8]],
                               5: [pieces1[9]],
                               6: [pieces1[10]],
                               7: [pieces1[11]]}
                        emit_w2(0, v_h3_0, inserts=ins)
                        v_h3_1 = emit_w1(1, box1["v_hp_c0"], box1["v_hp"])
                        emit_w2(1, v_h3_1)

            for l in range(NLAY):
                emit_layerB(l)
            stPB.close()

        if STAGE >= 3:
            with ExitStack() as stf:
                x1r = [sb(f"x1r{i}", [128, T], F32, stf) for i in range(NPT)]
                R = sb("lnrows", [1, 5 * T], F32, stf)
                lnbf = sb("lnbf", [128, T], BF16, stf)
                lnsq = sb("lnsq", [128, T], BF16, stf)
                otok = sb("otok", [128, 4 * D], F32, stf)
                r_sy = R[0:1, 0:T]
                r_sq = R[0:1, T:2 * T]
                r_t = R[0:1, 2 * T:3 * T]
                r_A = R[0:1, 3 * T:4 * T]
                r_B = R[0:1, 4 * T:5 * T]
                dma_phase_gate()
                sX1 = [sem(f"sX1{i}", 16) for i in range(NPT)]
                for pt in range(NPT):
                    sch.emit("y", lambda e, pt=pt: e.dma_start(
                        out=x1r[pt][:, :], in_=x1s[:, pt * T:(pt + 1) * T])
                        .then_inc(sX1[pt].sem, 16))
                    sX1[pt].inc()
                sch.wait("v", sPE, sPE.val)
                with ExitStack() as stp:
                    pSum = ps("pSum", [1, T], stp)
                    pSq = ps("pSq", [1, T], stp)
                    for pt in range(NPT):
                        sch.wait("v", sX1[pt], 16)
                        fns = [lambda e, pt=pt: e.tensor_add(
                                   XT[pt][:, :], XT[pt][:, :], x1r[pt][:, :]),
                               lambda e, pt=pt: e.tensor_copy(lnbf[:, :], XT[pt][:, :]),
                               lambda e: e.tensor_mul(lnsq[:, :], lnbf[:, :], lnbf[:, :])]
                        vc = sch.chain_inc("v", fns, sVE)
                        sch.wait("t", sVE, vc)
                        fns = []
                        for ch in range(4):
                            fns.append(lambda e, ch=ch, pt=pt: e.matmul(
                                pSum[0:1, ch * 512:(ch + 1) * 512], conesc[:, :],
                                lnbf[:, ch * 512:(ch + 1) * 512],
                                start=(pt == 0), stop=(pt == NPT - 1)))
                            fns.append(lambda e, ch=ch, pt=pt: e.matmul(
                                pSq[0:1, ch * 512:(ch + 1) * 512], conesc[:, :],
                                lnsq[:, ch * 512:(ch + 1) * 512],
                                start=(pt == 0), stop=(pt == NPT - 1)))
                        vp = sch.last_inc("t", fns, sPE)
                        sch.wait("v", sPE, vp)
                    sch.wait("s", sPE, sPE.val)
                    vs = sch.last_inc("s", [
                        lambda e: e.copy(r_sy, pSum[0:1, :]),
                        lambda e: e.copy(r_sq, pSq[0:1, :])], sSC)
                sch.wait("v", sSC, vs)
                vv = sch.chain_inc("v", [
                    lambda e: e.tensor_scalar_mul(r_sy, r_sy, 1.0 / D),
                    lambda e: e.tensor_scalar_mul(r_sq, r_sq, 1.0 / D),
                    lambda e: e.tensor_mul(r_t, r_sy, r_sy),
                    lambda e: e.tensor_sub(r_t, r_sq, r_t)], sVE)
                sch.wait("s", sVE, vv)
                vs = sch.chain_inc("s", [
                    lambda e: e.activation(r_t, r_t, AF.Ln, bias=1e-5),
                    lambda e: e.activation(r_A, r_t, AF.Exp, scale=-0.5)], sSC)
                sch.wait("v", sSC, vs)
                vv = sch.last_inc("v", [lambda e: e.scalar_tensor_tensor(
                    r_B, r_sy, -1.0, r_A, ALU.mult, ALU.mult)], sVE)
                v_ch = [0, 0, 0, 0]
                with ExitStack() as stp:
                    pBA = [ps("pBA0", [128, 512], stp), ps("pBA1", [128, 512], stp)]
                    pBB = [ps("pBB0", [128, 512], stp), ps("pBB1", [128, 512], stp)]
                    tmpf = sb("lntmp", [128, 512], F32, stp)
                    ev2 = [0, 0]
                    pO = [ps("pO0", [128, 512], stp), ps("pO1", [128, 512], stp)]
                    ev = [0, 0]
                    idx = 0
                    sSTs = [sem(f"sSTs{i}", 16) for i in range(4)]
                    yr = y_out.rearrange("(tt p) d -> tt p d", p=128)
                    for ch in range(4):
                        buf = ch % 2
                        sch.wait("t", sVE, vv)
                        if ch >= 2:
                            sch.wait("t", sVE, ev2[buf])
                        fns = [lambda e, buf=buf, ch=ch: e.matmul(
                            pBA[buf][:, :], conesrf[:, :],
                            R[0:1, 3 * T + ch * 512: 3 * T + (ch + 1) * 512],
                            start=True, stop=True),
                            lambda e, buf=buf, ch=ch: e.matmul(
                            pBB[buf][:, :], conesrf[:, :],
                            R[0:1, 4 * T + ch * 512: 4 * T + (ch + 1) * 512],
                            start=True, stop=True)]
                        vp = sch.last_inc("t", fns, sPE)
                        sch.wait("v", sPE, vp)
                        fns = []
                        for pt in range(NPT):
                            fns.append(lambda e, pt=pt, ch=ch, buf=buf: e.tensor_mul(
                                tmpf[:, :], XT[pt][:, ch * 512:(ch + 1) * 512], pBA[buf][:, :]))
                            fns.append(lambda e, pt=pt, ch=ch, buf=buf: e.tensor_add(
                                XT[pt][:, ch * 512:(ch + 1) * 512], tmpf[:, :], pBB[buf][:, :]))
                        ev2[buf] = sch.chain_inc("v", fns, sVE)
                        v_ch[ch] = ev2[buf]
                        for tt in range(ch * 4, (ch + 1) * 4):
                            ts = tt % 4
                            for dg in range(2):
                                tbuf = idx % 2
                                sch.wait("t", sVE, v_ch[ch])
                                if idx >= 2:
                                    sch.wait("t", sVE, ev[tbuf])
                                fns = []
                                for j in range(4):
                                    dt_i = dg * 4 + j
                                    fns.append(lambda e, tbuf=tbuf, j=j, tt=tt, dt_i=dt_i:
                                               e.matmul(
                                                   pO[tbuf][:, j * 128:(j + 1) * 128],
                                                   XT[dt_i][:, tt * 128:(tt + 1) * 128],
                                                   cid32[:, :], start=True, stop=True))
                                vp = sch.last_inc("t", fns, sPE)
                                sch.wait("v", sPE, vp)
                                if tt >= 4 and dg == 0:
                                    sch.wait("v", sSTs[ts], sSTs[ts].val)
                                fns = [lambda e, tbuf=tbuf, j=j, ts=ts, dt_i=dg * 4 + j:
                                       e.tensor_copy(
                                           otok[:, ts * D + dt_i * 128:
                                                ts * D + dt_i * 128 + 128],
                                           pO[tbuf][:, j * 128:(j + 1) * 128])
                                       for j in range(4)]
                                ev[tbuf] = sch.last_inc("v", fns, sVE)
                                idx += 1
                            vv6 = sVE.val

                            def stf_(e, tt=tt, ts=ts, vv6=vv6):
                                e.wait_ge(sVE.sem, vv6)
                                return e.dma_start(
                                    out=yr[tt],
                                    in_=otok[:, ts * D:(ts + 1) * D]).then_inc(sSTs[ts].sem, 16)
                            sch.emit("y", stf_)
                            sSTs[ts].inc()
                    for i in range(4):
                        sch.emit("y", lambda e, s=sSTs[i].sem, v=sSTs[i].val:
                                 e.wait_ge(s, v))

        with nc.Block() as blk:
            @blk.tensor
            def _(t):
                for fn in sch.ops["t"]:
                    fn(t)

            @blk.vector
            def _(v):
                for fn in sch.ops["v"]:
                    fn(v)

            @blk.scalar
            def _(s):
                for fn in sch.ops["s"]:
                    fn(s)

            @blk.gpsimd
            def _(g):
                for fn in sch.ops["g"]:
                    fn(g)

            @blk.sync
            def _(y):
                for fn in sch.ops["y"]:
                    fn(y)

    return nc


def kernel(**inputs):
    x = np.ascontiguousarray(inputs["L0_attn_x"]).astype(np.float32, copy=False)
    w = _prep_weights(inputs)
    wspecs = {nm: arr.shape for nm, arr in w.items()}
    nc = build_nc(wspecs)
    in_maps = []
    for c in range(8):
        im = {"x_in": np.ascontiguousarray(x[c])}
        for nm, arr in _CONSTS.items():
            im[nm] = arr
        for nm, arr in w.items():
            im[nm] = arr
        in_maps.append(im)
    res = run_bass_kernel_spmd(nc, in_maps, core_ids=list(range(8)))
    return np.stack([res.results[c]["y_out"] for c in range(8)], axis=0)


# version 14 (final)
# speedup vs baseline: 1.1918x; 1.0295x over previous
import os
import numpy as np
import ml_dtypes
from contextlib import ExitStack

import concourse.bass as bass
import concourse.mybir as mybir
from concourse.bass_utils import run_bass_kernel_spmd

F32 = mybir.dt.float32
BF16 = mybir.dt.bfloat16
AF = mybir.ActivationFunctionType
ALU = mybir.AluOpType
AX = mybir.AxisListType
BF = ml_dtypes.bfloat16

T = 2048
D = 1024
NPT = D // 128
NTT = T // 128
HID = 2730
NJ = 22
SC = 512
NSC = T // SC
CPS = SC // 128
STAGE = int(os.environ.get("K_STAGE", "3"))
PH = int(os.environ.get("K_PH", "99"))
NCHL = int(os.environ.get("K_NCHL", "99"))
SOP = int(os.environ.get("K_SOP", "63"))
NLAY = {1: 0, 2: 1, 3: 2}[STAGE]


def _host_consts():
    c = {}
    c["c_idf32"] = np.eye(128, dtype=np.float32)
    c["c_idbf"] = np.eye(128, dtype=BF)
    tri = (np.arange(128)[:, None] <= np.arange(128)[None, :]).astype(np.float32)
    c["c_maskA"] = np.tile(tri, (1, 8)).astype(BF)
    big = np.float32(-1e30)
    c["c_mko"] = np.where(np.arange(128)[:, None] <= np.arange(128)[None, :], 0.0, big).astype(BF)
    c["c_mkp"] = np.where(np.arange(128)[:, None] >= np.arange(128)[None, :], 0.0, big).astype(BF)
    c["c_ones_col_bf"] = np.ones((128, 1), dtype=BF)
    c["c_ones_row_f32"] = np.ones((1, 128), dtype=np.float32)
    bd2 = np.zeros((2, 128), dtype=np.float32)
    bd2[0, 0:64] = 1.0
    bd2[1, 64:128] = 1.0
    bd2r = np.zeros((128, 128), dtype=np.float32)
    bd2r[0:2] = bd2
    bd2r[64:66] = bd2
    c["c_bd2_f32"] = bd2r
    c["c_bd2T_bf"] = np.ascontiguousarray(bd2.T).astype(BF)
    P1 = np.zeros((64, 64), dtype=np.float32)
    for dp in range(32):
        P1[dp + 32, dp] = -1.0
    for dp in range(32, 64):
        P1[dp - 32, dp] = 1.0
    Pr = np.zeros((128, 128), dtype=np.float32)
    Pr[0:64, 0:64] = P1
    Pr[64:128, 64:128] = P1
    c["c_prot"] = Pr.astype(BF)
    c["c_eps"] = np.tile(np.array([[1e-5, 1e-24]], dtype=np.float32), (128, 1))
    bdw = np.zeros((128, 32), dtype=np.float32)
    bdw[0:64, 15] = 1.0
    bdw[64:128, 16] = 1.0
    c["c_bdw"] = bdw.astype(BF)
    bsel = np.zeros((16, 8 * 128), dtype=np.float32)
    for mtq in range(8):
        bsel[2 * mtq, mtq * 128:mtq * 128 + 64] = 1.0
        bsel[2 * mtq + 1, mtq * 128 + 64:mtq * 128 + 128] = 1.0
    c["c_bsel"] = bsel.astype(BF)
    DIM_HEAD, WINDOW = 64, 512
    half = np.arange(0, DIM_HEAD, 2, dtype=np.float64)
    inv_freq = 1.0 / 10000.0 ** (half / DIM_HEAD)
    t = np.arange(T, dtype=np.float64)
    freqs = np.concatenate([t[:, None] * inv_freq[None, :]] * 2, axis=-1)
    base = (half + 0.4 * DIM_HEAD) / (1.4 * DIM_HEAD)
    power = (t - T // 2) / float(WINDOW // 2)
    scale = np.concatenate([base[None, :] ** power[:, None]] * 2, axis=-1)
    cos, sin = np.cos(freqs), np.sin(freqs)
    QKS = 8.0
    for nm, tab in (("c_csq", QKS * (cos * scale).T), ("c_ssq", QKS * (sin * scale).T),
                    ("c_csk", (cos / scale).T), ("c_ssk", (sin / scale).T)):
        c[nm] = np.concatenate([tab, tab], axis=0).astype(np.float32).astype(BF)
    return c


_CONSTS = _host_consts()
DBGNAMES = {}


def _prep_weights(inp):
    w = {}
    asc = np.ascontiguousarray

    def tile_cols(M):
        K, Mf = M.shape
        nmt = -(-Mf // 128)
        P = np.zeros((nmt, K, 128), dtype=np.float32)
        for i in range(nmt):
            blk = M[:, i * 128:(i + 1) * 128]
            P[i, :, :blk.shape[1]] = blk
        return asc(P)

    def col_pt(v):
        return asc(v.reshape(-1, 128).T.astype(np.float32))

    for nm in ("Wq", "Wk", "Wv", "Wo"):
        w["A_" + nm] = tile_cols(asc(inp[f"L0_attn_{nm}"].astype(np.float32).T))
    for nm in ("bq", "bk", "bv", "bo"):
        w["A_" + nm] = col_pt(inp[f"L0_attn_{nm}"].astype(np.float32))
    for l in range(2):
        w[f"B{l}_wqkv"] = tile_cols(inp["L0_ffn_wqkv"][l].astype(np.float32))
        w[f"B{l}_wo"] = tile_cols(inp["L0_ffn_wo"][l].astype(np.float32))
        w1 = inp["L0_ffn_w1"][l].astype(np.float32)
        a_t = tile_cols(w1[:, :HID])
        g_t = tile_cols(w1[:, HID:])
        inter = np.empty((2 * NJ, D, 128), dtype=np.float32)
        inter[0::2] = a_t
        inter[1::2] = g_t
        w[f"B{l}_w1"] = asc(inter)
        w2p = np.zeros((NJ * 128, D), dtype=np.float32)
        w2p[:HID] = inp["L0_ffn_w2"][l].astype(np.float32)
        w[f"B{l}_w2"] = tile_cols(w2p)
        b1 = inp["L0_ffn_b1"][l].astype(np.float32)
        b1a = np.zeros(NJ * 128, dtype=np.float32)
        b1a[:HID] = b1[:HID]
        b1g = np.zeros(NJ * 128, dtype=np.float32)
        b1g[:HID] = b1[HID:]
        w[f"B{l}_b1a"] = col_pt(b1a)
        w[f"B{l}_b1g"] = col_pt(b1g)
        w[f"B{l}_b2"] = col_pt(inp["L0_ffn_b2"][l].astype(np.float32))
        w[f"B{l}_lng"] = col_pt(inp["L0_ffn_ln_g"][l].astype(np.float32))
        w[f"B{l}_lnb"] = col_pt(inp["L0_ffn_ln_b"][l].astype(np.float32))
        w[f"B{l}_qs"] = asc(inp["L0_ffn_q_scale"][l].astype(np.float32)[None, :])
        w[f"B{l}_ks"] = asc(inp["L0_ffn_k_scale"][l].astype(np.float32)[None, :])
    return w


class Ctr:
    def __init__(self, sem, step):
        self.sem = sem
        self.step = step
        self.val = 0

    def inc(self):
        self.val += self.step
        return self.val


class Sched:
    ENGS = ("t", "v", "s", "g", "y")

    def __init__(self):
        self.ops = {e: [] for e in self.ENGS}
        self.wmax = {e: {} for e in self.ENGS}

    def emit(self, eng, fn):
        self.ops[eng].append(fn)

    def wait(self, eng, ctr, val=None):
        v = ctr.val if val is None else val
        if v <= 0:
            return
        key = id(ctr.sem)
        if self.wmax[eng].get(key, -1) >= v:
            return
        self.wmax[eng][key] = v
        self.ops[eng].append(lambda e, s=ctr.sem, v=v: e.wait_ge(s, v))

    def chain_inc(self, eng, fns, ctr):
        self.wait(eng, ctr, ctr.val)
        v = None
        for f in fns[:-1]:
            v = self.last_inc(eng, [f], ctr)
            self.wait(eng, ctr, v)
        return self.last_inc(eng, [fns[-1]], ctr)

    def last_inc(self, eng, fns, ctr):
        assert fns
        for f in fns[:-1]:
            self.ops[eng].append(f)
        self.ops[eng].append(lambda e, f=fns[-1]: f(e).then_inc(ctr.sem, ctr.step))
        return ctr.inc()


class Gate:
    def __init__(self):
        self.ctr = None
        self.val = 0

    def set(self, ctr, val):
        self.ctr = ctr
        self.val = val

    def require(self, sch, eng):
        if self.ctr is not None and self.val > 0:
            sch.wait(eng, self.ctr, self.val)


def build_nc(wspecs):
    nc = bass.Bass("TRN2", target_bir_lowering=False)

    x_in = nc.dram_tensor("x_in", [T, D], F32, kind="ExternalInput").ap()
    y_out = nc.dram_tensor("y_out", [T, D], F32, kind="ExternalOutput").ap()
    x1s = nc.dram_tensor("x1s", [128, NPT * T], F32, kind="Internal").ap()

    cons = {}
    for nm, arr in _CONSTS.items():
        dt = BF16 if arr.dtype == BF else F32
        cons[nm] = nc.dram_tensor(nm, list(arr.shape), dt, kind="ExternalInput").ap()
    wdr = {}
    for nm, shape in wspecs.items():
        wdr[nm] = nc.dram_tensor(nm, list(shape), F32, kind="ExternalInput").ap()

    sch = Sched()
    ctx = ExitStack()

    _uid = [0]

    def _un(name):
        _uid[0] += 1
        un = f"{name}_{_uid[0]}"
        DBGNAMES.setdefault(name, []).append(un)
        return un

    def sb(name, shape, dt, st=None):
        return (st or ctx).enter_context(nc.sbuf_tensor(_un(name), shape, dt))

    def ps(name, shape, st=None):
        return (st or ctx).enter_context(nc.psum_tensor(_un(name), shape, F32))

    def sem(name, step=1):
        return Ctr(ctx.enter_context(nc.semaphore(name)), step)

    with ctx:
        XT = [sb(f"XT{i}", [128, T], F32) for i in range(NPT)]
        cid32 = sb("cid32", [128, 128], F32)
        cidbf = sb("cidbf", [128, 128], BF16)
        conesc = sb("conesc", [128, 1], BF16)
        conesrf = sb("conesrf", [1, 128], F32)
        cbd2f = sb("cbd2f", [128, 128], F32)
        cbd2t = sb("cbd2t", [128, 2], BF16)
        cprot = sb("cprot", [128, 128], BF16)
        cmko = sb("cmko", [128, 128], BF16)
        cmkp = sb("cmkp", [128, 128], BF16)
        cepsc = sb("cepsc", [128, 2], F32)
        cbdw = sb("cbdw", [128, 32], BF16)
        cbsel = sb("cbsel", [16, 1024], BF16)
        nc.const_aps.register(F32, 1e-5, cepsc[:, 0:1])
        nc.const_aps.register(F32, 1e-24, cepsc[:, 1:2])
        wrA = [sb(f"wrA{i}", [128, 1024], BF16) for i in range(4)]

        sLD = sem("sLD", 16)
        sGD = sem("sGD", 16)
        sGW = [sem(f"sGW{i}", 16) for i in range(4)]
        sGO = [sem(f"sGO{i}", 16) for i in range(4)]
        sG2 = [sem("sG20", 16), sem("sG21", 16)]
        sPE = sem("sPE", 1)
        sVE = sem("sVE", 1)
        sSC = sem("sSC", 1)
        sST = sem("sST", 16)

        for nm, dst in (("c_idf32", cid32), ("c_idbf", cidbf),
                        ("c_ones_col_bf", conesc), ("c_ones_row_f32", conesrf),
                        ("c_bd2_f32", cbd2f), ("c_bd2T_bf", cbd2t),
                        ("c_prot", cprot),
                        ("c_mko", cmko), ("c_mkp", cmkp), ("c_eps", cepsc),
                        ("c_bdw", cbdw), ("c_bsel", cbsel)):
            sch.emit("y", lambda e, nm=nm, dst=dst:
                     e.dma_start(out=dst[:, :], in_=cons[nm]).then_inc(sLD.sem, 16))
            sLD.inc()

        wrA_gate = [Gate(), Gate(), Gate(), Gate()]
        wrO_gate = [Gate(), Gate(), Gate(), Gate()]
        wr2_gate = [Gate(), Gate()]

        def wload(nm, sub, dstbuf, gate=None, ctr=None, cols=None):
            src = wdr[nm][sub].rearrange("(a p) m -> p a m", p=128)
            dst = (dstbuf[:, 0:cols] if cols else dstbuf[:, :]).rearrange(
                "p (a m) -> p a m", m=128)
            ctr = ctr or sGD
            if gate is not None:
                gate.require(sch, "g")
            sch.emit("g", lambda e, dst=dst, src=src, ctr=ctr:
                     e.dma_start(out=dst, in_=src).then_inc(ctr.sem, 16))
            return ctr


        def dma_phase_gate():
            for qq in ("y", "g"):
                sch.wait(qq, sPE, sPE.val)
                sch.wait(qq, sVE, sVE.val)
                sch.wait(qq, sSC, sSC.val)

        def colload(nm, dst, ctr=None):
            ctr = ctr or sLD
            sch.emit("y", lambda e, nm=nm, dst=dst, ctr=ctr:
                     e.dma_start(out=dst, in_=wdr[nm]).then_inc(ctr.sem, 16))
            return ctr.inc()

        def transpose_batch(srcs, dstw, pbufs, ev):
            nb = len(srcs) // 4
            for b in range(nb):
                buf = b % 2
                ev[buf].require(sch, "t")
                fns = []
                for j in range(4):
                    sap = srcs[b * 4 + j]
                    fns.append(lambda e, buf=buf, j=j, sap=sap: e.matmul(
                        pbufs[buf][:, j * 128:(j + 1) * 128], sap, cidbf[:, :],
                        start=True, stop=True))
                vp = sch.last_inc("t", fns, sPE)
                sch.wait("s", sPE, vp)
                vs = sch.last_inc("s", [lambda e, buf=buf, b=b: dstw[b](e, pbufs[buf])], sSC)
                ev[buf].set(sSC, vs)
            return sSC.val

        if STAGE >= 2:
            stPB = ExitStack()
            tabq = [sb("tcsq", [128, T], BF16, stPB), sb("tssq", [128, T], BF16, stPB)]
            tabk = [sb("tcsk", [128, T], BF16, stPB), sb("tssk", [128, T], BF16, stPB)]
            LCON = []
            sTB = sem("sTB", 16)
            sCL = [sem("sCL0", 16), sem("sCL1", 16)]
            for l in range(NLAY):
                cc = {}
                cc["qscol"] = sb("qscol", [128, 1], F32, stPB)
                cc["kscol"] = sb("kscol", [128, 1], F32, stPB)
                SROW = sb("SROW", [1, 256], F32, stPB)
                cc["qsrow"] = SROW[0:1, 0:64]
                cc["ksrow"] = SROW[0:1, 64:128]
                cc["m0r"] = SROW[0:1, 128:129]
                cc["m0r2"] = SROW[0:1, 129:130]
                cc["negm0"] = sb("negm0", [128, 1], F32, stPB)
                cc["lngc"] = sb("lngc", [128, 8], F32, stPB)
                cc["lnbc"] = sb("lnbc", [128, 8], F32, stPB)
                cc["b1ac"] = sb("b1ac", [128, NJ], F32, stPB)
                cc["b1gc"] = sb("b1gc", [128, NJ], F32, stPB)
                cc["b2c"] = sb("b2c", [128, 8], F32, stPB)
                LCON.append(cc)

        with ExitStack() as st0:
            xtok = sb("xtok", [128, NTT * D], F32, st0)
            xr = x_in.rearrange("(tt p) d -> p tt d", p=128)
            xtok3 = xtok[:, :].rearrange("p (tt d) -> p tt d", d=D)
            sch.emit("y", lambda e: e.dma_start(out=xtok3, in_=xr).then_inc(sLD.sem, 16))
            sLD.inc()
            sch.wait("t", sLD, sLD.val)
            pT = [ps("pT0", [128, 512], st0), ps("pT1", [128, 512], st0)]
            ev = [0, 0]
            idx = 0
            for tt in range(NTT):
                for dg in range(2):
                    buf = idx % 2
                    if idx >= 2:
                        sch.wait("t", sVE, ev[buf])
                    fns = []
                    for j in range(4):
                        dt_i = dg * 4 + j
                        fns.append(lambda e, buf=buf, j=j, tt=tt, dt_i=dt_i: e.matmul(
                            pT[buf][:, j * 128:(j + 1) * 128],
                            xtok[:, tt * D + dt_i * 128: tt * D + dt_i * 128 + 128],
                            cid32[:, :], start=True, stop=True))
                    vp = sch.last_inc("t", fns, sPE)
                    sch.wait("v", sPE, vp)
                    fns = [lambda e, buf=buf, j=j, tt=tt, dt_i=dg * 4 + j: e.tensor_copy(
                        XT[dt_i][:, tt * 128:(tt + 1) * 128],
                        pT[buf][:, j * 128:(j + 1) * 128]) for j in range(4)]
                    ev[buf] = sch.last_inc("v", fns, sVE)
                    idx += 1
        v_xt_ready = sVE.val

        if STAGE >= 2:
            for nm, dst in (("c_csq", tabq[0]), ("c_ssq", tabq[1]),
                            ("c_csk", tabk[0]), ("c_ssk", tabk[1])):
                sch.emit("y", lambda e, nm=nm, dst=dst:
                         e.dma_start(out=dst[:, :], in_=cons[nm]).then_inc(sTB.sem, 16))
                sTB.inc()
            v_tabs_all = sTB.val
            for l in range(NLAY):
                WN = f"B{l}_"
                cc = LCON[l]
                m0r = cc["m0r"]
                m0r2 = cc["m0r2"]
                for nm, dst in ((WN + "lng", cc["lngc"][:, :]), (WN + "lnb", cc["lnbc"][:, :]),
                                (WN + "b1a", cc["b1ac"][:, :]), (WN + "b1g", cc["b1gc"][:, :]),
                                (WN + "b2", cc["b2c"][:, :]), (WN + "qs", cc["qsrow"]),
                                (WN + "ks", cc["ksrow"])):
                    colload(nm, dst, sCL[l])
                for srcn, dst in ((WN + "qs", cc["qscol"]), (WN + "ks", cc["kscol"])):
                    for hb in range(2):
                        sch.emit("y", lambda e, srcn=srcn, dst=dst, hb=hb, l=l: e.dma_start(
                            out=dst[hb * 64:(hb + 1) * 64, 0:1],
                            in_=wdr[srcn].rearrange("a b -> b a")).then_inc(sCL[l].sem, 16))
                        sCL[l].inc()
                cc["ctr"] = sCL[l]
                cc["v_bias"] = sCL[l].val
                sch.wait("v", sCL[l], cc["v_bias"])
                vv = sch.chain_inc("v", [
                    lambda e, m0r=m0r, q=cc["qsrow"]: e.reduce_max(
                        m0r, q, AX.X, apply_absolute_value=True),
                    lambda e, m0r2=m0r2, ks=cc["ksrow"]: e.reduce_max(
                        m0r2, ks, AX.X, apply_absolute_value=True),
                    lambda e, m0r=m0r, m0r2=m0r2: e.tensor_mul(m0r, m0r, m0r2),
                    lambda e, m0r=m0r: e.tensor_scalar_mul(m0r, m0r, -8.0)], sVE)
                with ExitStack() as stm:
                    pM = ps("pM0", [128, 1], stm)
                    sch.wait("t", sVE, vv)
                    if l > 0:
                        sch.wait("t", sSC, LCON[l - 1]["v_m0"])
                    vp = sch.last_inc("t", [lambda e, m0r=m0r: e.matmul(
                        pM[:, :], conesrf[:, :], m0r, start=True, stop=True)], sPE)
                    sch.wait("s", sPE, vp)
                    cc["v_m0"] = sch.last_inc("s", [lambda e, n=cc["negm0"]: e.copy(
                        n[:, :], pM[:, :])], sSC)


        def emit_ln_plain(spill=False):
            sch.wait("v", sPE, sPE.val)
            with ExitStack() as stl:
                R = sb("lnrows", [1, 5 * T], F32, stl)
                lnbf = sb("lnbf", [128, T], BF16, stl)
                lnsq = sb("lnsq", [128, T], BF16, stl)
                r_sy = R[0:1, 0:T]
                r_sq = R[0:1, T:2 * T]
                r_t = R[0:1, 2 * T:3 * T]
                r_A = R[0:1, 3 * T:4 * T]
                r_B = R[0:1, 4 * T:5 * T]
                with ExitStack() as stp:
                    pSum = ps("pSum", [1, T], stp)
                    pSq = ps("pSq", [1, T], stp)
                    for pt in range(NPT):
                        fns = [lambda e, pt=pt: e.tensor_copy(lnbf[:, :], XT[pt][:, :]),
                               lambda e: e.tensor_mul(lnsq[:, :], lnbf[:, :], lnbf[:, :])]
                        vc = sch.chain_inc("v", fns, sVE)
                        sch.wait("t", sVE, vc)
                        fns = []
                        for ch in range(4):
                            fns.append(lambda e, ch=ch, pt=pt: e.matmul(
                                pSum[0:1, ch * 512:(ch + 1) * 512], conesc[:, :],
                                lnbf[:, ch * 512:(ch + 1) * 512],
                                start=(pt == 0), stop=(pt == NPT - 1)))
                            fns.append(lambda e, ch=ch, pt=pt: e.matmul(
                                pSq[0:1, ch * 512:(ch + 1) * 512], conesc[:, :],
                                lnsq[:, ch * 512:(ch + 1) * 512],
                                start=(pt == 0), stop=(pt == NPT - 1)))
                        vp = sch.last_inc("t", fns, sPE)
                        sch.wait("v", sPE, vp)
                    sch.wait("s", sPE, sPE.val)
                    vs = sch.last_inc("s", [
                        lambda e: e.copy(r_sy, pSum[0:1, :]),
                        lambda e: e.copy(r_sq, pSq[0:1, :])], sSC)
                sch.wait("v", sSC, vs)
                vv = sch.chain_inc("v", [
                    lambda e: e.tensor_scalar_mul(r_sy, r_sy, 1.0 / D),
                    lambda e: e.tensor_scalar_mul(r_sq, r_sq, 1.0 / D),
                    lambda e: e.tensor_mul(r_t, r_sy, r_sy),
                    lambda e: e.tensor_sub(r_t, r_sq, r_t)], sVE)
                sch.wait("s", sVE, vv)
                vs = sch.chain_inc("s", [
                    lambda e: e.activation(r_t, r_t, AF.Ln, bias=1e-5),
                    lambda e: e.activation(r_A, r_t, AF.Exp, scale=-0.5)], sSC)
                sch.wait("v", sSC, vs)
                vv = sch.last_inc("v", [lambda e: e.scalar_tensor_tensor(
                    r_B, r_sy, -1.0, r_A, ALU.mult, ALU.mult)], sVE)
                with ExitStack() as stp:
                    pBA = [ps("pBA0", [128, 512], stp), ps("pBA1", [128, 512], stp)]
                    pBB = [ps("pBB0", [128, 512], stp), ps("pBB1", [128, 512], stp)]
                    tmpf = sb("lntmp", [128, 512], F32, stp)
                    ev2 = [0, 0]
                    for ch in range(4):
                        buf = ch % 2
                        sch.wait("t", sVE, vv)
                        if ch >= 2:
                            sch.wait("t", sVE, ev2[buf])
                        fns = [lambda e, buf=buf, ch=ch: e.matmul(
                            pBA[buf][:, :], conesrf[:, :],
                            R[0:1, 3 * T + ch * 512: 3 * T + (ch + 1) * 512],
                            start=True, stop=True),
                            lambda e, buf=buf, ch=ch: e.matmul(
                            pBB[buf][:, :], conesrf[:, :],
                            R[0:1, 4 * T + ch * 512: 4 * T + (ch + 1) * 512],
                            start=True, stop=True)]
                        vp = sch.last_inc("t", fns, sPE)
                        sch.wait("v", sPE, vp)
                        fns = []
                        for pt in range(NPT):
                            fns.append(lambda e, pt=pt, ch=ch, buf=buf: e.tensor_mul(
                                tmpf[:, :], XT[pt][:, ch * 512:(ch + 1) * 512], pBA[buf][:, :]))
                            fns.append(lambda e, pt=pt, ch=ch, buf=buf: e.tensor_add(
                                XT[pt][:, ch * 512:(ch + 1) * 512], tmpf[:, :], pBB[buf][:, :]))
                        ev2[buf] = sch.chain_inc("v", fns, sVE)
                    if spill:
                        vv2 = sVE.val
                        for pt in range(NPT):
                            def sp(e, pt=pt, vv2=vv2):
                                e.wait_ge(sVE.sem, vv2)
                                return e.dma_start(out=x1s[:, pt * T:(pt + 1) * T],
                                                   in_=XT[pt][:, :]).then_inc(sLD.sem, 16)
                            sch.emit("y", sp)
                            sLD.inc()

        with ExitStack() as stA:
            XB = [sb(f"XB{i}", [128, SC], BF16, stA) for i in range(NPT)]
            qT = [sb(f"qTs{i}", [128, SC], BF16, stA) for i in range(NPT)]
            kT = [sb(f"kTs{i}", [128, SC], BF16, stA) for i in range(NPT)]
            vtokA = sb("vtokA", [128, CPS * 1024], BF16, stA)
            ktokA = sb("ktokA", [128, CPS * 1024], BF16, stA)
            aout = sb("aout", [128, CPS * 1024], BF16, stA)
            aoT = sb("aoT", [128, NPT * SC], BF16, stA)
            vdm = [sb(f"vdm{i}", [128, SC], BF16, stA) for i in range(2)]
            eqs = [sb(f"eqs{i}", [128, SC], F32, stA) for i in range(2)]
            rus = [sb(f"rus{i}", [128, SC], F32, stA) for i in range(2)]
            bqc = sb("bqc", [128, 8], F32, stA)
            bkc = sb("bkc", [128, 8], F32, stA)
            bvc = sb("bvc", [128, 8], F32, stA)
            boc = sb("boc", [128, 8], F32, stA)
            maskedS = [sb(f"mskd{i}", [128, 2048], BF16, stA) for i in range(2)]
            Ssnap = sb("Ssnap", [128, 1040], BF16, stA)
            Ssb = sb("Ssb", [64, 1024], F32, stA)
            zsb = sb("zsb", [64, 16], F32, stA)
            rdenA = sb("rdenA", [128, 16], F32, stA)

            cmaskA = sb("cmaskA", [128, 1024], BF16, stA)
            dma_phase_gate()
            for nm, dst in (("A_bq", bqc[:, :]), ("A_bk", bkc[:, :]),
                            ("A_bv", bvc[:, :]), ("A_bo", boc[:, :])):
                colload(nm, dst)
            sch.emit("y", lambda e: e.dma_start(
                out=cmaskA[:, :], in_=cons["c_maskA"]).then_inc(sLD.sem, 16))
            sLD.inc()
            v_abias = sLD.val
            sch.chain_inc("v", [lambda e: e.memset(Ssb[:, :], 0.0),
                                lambda e: e.memset(zsb[:, :], 0.0)], sVE)

            stScan = ExitStack()
            sSN = sem("sSN", 16)
            tev = [Gate(), Gate()]

            def emit_partA_sc(sc):
                c0 = sc * SC
                if sc > 0:
                    sch.wait("v", sPE, sPE.val)
                    sch.wait("s", sPE, sPE.val)
                sch.wait("v", sVE, v_xt_ready)
                fns = [lambda e, i=i, c0=c0: e.tensor_copy(XB[i][:, :], XT[i][:, c0:c0 + SC])
                       for i in range(NPT)]
                v_xb = sch.last_inc("v", fns, sVE)
                sch.wait("v", sLD, v_abias)
                if sc > 0:
                    sch.wait("t", sSC, sSC.val)

                with ExitStack() as stp:
                    pQ = [ps(f"pQ{i}", [128, SC], stp) for i in range(4)]
                    pVT = [ps("pVTa", [128, 512], stp), ps("pVTb", [128, 512], stp)]
                    slot_free = [Gate(), Gate(), Gate(), Gate()]
                    idx = 0
                    last_phi = [0, 0]
                    for which, bc, kind in (("A_Wq", bqc, "q"), ("A_Wk", bkc, "k"),
                                            ("A_Wv", bvc, "v")):
                        for mt in range(NPT):
                            wslot = idx % 4
                            slot = idx % 4
                            s2 = idx % 2
                            wc = wload(which, mt, wrA[wslot], wrA_gate[wslot], sGW[wslot])
                            sch.wait("t", wc, wc.inc())
                            sch.wait("t", sVE, v_xb)
                            slot_free[slot].require(sch, "t")
                            fns = []
                            for kt in range(NPT):
                                fns.append(lambda e, wslot=wslot, slot=slot, kt=kt: e.matmul(
                                    pQ[slot][:, :],
                                    wrA[wslot][:, kt * 128:(kt + 1) * 128],
                                    XB[kt][:, :],
                                    start=(kt == 0), stop=(kt == NPT - 1)))
                            vp = sch.last_inc("t", fns, sPE)
                            wrA_gate[wslot].set(sPE, vp)
                            sch.wait("s", sPE, vp)
                            sch.wait("s", sLD, v_abias)
                            if kind in ("q", "k"):
                                dstT = qT if kind == "q" else kT
                                sch.wait("s", sVE, last_phi[s2])
                                vs = sch.last_inc("s", [
                                    lambda e, slot=slot, s2=s2, bc=bc, mt=mt:
                                    e.activation(eqs[s2][:, :], pQ[slot][:, :], AF.Exp,
                                                 bias=bc[:, mt:mt + 1])], sSC)
                                sch.wait("v", sSC, vs)
                                fns = [lambda e, slot=slot, s2=s2, bc=bc, mt=mt:
                                       e.tensor_scalar(rus[s2][:, :], pQ[slot][:, :],
                                                       bc[:, mt:mt + 1], 0.0, ALU.add, ALU.max),
                                       lambda e, dstT=dstT, s2=s2, mt=mt:
                                       e.scalar_tensor_tensor(dstT[mt][:, :], eqs[s2][:, :],
                                                              1.0, rus[s2][:, :],
                                                              ALU.min, ALU.add)]
                                vv = sch.chain_inc("v", fns, sVE)
                                slot_free[slot].set(sVE, vv)
                                last_phi[s2] = vv
                            else:
                                sch.wait("v", sPE, vp)
                                vs = sch.last_inc("v", [lambda e, slot=slot, s2=s2, mt=mt:
                                                        e.tensor_scalar_add(vdm[s2][:, :],
                                                                            pQ[slot][:, :],
                                                                            bvc[:, mt:mt + 1])], sVE)
                                slot_free[slot].set(sVE, vs)
                                sch.wait("t", sVE, vs)
                                srcs = [vdm[s2][:, j * 128:(j + 1) * 128] for j in range(CPS)]

                                def mkv(mt, s2):
                                    def wr(e, pbuf):
                                        oap = bass.AP(
                                            tensor=vtokA, offset=mt * 128,
                                            ap=[[CPS * 1024, 128], [1024, CPS], [1, 128]])
                                        iap = pbuf[:, :].rearrange("p (a m) -> p a m", m=128)
                                        return e.copy(oap, iap)
                                    return wr
                                transpose_batch(srcs, [mkv(mt, s2)], pVT, tev)
                            idx += 1
                    for dpt in range(NPT):
                        for sl in range(4):
                            slot_free[sl].require(sch, "t")
                        srcs = [kT[dpt][:, j * 128:(j + 1) * 128] for j in range(CPS)]

                        def mkk(dpt):
                            def wr(e, pbuf):
                                oap = bass.AP(
                                    tensor=ktokA, offset=dpt * 128,
                                    ap=[[CPS * 1024, 128], [1024, CPS], [1, 128]])
                                iap = pbuf[:, :].rearrange("p (a m) -> p a m", m=128)
                                return e.copy(oap, iap)
                            return wr
                        transpose_batch(srcs, [mkk(dpt)], pVT, tev)
                v_proj_done = sSC.val

                if PH < 2:
                    return
                with ExitStack() as sts:
                    pSC0 = ps("pSCa", [128, 1024], sts)
                    pSC = [pSC0, pSC0]
                    pN = [ps("pNa", [128, 512], sts), ps("pNb", [128, 512], sts)]
                    pS = ps("pS", [64, 1024], sts)
                    pZz = ps("pZz", [64, 32], sts)
                    pZden = ps("pZden", [128, 32], sts)
                    v_evac_prev = sVE.val
                    v_state_prev = sVE.val
                    v_div_prev = sSC.val
                    v_div_half = [sSC.val, sSC.val]
                    pe_chunk_done = sPE.val
                    last_tt = [0, 0]
                    sch.wait("t", sSC, v_proj_done)
                    sch.wait("t", sVE, sVE.val)
                    for cc in range(CPS):
                        c = sc * CPS + cc
                        if c >= NCHL:
                            continue
                        par = c % 2
                        if c > 0:
                            sch.wait("v", sPE, pe_chunk_done)
                            vsnap = sch.chain_inc("v", [
                                lambda e: e.tensor_copy(Ssnap[0:64, 0:1024], Ssb[:, :]),
                                lambda e: e.tensor_copy(Ssnap[0:64, 1024:1040],
                                                        zsb[:, :])], sVE)
                            sch.wait("g", sVE, vsnap)
                            vdup = sch.last_inc("g", [lambda e: e.dma_start(
                                out=Ssnap[64:128, :], in_=Ssnap[0:64, :])], sSN)
                        else:
                            vsnap = sVE.val
                            vdup = sSN.val
                        v_tt = [0, 0]
                        for half in range(2):
                            sch.wait("t", sVE, last_tt[0])
                            sch.wait("t", sVE, last_tt[1])
                            fns = []
                            order = [1, 3, 5, 7, None, 0, 2, 4, 6] if half == 0 \
                                else [0, 2, 4, 6, None, 1, 3, 5, 7]
                            for hh in order:
                                if hh is None:
                                    fns.append(lambda e: e.drain())
                                    continue
                                h = half * 8 + hh
                                pp, sub = h // 2, (h % 2) * 64
                                fns.append(lambda e, half=half, hh=hh, pp=pp, sub=sub, cc=cc:
                                           e.matmul(pSC[half][:, hh * 128:(hh + 1) * 128],
                                                    kT[pp][sub:sub + 64, cc * 128:(cc + 1) * 128],
                                                    qT[pp][sub:sub + 64, cc * 128:(cc + 1) * 128],
                                                    start=True, stop=True))
                            vm1 = sch.last_inc("t", fns, sPE)
                            sch.wait("v", sPE, vm1)
                            v_tt[half] = sch.last_inc("v", [
                                lambda e, half=half, par=par: e.tensor_mul(
                                    maskedS[par][:, half * 1024:(half + 1) * 1024],
                                    pSC[half][:, :], cmaskA[:, :])], sVE)
                            last_tt[half] = v_tt[half]
                            last_tt[1 - half] = max(last_tt[1 - half], 0)
                        sch.wait("t", sVE, vsnap)
                        sch.wait("t", sSN, vdup)
                        sch.wait("t", sVE, v_evac_prev)
                        for half in range(2):
                            sch.wait("t", sSC, v_div_half[half])
                            sch.wait("t", sVE, v_tt[half])
                            fns = []
                            for hh in range(8):
                                h = half * 8 + hh
                                pp, sub = h // 2, (h % 2) * 64
                                moff = half * 1024 + hh * 128
                                vcol = cc * 1024 + h * 64
                                if SOP & 1:
                                    fns.append(lambda e, par=par, half=half, hh=hh, moff=moff,
                                           vcol=vcol, st=(hh == 0),
                                           sp=(c == 0 and hh == 7):
                                           e.matmul(pN[half][:, hh * 64:(hh + 1) * 64],
                                                    maskedS[par][:, moff:moff + 128],
                                                    vtokA[:, vcol:vcol + 64],
                                                    start=st, stop=sp))
                                if SOP & 2:
                                    fns.append(lambda e, par=par, moff=moff, h=h,
                                           st=(h == 0), sp=(c == 0 and h == 15):
                                           e.matmul(pZden[:, 2 * h:2 * h + 1],
                                                    maskedS[par][:, moff:moff + 128],
                                                    conesc[:, :], start=st, stop=sp))
                                if c > 0 and (SOP & 4):
                                    fns.append(lambda e, pp=pp, sub=sub, cc=cc, half=half, hh=hh, h=h:
                                               e.matmul(pN[half][:, hh * 64:(hh + 1) * 64],
                                                        qT[pp][sub:sub + 64, cc * 128:(cc + 1) * 128],
                                                        Ssnap[sub:sub + 64, h * 64:h * 64 + 64],
                                                        start=False, stop=(hh == 7)))
                                if SOP & 8:
                                    fns.append(lambda e, vcol=vcol, h=h:
                                           e.matmul(pS[0:64, h * 64:h * 64 + 64],
                                                    ktokA[:, vcol:vcol + 64],
                                                    vtokA[:, vcol:vcol + 64],
                                                    start=True, stop=True))
                                if SOP & 16:
                                    fns.append(lambda e, vcol=vcol, h=h:
                                           e.matmul(pZz[:, 2 * h:2 * h + 1],
                                                    ktokA[:, vcol:vcol + 64],
                                                    conesc[:, :], start=True, stop=True))
                                if c > 0 and (SOP & 2):
                                    fns.append(lambda e, pp=pp, sub=sub, cc=cc, h=h:
                                               e.matmul(pZden[:, 2 * h:2 * h + 1],
                                                        qT[pp][sub:sub + 64, cc * 128:(cc + 1) * 128],
                                                        Ssnap[sub:sub + 64, 1024 + h:1025 + h],
                                                        start=False, stop=(h == 15)))
                            pe_chunk_done = sch.last_inc("t", fns, sPE)
                        sch.wait("v", sPE, pe_chunk_done)
                        if not (SOP & 32):
                            continue
                        pZdh = bass.AP(tensor=pZden, offset=0, ap=[[32, 128], [2, 16]])
                        pZzh = bass.AP(tensor=pZz, offset=0, ap=[[32, 64], [2, 16]])
                        fns = []
                        if SOP & 8:
                            fns.append(lambda e: e.tensor_add(Ssb[:, :], Ssb[:, :], pS[0:64, :]))
                        if SOP & 16:
                            fns.append(lambda e, pZzh=pZzh: e.tensor_add(zsb[:, :], zsb[:, :], pZzh))
                        if fns:
                            v_state_prev = sch.chain_inc("v", fns, sVE)
                        if SOP & 2:
                            sch.wait("v", sSC, v_div_prev)
                            v_recip = sch.chain_inc("v", [
                                lambda e, pZdh=pZdh: e.reciprocal(rdenA[:, :], pZdh)], sVE)
                            sch.wait("s", sVE, v_recip)
                        for half in (range(2) if (SOP & 1) and (SOP & 2) else []):
                            fns = []
                            for hh in range(8):
                                h = half * 8 + hh
                                fns.append(lambda e, half=half, hh=hh, h=h, cc=cc:
                                           e.mul(
                                               aout[:, cc * 1024 + h * 64: cc * 1024 + h * 64 + 64],
                                               pN[half][:, hh * 64:(hh + 1) * 64],
                                               rdenA[:, h:h + 1]))
                            v_div_half[half] = sch.last_inc("s", fns, sSC)
                        v_div_prev = v_div_half[1]
                        v_evac_prev = v_state_prev

                if PH < 3:
                    return
                with ExitStack() as stt:
                    pVT2 = [ps("pVTc", [128, 512], stt), ps("pVTd", [128, 512], stt)]
                    tev2 = [Gate(), Gate()]
                    sch.wait("t", sVE, v_evac_prev)
                    sch.wait("t", sSC, v_div_prev)
                    for tt in range(CPS):
                        srcs = [aout[:, tt * 1024 + j * 128: tt * 1024 + (j + 1) * 128]
                                for j in range(8)]
                        dstw = []
                        for b in range(2):
                            def mka(b, tt):
                                def wr(e, pbuf):
                                    oap = bass.AP(
                                        tensor=aoT, offset=(b * 4) * SC + tt * 128,
                                        ap=[[NPT * SC, 128], [SC, 4], [1, 128]])
                                    iap = pbuf[:, :].rearrange("p (a m) -> p a m", m=128)
                                    return e.copy(oap, iap)
                                return wr
                            dstw.append(mka(b, tt))
                        transpose_batch(srcs, dstw, pVT2, tev2)
                v_aoT = sSC.val

                with ExitStack() as stw:
                    pW = [ps("pWa", [128, SC], stw), ps("pWb", [128, SC], stw)]
                    slot_free = [Gate(), Gate()]
                    for mt in range(NPT):
                        wslot = mt % 4
                        slot = mt % 2
                        wc = wload("A_Wo", mt, wrA[wslot], wrA_gate[wslot], sGW[wslot])
                        sch.wait("t", wc, wc.inc())
                        sch.wait("t", sSC, v_aoT)
                        slot_free[slot].require(sch, "t")
                        fns = []
                        for kt in range(NPT):
                            fns.append(lambda e, wslot=wslot, slot=slot, kt=kt: e.matmul(
                                pW[slot][:, :],
                                wrA[wslot][:, kt * 128:(kt + 1) * 128],
                                aoT[:, kt * SC:(kt + 1) * SC],
                                start=(kt == 0), stop=(kt == NPT - 1)))
                        vp = sch.last_inc("t", fns, sPE)
                        wrA_gate[wslot].set(sPE, vp)
                        sch.wait("v", sPE, vp)
                        vv = sch.last_inc("v", [
                            lambda e, slot=slot, mt=mt, c0=c0: e.scalar_tensor_tensor(
                                XT[mt][:, c0:c0 + SC], pW[slot][:, :], boc[:, mt:mt + 1],
                                XT[mt][:, c0:c0 + SC], ALU.add, ALU.add)], sVE)
                        slot_free[slot].set(sVE, vv)
            for sc in range(NSC):
                emit_partA_sc(sc)
            stScan.close()
        if PH >= 3:
            emit_ln_plain(spill=(STAGE >= 2))
        v_spill = sLD.val

        if STAGE >= 2:

            def emit_layerB(l):
                WN = f"B{l}_"
                cc = LCON[l]
                qscol = cc["qscol"]
                kscol = cc["kscol"]
                negm0 = cc["negm0"]
                lngc = cc["lngc"]
                lnbc = cc["lnbc"]
                b1ac = cc["b1ac"]
                b1gc = cc["b1gc"]
                b2c = cc["b2c"]
                v_bias = cc["v_bias"]
                v_m0 = cc["v_m0"]
                v_tabs = v_tabs_all
                with ExitStack() as stL:
                    oTsb = sb("oTsb", [128, 4 * T], BF16, stL)
                    with ExitStack() as stAt:
                        q2sb = sb("q2sb", [128, 4 * T], BF16, stAt)
                        k2sb = sb("k2sb", [128, 4 * T], BF16, stAt)
                        vtokB = sb("vtokB", [128, NTT * 520], BF16, stAt)
                        with ExitStack() as stq:
                            hbuf = [sb(f"hbuf{i}", [128, 1024], BF16, stq)
                                    for i in range(NPT)]
                            sqb = [sb(f"sqb{i}", [128, 1024], BF16, stq) for i in range(2)]
                            qraw = [sb(f"qraw{i}", [128, 1024], BF16, stq)
                                    for i in range(8)]
                            g_sqb = [Gate() for _ in range(2)]
                            g_qraw = [Gate() for _ in range(8)]
                            rstd16 = sb("rstd16", [16, 1024], BF16, stq)
                            qqs = [sb(f"qqs{i}", [128, 512], BF16, stq) for i in range(2)]
                            t1sb = sb("t1sb", [128, 512], BF16, stq)
                            t2sb = sb("t2sb", [128, 512], BF16, stq)
                            vdmB = sb("vdmB", [128, 1024], BF16, stq)

                            def shift_cast(nb, hb):
                                c0n = nb * 1024
                                fns = []
                                for pt in range(NPT):
                                    if pt < 4:
                                        fns.append(lambda e, pt=pt, c0n=c0n: e.tensor_copy(
                                            hb[pt][:, :], XT[pt][:, c0n:c0n + 1024]))
                                    elif nb == 0:
                                        fns.append(lambda e, pt=pt: e.memset(
                                            hb[pt][:, 0:1], 0.0))
                                        fns.append(lambda e, pt=pt: e.tensor_copy(
                                            hb[pt][:, 1:1024], XT[pt][:, 0:1023]))
                                    else:
                                        fns.append(lambda e, pt=pt, c0n=c0n: e.tensor_copy(
                                            hb[pt][:, :], XT[pt][:, c0n - 1:c0n + 1023]))
                                return sch.last_inc("v", fns, sVE)

                            pQK = [ps("pQKa", [128, 1024], stq), ps("pQKb", [128, 1024], stq)]
                            pS16 = ps("pS16", [16, 1024], stq)
                            pRT = [ps("pRTa", [128, 512], stq), ps("pRTb", [128, 512], stq)]
                            slot_free = [Gate(), Gate()]
                            g_bRB = Gate()
                            g_rot = Gate()
                            g_qqs = [Gate(), Gate()]
                            tevB = [Gate(), Gate()]
                            v_bc_end = 0
                            v_rstd_prev = 0
                            for nb in range(2):
                                c0n = nb * 1024
                                sch.wait("v", sPE, sPE.val)
                                v_h = shift_cast(nb, hbuf)
                                if nb == 0 and l == 0:
                                    sch.wait("t", sSC, LCON[-1]["v_m0"])
                                v_sum_done = 0
                                for mt in range(12):
                                    wslot = mt % 4
                                    slot = mt % 2
                                    wc = wload(WN + "wqkv", mt, wrA[wslot], wrA_gate[wslot], sGW[wslot])
                                    sch.wait("t", wc, wc.inc())
                                    sch.wait("t", sVE, v_h)
                                    slot_free[slot].require(sch, "t")
                                    fns = []
                                    for kt in range(NPT):
                                        for h2 in range(2):
                                            fns.append(lambda e, wslot=wslot, slot=slot, kt=kt, h2=h2:
                                                       e.matmul(
                                                           pQK[slot][:, h2 * 512:(h2 + 1) * 512],
                                                           wrA[wslot][:, kt * 128:(kt + 1) * 128],
                                                           hbuf[kt][:, h2 * 512:(h2 + 1) * 512],
                                                           start=(kt == 0), stop=(kt == NPT - 1)))
                                    vp = sch.last_inc("t", fns, sPE)
                                    wrA_gate[wslot].set(sPE, vp)
                                    if mt < 8:
                                        s4 = mt % 2
                                        sch.wait("s", sPE, vp)
                                        sch.wait("s", cc["ctr"], v_bias)
                                        g_sqb[s4].require(sch, "s")
                                        g_qraw[mt].require(sch, "s")
                                        vs = sch.last_inc("s", [
                                            lambda e, slot=slot, s4=s4: e.activation(
                                                sqb[s4][:, :], pQK[slot][:, :],
                                                AF.Square)], sSC)
                                        v_qcp = sch.last_inc("s", [
                                            lambda e, slot=slot, mt=mt: e.copy(
                                                qraw[mt][:, :], pQK[slot][:, :])], sSC)
                                        slot_free[slot].set(sSC, v_qcp)
                                        sch.wait("t", sSC, vs)
                                        if mt == 0 and nb == 1:
                                            sch.wait("t", sSC, v_rstd_prev)
                                        fns = []
                                        for ch in range(2):
                                            fns.append(lambda e, s4=s4, ch=ch, mt=mt: e.matmul(
                                                pS16[:, ch * 512:(ch + 1) * 512],
                                                cbdw[:, 15 - 2 * mt:31 - 2 * mt],
                                                sqb[s4][:, ch * 512:(ch + 1) * 512],
                                                start=(mt == 0), stop=(mt == 7)))
                                        v_sum_done = sch.last_inc("t", fns, sPE)
                                        g_sqb[s4].set(sPE, v_sum_done)
                                    else:
                                        mtv = mt - 8
                                        if mtv == 0:
                                            tevB[0].set(sVE, sVE.val)
                                            tevB[1].set(sVE, sVE.val)
                                        sch.wait("s", sPE, vp)
                                        vs = sch.last_inc("s", [
                                            lambda e, slot=slot: e.copy(
                                                vdmB[:, :], pQK[slot][:, :])], sSC)
                                        slot_free[slot].set(sSC, vs)
                                        sch.wait("t", sSC, vs)
                                        srcs = [vdmB[:, j * 128:(j + 1) * 128]
                                                for j in range(8)]
                                        dstw = []
                                        for b in range(2):
                                            def mkb(b, mtv, nb):
                                                def wr(e, pbuf):
                                                    tt0 = nb * 8 + b * 4
                                                    oap = bass.AP(
                                                        tensor=vtokB,
                                                        offset=tt0 * 520 + (2 * mtv) * 65,
                                                        ap=[[NTT * 520, 128], [520, 4],
                                                            [65, 2], [1, 64]])
                                                    iap = pbuf[:, :].rearrange(
                                                        "p (a h m) -> p a h m", h=2, m=64)
                                                    return e.copy(oap, iap)
                                                return wr
                                            dstw.append(mkb(b, mtv, nb))
                                        transpose_batch(srcs, dstw, pRT, tevB)
                                sch.wait("s", sPE, v_sum_done)
                                sch.wait("s", sPE, v_bc_end)
                                v_rstd = sch.chain_inc("s", [
                                    lambda e: e.activation(rstd16[:, :], pS16[:, :],
                                                           AF.Ln, bias=1e-24),
                                    lambda e: e.activation(rstd16[:, :], rstd16[:, :],
                                                           AF.Exp, scale=-0.5)], sSC)
                                v_rstd_prev = v_rstd
                                sch.wait("v", cc["ctr"], v_bias)
                                g_bRB.set(sSC, sSC.val)
                                g_rot.set(sSC, sSC.val)
                                for mt in range(8):
                                    isq = mt < 4
                                    pp = mt % 4
                                    scol = qscol if isq else kscol
                                    tabs = tabq if isq else tabk
                                    dst2 = q2sb if isq else k2sb
                                    for ch in range(2):
                                        sch.wait("t", sSC, v_rstd)
                                        g_bRB.require(sch, "t")
                                        vpb = sch.last_inc("t", [
                                            lambda e, mt=mt, ch=ch: e.matmul(
                                                pRT[0][:, :],
                                                cbsel[:, mt * 128:(mt + 1) * 128],
                                                rstd16[:, ch * 512:(ch + 1) * 512],
                                                start=True, stop=True)], sPE)
                                        v_bc_end = vpb
                                        sch.wait("v", sPE, vpb)
                                        g_qqs[ch].require(sch, "v")
                                        vq = sch.last_inc("v", [
                                            lambda e, mt=mt, ch=ch, scol=scol:
                                            e.scalar_tensor_tensor(
                                                qqs[ch][:, :],
                                                qraw[mt][:, ch * 512:(ch + 1) * 512],
                                                scol[:, 0:1], pRT[0][:, :],
                                                ALU.mult, ALU.mult)], sVE)
                                        g_bRB.set(sVE, vq)
                                        if ch == 1:
                                            g_qraw[mt].set(sVE, vq)
                                        sch.wait("t", sVE, vq)
                                        g_rot.require(sch, "t")
                                        vpr = sch.last_inc("t", [
                                            lambda e, ch=ch: e.matmul(
                                                pRT[1][:, :], cprot[:, :],
                                                qqs[ch][:, :], start=True, stop=True)], sPE)
                                        gcol = c0n + ch * 512
                                        sch.wait("v", sPE, vpr)
                                        sch.wait("v", sTB, v_tabs)
                                        vq2 = sch.chain_inc("v", [
                                            lambda e, ch=ch, tabs=tabs, gcol=gcol:
                                            e.tensor_mul(t1sb[:, :], qqs[ch][:, :],
                                                         tabs[0][:, gcol:gcol + 512]),
                                            lambda e, ch=ch, tabs=tabs, gcol=gcol:
                                            e.tensor_mul(t2sb[:, :], pRT[1][:, :],
                                                         tabs[1][:, gcol:gcol + 512]),
                                            lambda e, ch=ch, pp=pp, gcol=gcol, dst2=dst2:
                                            e.tensor_add(
                                                dst2[:, pp * T + gcol: pp * T + gcol + 512],
                                                t1sb[:, :], t2sb[:, :])], sVE)
                                        g_rot.set(sVE, vq2)
                                        g_qqs[ch].set(sVE, vq2)
                            fns = [lambda e, tt=tt: e.memset(bass.AP(
                                tensor=vtokB, offset=tt * 520 + 64,
                                ap=[[NTT * 520, 128], [65, 8]]), 1.0) for tt in range(NTT)]
                            v_vones = sch.last_inc("v", fns, sVE)
                        v_qk = sVE.val

                        DEN = sb("DENROW", [1, 3 * T], F32, stAt)
                        rdrow = DEN[0:1, 2 * T:3 * T]
                        RDENs = [sb("RDENa", [128, T], F32, stAt),
                                 sb("RDENb", [128, T], F32, stAt)]
                        with ExitStack() as sta:
                            expT = [sb(f"expT{i}", [128, 640], BF16, sta)
                                    for i in range(3)]
                            pSCb = [ps("pSCx", [128, 640], sta), ps("pSCy", [128, 640], sta)]
                            pOT = ps("pOT", [65, T], sta)
                            sc_free = [0, 0]
                            av_done = {}
                            v_evac_head = sSC.val
                            v_ev2_prev = sSC.val
                            v_ev3_box = [sSC.val]
                            v_rd = sVE.val
                            v_rd_box = [sVE.val]
                            dma_after_head = {}
                            pending_fin = [None]
                            pending_ev3 = [None]
                            pending_av = [None]

                            def mk_fin(h, pp):
                                def fin():
                                    sch.wait("v", sVE, v_ev3_box[0])
                                    sch.wait("v", sGD, sGD.val)
                                    v_rd = sch.last_inc("v", [lambda e, h=h: e.reciprocal(
                                        rdrow, DEN[0:1, (h % 2) * T:(h % 2) * T + T])], sVE)
                                    v_rd_box[0] = v_rd

                                    def gdma(e, pp=pp, s2=(h % 2), v_rd=v_rd):
                                        e.wait_ge(sVE.sem, v_rd)
                                        base = 64 * (pp % 2) + s2
                                        return e.dma_start(
                                            out=RDENs[pp // 2][base:base + 1, :],
                                            in_=rdrow).then_inc(sGD.sem, 16)
                                    sch.emit("g", gdma)
                                    sGD.inc()
                                    dma_after_head[h] = sGD.val
                                return fin
                            for h in range(8):
                                pp, sub = h // 2, (h % 2) * 64
                                banks_touched = set()
                                reg_open = set()
                                for kt in range(NTT):
                                    g1 = min(kt + 4, NTT - 1)
                                    vq = (g1 - kt + 1) * 128
                                    q0 = kt * 128
                                    slot = kt % 2
                                    esl = (h * NTT + kt) % 3
                                    sch.wait("t", sVE, v_qk)
                                    sch.wait("t", sVE, v_vones)
                                    sch.wait("t", sSC, sc_free[slot])
                                    fns = []
                                    has_mkp = kt + 4 <= NTT - 1
                                    segs = [(0, min(128, vq), True, False)]
                                    if vq > 128:
                                        segs.append((128, min(512, vq), False, False))
                                    if vq > 512:
                                        segs.append((512, vq, True, not has_mkp))
                                    for o0, o1, st, sp in segs:
                                        fns.append(lambda e, slot=slot, pp=pp, sub=sub,
                                                   kt=kt, o0=o0, o1=o1, q0=q0, st=st, sp=sp:
                                                   e.matmul(
                                                       pSCb[slot][:, o0:o1],
                                                       k2sb[sub:sub + 64,
                                                            pp * T + kt * 128:
                                                            pp * T + (kt + 1) * 128],
                                                       q2sb[sub:sub + 64,
                                                            pp * T + q0 + o0:
                                                            pp * T + q0 + o1],
                                                       start=st, stop=sp))
                                    fns.append(lambda e, slot=slot: e.matmul(
                                        pSCb[slot][:, 0:128], cidbf[:, :], cmko[:, :],
                                        start=False, stop=True))
                                    if kt + 4 <= NTT - 1:
                                        fns.append(lambda e, slot=slot: e.matmul(
                                            pSCb[slot][:, 512:640], cidbf[:, :], cmkp[:, :],
                                            start=False, stop=True))
                                    vp = sch.last_inc("t", fns, sPE)
                                    if pending_av[0] is not None:
                                        pending_av[0]()
                                        pending_av[0] = None
                                    sch.wait("s", sPE, vp)
                                    prev = (h, kt - 3) if kt >= 3 else (h - 1, NTT - 3 + kt)
                                    if prev in av_done:
                                        sch.wait("s", sPE, av_done[prev])
                                    sch.wait("s", sSC, v_m0)
                                    vs = sch.last_inc("s", [
                                        lambda e, slot=slot, esl=esl, vq=vq:
                                        e.activation(expT[esl][:, 0:vq],
                                                     pSCb[slot][:, 0:vq], AF.Exp,
                                                     bias=negm0[:, 0:1])], sSC)
                                    sc_free[slot] = vs

                                    def do_av(kt=kt, h=h, pp=pp, sub=sub, esl=esl, vs=vs,
                                              reg_st=None):
                                        nonlocal v_ev2_prev
                                        sch.wait("t", sSC, vs)
                                        if h > 0 and kt == 0:
                                            sch.wait("t", sVE, v_ev2_prev)
                                        if h > 0 and kt == 8:
                                            sch.wait("t", sVE, v_ev3_box[0])
                                        fns = []
                                        for g, st_flag, sp_flag, off in reg_st:
                                            fns.append(lambda e, kt=kt, h=h, g=g, off=off,
                                                       esl=esl, st_flag=st_flag,
                                                       sp_flag=sp_flag: e.matmul(
                                                           pOT[:, g * 128:(g + 1) * 128],
                                                           vtokB[:, kt * 520 + h * 65:
                                                                 kt * 520 + h * 65 + 65],
                                                           expT[esl][:, off:off + 128],
                                                           start=st_flag, stop=sp_flag))
                                        av_done[(h, kt)] = sch.last_inc("t", fns, sPE)
                                        if kt in (3, 7, 11):
                                            r = kt // 4
                                            sch.wait("v", sPE, av_done[(h, kt)])
                                            vev = sch.last_inc("v", [
                                                lambda e, pp=pp, sub=sub, r=r: e.tensor_copy(
                                                    oTsb[sub:sub + 64,
                                                         pp * T + r * 512:pp * T + (r + 1) * 512],
                                                    pOT[0:64, r * 512:(r + 1) * 512]),
                                                lambda e, r=r, h=h: e.tensor_copy(
                                                    DEN[0:1, (h % 2) * T + r * 512:
                                                        (h % 2) * T + (r + 1) * 512],
                                                    pOT[64:65, r * 512:(r + 1) * 512])], sVE)
                                            if kt == 11:
                                                v_ev2_prev = vev
                                    reg_st = []
                                    for g in range(kt, min(kt + 4, NTT - 1) + 1):
                                        r = g // 4
                                        st_flag = r not in reg_open
                                        reg_open.add(r)
                                        sp_flag = (kt == min(4 * r + 3, NTT - 1)
                                                   and g == kt)
                                        if sp_flag:
                                            reg_open.discard(r)
                                        reg_st.append((g, st_flag, sp_flag, (g - kt) * 128))
                                    pending_av[0] = (lambda d=do_av, rs=reg_st:
                                                     d(reg_st=rs))
                                    if kt == 4 and pending_ev3[0] is not None:
                                        pending_ev3[0]()
                                        pending_ev3[0] = None
                                    if kt == 6 and pending_fin[0] is not None:
                                        pending_fin[0]()
                                        pending_fin[0] = None
                                pending_av[0]()
                                pending_av[0] = None

                                def mkev3(pp, sub, h=None):
                                    def emitev3():
                                        sch.wait("v", sPE, av_done[(h, NTT - 1)])
                                        v_ev3_box[0] = sch.last_inc("v", [
                                            lambda e, pp=pp, sub=sub: e.tensor_copy(
                                                oTsb[sub:sub + 64, pp * T + 1536:(pp + 1) * T],
                                                pOT[0:64, 1536:2048]),
                                            lambda e, h=h: e.tensor_copy(
                                                DEN[0:1, (h % 2) * T + 1536:
                                                    (h % 2) * T + 2048],
                                                pOT[64:65, 1536:2048])], sVE)
                                    return emitev3
                                pending_ev3[0] = mkev3(pp, sub, h)
                                pending_fin[0] = mk_fin(h, pp)
                            pending_ev3[0]()
                            pending_ev3[0] = None
                            pending_fin[0]()
                            pending_fin[0] = None
                            v_alldma = sGD.val
                        with ExitStack() as std:
                            pDV = [ps("pDVa", [128, 512], std),
                                   ps("pDVb", [128, 512], std)]
                            ev3 = [0, 0]
                            i = 0
                            for pp in range(4):
                                for ch in range(4):
                                    buf = i % 2
                                    sch.wait("t", sGD, dma_after_head[2 * pp + 1])
                                    if i >= 2:
                                        sch.wait("t", sVE, ev3[buf])
                                    vp = sch.last_inc("t", [
                                        lambda e, pp=pp, ch=ch, buf=buf: e.matmul(
                                            pDV[buf][:, :],
                                            cbd2f[64 * (pp % 2):64 * (pp % 2) + 2, :],
                                            RDENs[pp // 2][64 * (pp % 2):64 * (pp % 2) + 2,
                                                           ch * 512:(ch + 1) * 512],
                                            start=True, stop=True)], sPE)
                                    sch.wait("v", sPE, vp)
                                    ev3[buf] = sch.last_inc("v", [
                                        lambda e, pp=pp, ch=ch, buf=buf: e.tensor_mul(
                                            oTsb[:, pp * T + ch * 512:
                                                 pp * T + (ch + 1) * 512],
                                            oTsb[:, pp * T + ch * 512:
                                                 pp * T + (ch + 1) * 512],
                                            pDV[buf][:, :])], sVE)
                                    i += 1
                            v_oT = sVE.val

                    with ExitStack() as stF:
                        _ = None
                        hbufF2 = [[sb(f"hbufF{n}_{i}", [128, 1024], BF16, stF)
                                   for i in range(NPT)] for n in range(2)]

                        RW = sb("lnrowsB", [1, 3 * 1024], F32, stF)

                        h3sb = sb("h3sb", [128, NJ * 1024], BF16, stF)
                        NJH = NJ // 2
                        wr2 = [sb(f"wr2_{i}", [128, NJH * 128], BF16, stF)
                               for i in range(2)]
                        lntmp2 = [sb("lntmpB0", [128, 512], BF16, stF)]
                        lntmp2.append(lntmp2[0])
                        xedge = sb("xedge", [128, 8], F32, stF)
                        dma_phase_gate()

                        def shift_castF(nb):
                            hb = hbufF2[nb]
                            c0n = nb * 1024
                            sch.wait("v", sVE, sVE.val)
                            fns = []
                            for pt in range(NPT):
                                if pt < 4:
                                    fns.append(lambda e, hb=hb, pt=pt, c0n=c0n: e.tensor_copy(
                                        hb[pt][:, :], XT[pt][:, c0n:c0n + 1024]))
                                elif nb == 0:
                                    fns.append(lambda e, hb=hb, pt=pt: e.memset(
                                        hb[pt][:, 0:1], 0.0))
                                    fns.append(lambda e, hb=hb, pt=pt: e.tensor_copy(
                                        hb[pt][:, 1:1024], XT[pt][:, 0:1023]))
                                else:
                                    fns.append(lambda e, hb=hb, pt=pt: e.tensor_copy(
                                        hb[pt][:, 0:1], xedge[:, pt:pt + 1]))
                                    fns.append(lambda e, hb=hb, pt=pt, c0n=c0n: e.tensor_copy(
                                        hb[pt][:, 1:1024], XT[pt][:, c0n:c0n + 1023]))
                            return sch.chain_inc("v", fns, sVE)

                        W1 = 1024
                        b_sy = RW[0:1, 0:W1]
                        b_s1 = RW[0:1, W1:2 * W1]
                        b_s2 = RW[0:1, 2 * W1:3 * W1]
                        def ln2_pieces(nb):
                            hbufF = hbufF2[nb]
                            c0n = nb * 1024
                            st = {"es": ExitStack()}
                            box = {}

                            def p_start():
                                st["pSum"] = ps("pSumB", [1, 1024], st["es"])
                                st["pSq"] = ps("pSqB", [1, 1024], st["es"])
                                sch.wait("v", sPE, sPE.val)
                                shift_castF(nb)

                            sqF = hbufF2[1 - nb][0]

                            def p_pt(pt):
                                pSum, pSq2 = st["pSum"], st["pSq"]
                                vc = sch.last_inc("v", [lambda e, pt=pt: e.tensor_mul(
                                    sqF[:, :], hbufF[pt][:, :], hbufF[pt][:, :])], sVE)
                                sch.wait("t", sVE, vc)
                                fns = []
                                for ch in range(2):
                                    fns.append(lambda e, ch=ch, pt=pt: e.matmul(
                                        pSum[0:1, ch * 512:(ch + 1) * 512], conesc[:, :],
                                        hbufF[pt][:, ch * 512:(ch + 1) * 512],
                                        start=(pt == 0), stop=(pt == NPT - 1)))
                                    fns.append(lambda e, ch=ch, pt=pt: e.matmul(
                                        pSq2[0:1, ch * 512:(ch + 1) * 512], conesc[:, :],
                                        sqF[:, ch * 512:(ch + 1) * 512],
                                        start=(pt == 0), stop=(pt == NPT - 1)))
                                vp = sch.last_inc("t", fns, sPE)
                                sch.wait("v", sPE, vp)

                            def p_rows():
                                pSum, pSq2 = st["pSum"], st["pSq"]
                                sch.wait("s", sPE, sPE.val)
                                vs = sch.last_inc("s", [
                                    lambda e: e.copy(b_sy, pSum[0:1, :]),
                                    lambda e: e.copy(b_s1, pSq2[0:1, :])], sSC)
                                st["es"].close()
                                sch.wait("v", sSC, vs)
                                sch.chain_inc("v", [
                                    lambda e: e.tensor_scalar_mul(b_sy, b_sy, 1.0 / D),
                                    lambda e: e.tensor_scalar_mul(b_s1, b_s1, 1.0 / D),
                                    lambda e: e.tensor_mul(b_s2, b_sy, b_sy),
                                    lambda e: e.tensor_sub(b_s1, b_s1, b_s2)], sVE)
                                sch.wait("s", sVE, sVE.val)
                                vs = sch.chain_inc("s", [
                                    lambda e: e.activation(b_s1, b_s1, AF.Ln, bias=1e-5),
                                    lambda e: e.activation(b_s1, b_s1, AF.Exp, scale=-0.5)], sSC)
                                sch.wait("v", sSC, vs)
                                st["vv"] = sch.last_inc("v", [
                                    lambda e: e.scalar_tensor_tensor(
                                        b_s2, b_sy, -1.0, b_s1, ALU.mult, ALU.mult)], sVE)

                            def p_apply(ch):
                                if ch == 0:
                                    st["es2"] = ExitStack()
                                    st["pBA"] = [ps("pBAx", [128, 512], st["es2"]),
                                                 ps("pBAy", [128, 512], st["es2"])]
                                    st["pBB"] = [ps("pBBx", [128, 512], st["es2"]),
                                                 ps("pBBy", [128, 512], st["es2"])]
                                pBA, pBB = st["pBA"], st["pBB"]
                                buf = ch % 2
                                sch.wait("t", sVE, st["vv"])
                                fns = [lambda e, buf=buf, ch=ch: e.matmul(
                                    pBA[buf][:, :], conesrf[:, :],
                                    RW[0:1, W1 + ch * 512: W1 + (ch + 1) * 512],
                                    start=True, stop=True),
                                    lambda e, buf=buf, ch=ch: e.matmul(
                                    pBB[buf][:, :], conesrf[:, :],
                                    RW[0:1, 2 * W1 + ch * 512: 2 * W1 + (ch + 1) * 512],
                                    start=True, stop=True)]
                                vp = sch.last_inc("t", fns, sPE)
                                sch.wait("v", sPE, vp)
                                for pt in range(NPT):
                                    cg = c0n + ch * 512
                                    if pt < 4:
                                        fns = [lambda e, pt=pt, cg=cg, buf=buf:
                                               e.tensor_mul(lntmp2[0][:, :],
                                                            XT[pt][:, cg:cg + 512],
                                                            pBA[buf][:, :]),
                                               lambda e, buf=buf: e.tensor_add(
                                            lntmp2[1][:, :], lntmp2[0][:, :],
                                            pBB[buf][:, :])]
                                    elif cg == 1024:
                                        fns = [lambda e, pt=pt, buf=buf:
                                               e.tensor_mul(lntmp2[0][:, 1:512],
                                                            XT[pt][:, 1024:1535],
                                                            pBA[buf][:, 1:512]),
                                               lambda e, pt=pt, buf=buf:
                                               e.tensor_mul(lntmp2[0][:, 0:1],
                                                            xedge[:, pt:pt + 1],
                                                            pBA[buf][:, 0:1]),
                                               lambda e, buf=buf: e.tensor_add(
                                            lntmp2[1][:, :], lntmp2[0][:, :],
                                            pBB[buf][:, :])]
                                    elif cg > 0:
                                        fns = [lambda e, pt=pt, cg=cg, buf=buf:
                                               e.tensor_mul(lntmp2[0][:, :],
                                                            XT[pt][:, cg - 1:cg + 511],
                                                            pBA[buf][:, :]),
                                               lambda e, buf=buf: e.tensor_add(
                                            lntmp2[1][:, :], lntmp2[0][:, :],
                                            pBB[buf][:, :])]
                                    else:
                                        fns = [lambda e, pt=pt, buf=buf:
                                               e.tensor_mul(lntmp2[0][:, 1:512],
                                                            XT[pt][:, 0:511],
                                                            pBA[buf][:, 1:512]),
                                               lambda e: e.memset(
                                            lntmp2[0][:, 0:1], 0.0),
                                               lambda e, buf=buf: e.tensor_add(
                                            lntmp2[1][:, :], lntmp2[0][:, :],
                                            pBB[buf][:, :])]
                                    fns.append(lambda e, pt=pt, ch=ch: e.tensor_scalar(
                                        hbufF[pt][:, ch * 512:(ch + 1) * 512],
                                        lntmp2[1][:, :], lngc[:, pt:pt + 1],
                                        lnbc[:, pt:pt + 1], ALU.mult, ALU.add))
                                    sch.chain_inc("v", fns, sVE)
                                box["v_hp_c%d" % ch] = sVE.val
                                if ch == 1:
                                    box["v_hp"] = sVE.val
                                    st["es2"].close()

                            pieces = [p_start]
                            pieces += [(lambda pt: (lambda: p_pt(pt)))(pt)
                                       for pt in range(NPT)]
                            pieces += [p_rows, lambda: p_apply(0), lambda: p_apply(1)]
                            return pieces, box

                        last_h3 = [0]

                        def emit_w1(nb, v_hp_c0, v_hp):
                            hbufF = hbufF2[nb]
                            e2sb = hbufF2[1 - nb][1]
                            with ExitStack() as stw1:
                                pW1 = [ps("pW1a", [128, 1024], stw1),
                                       ps("pW1b", [128, 1024], stw1),
                                       ps("pW1c", [128, 1024], stw1),
                                       ps("pW1d", [128, 1024], stw1)]
                                pair_free = [Gate(), Gate()]
                                for j in range(NJ):
                                    prslot = j % 2
                                    pa, pg = pW1[2 * prslot], pW1[2 * prslot + 1]
                                    pair_free[prslot].require(sch, "t")
                                    vps = []
                                    for ag in range(2):
                                        wslot = (2 * j + ag) % 4
                                        wc = wload(WN + "w1", 2 * j + ag, wrA[wslot],
                                                   wrA_gate[wslot], sGW[wslot])
                                        sch.wait("t", wc, wc.inc())
                                        if j == 0 and ag == 0:
                                            sch.wait("t", sVE, v_hp_c0)
                                        pdst = pa if ag == 0 else pg
                                        for h2 in range(2):
                                            fns = []
                                            for kt in range(NPT):
                                                fns.append(lambda e, pdst=pdst, wslot=wslot,
                                                           hbufF=hbufF,
                                                           kt=kt, h2=h2: e.matmul(
                                                               pdst[:, h2 * 512:(h2 + 1) * 512],
                                                               wrA[wslot][:, kt * 128:(kt + 1) * 128],
                                                               hbufF[kt][:, h2 * 512:(h2 + 1) * 512],
                                                               start=(kt == 0),
                                                               stop=(kt == NPT - 1)))
                                            if j == 0 and ag == 0 and h2 == 1:
                                                sch.wait("t", sVE, v_hp)
                                            vp = sch.last_inc("t", fns, sPE)
                                        wrA_gate[wslot].set(sPE, vp)
                                        vps.append(vp)
                                    sch.wait("s", sPE, vps[1])
                                    sch.wait("s", sVE, last_h3[0])
                                    vs = sch.last_inc("s", [
                                        lambda e, pg=pg, j=j: e.activation(
                                            e2sb[:, :], pg[:, :], AF.Gelu,
                                            bias=b1gc[:, j:j + 1])], sSC)
                                    sch.wait("v", sSC, vs)
                                    vv4 = sch.last_inc("v", [
                                        lambda e, pa=pa, j=j: e.scalar_tensor_tensor(
                                            h3sb[:, j * 1024:(j + 1) * 1024],
                                            pa[:, :], b1ac[:, j:j + 1],
                                            e2sb[:, :], ALU.add, ALU.mult)], sVE)
                                    pair_free[prslot].set(sVE, vv4)
                                    last_h3[0] = vv4
                            return sVE.val

                        def wload_half(sub, hf, dstbuf, gate, ctr):
                            src2 = wdr[WN + "w2"][sub][hf * NJH * 128:(hf + 1) * NJH * 128]
                            src2 = src2.rearrange("(a p) m -> p a m", p=128)
                            dst = dstbuf[:, :].rearrange("p (a m) -> p a m", m=128)
                            if gate is not None:
                                gate.require(sch, "g")
                            sch.emit("g", lambda e, dst=dst, src2=src2, ctr=ctr:
                                     e.dma_start(out=dst, in_=src2).then_inc(ctr.sem, 16))
                            return ctr

                        def emit_w2(nb, v_h3, inserts=None):
                            c0n = nb * 1024
                            with ExitStack() as stw2:
                                pW2 = [ps("pW2a", [128, 1024], stw2),
                                       ps("pW2b", [128, 1024], stw2)]
                                slot_free2 = [Gate(), Gate()]
                                for mt in range(NPT):
                                    slot = mt % 2
                                    pslot = mt % 2
                                    vp = 0
                                    sch.wait("t", sVE, v_h3)
                                    slot_free2[pslot].require(sch, "t")
                                    for hf in range(2):
                                        wsl = (2 * mt + hf) % 2
                                        wc = wload_half(mt, hf, wr2[wsl],
                                                        wr2_gate[wsl], sG2[wsl])
                                        sch.wait("t", wc, wc.inc())
                                        fns = []
                                        for kj in range(NJH):
                                            kt = hf * NJH + kj
                                            for h2 in range(2):
                                                fns.append(lambda e, pslot=pslot, wsl=wsl,
                                                           kj=kj, kt=kt, h2=h2:
                                                           e.matmul(
                                                               pW2[pslot][:, h2 * 512:(h2 + 1) * 512],
                                                               wr2[wsl][:, kj * 128:(kj + 1) * 128],
                                                               h3sb[:, kt * 1024 + h2 * 512:
                                                                    kt * 1024 + (h2 + 1) * 512],
                                                               start=(kt == 0), stop=(kt == NJ - 1)))
                                        vp = sch.last_inc("t", fns, sPE)
                                        wr2_gate[wsl].set(sPE, vp)
                                    sch.wait("v", sPE, vp)
                                    vv5 = sch.last_inc("v", [
                                        lambda e, pslot=pslot, mt=mt, c0n=c0n:
                                        e.scalar_tensor_tensor(
                                            XT[mt][:, c0n:c0n + 1024], pW2[pslot][:, :],
                                            b2c[:, mt:mt + 1], XT[mt][:, c0n:c0n + 1024],
                                            ALU.add, ALU.add)], sVE)
                                    slot_free2[pslot].set(sVE, vv5)
                                    for f in (inserts or {}).get(mt, []):
                                        f()

                        def emit_wo(inserts=None):
                            with ExitStack() as stw:
                                pW = [ps("pWc", [128, 1024], stw),
                                      ps("pWd", [128, 1024], stw)]
                                slot_free = [Gate(), Gate()]
                                i = 0
                                for nb in range(2):
                                    for mt in range(NPT):
                                        wslot = i % 4
                                        slot = i % 2
                                        wc = wload(WN + "wo", mt, wrA[wslot],
                                                   wrA_gate[wslot], sGW[wslot], cols=512)
                                        sch.wait("t", wc, wc.inc())
                                        sch.wait("t", sVE, v_oT)
                                        slot_free[slot].require(sch, "t")
                                        fns = []
                                        for kt in range(4):
                                            for h2 in range(2):
                                                fns.append(lambda e, slot=slot, wslot=wslot,
                                                           kt=kt, h2=h2, nb=nb: e.matmul(
                                                               pW[slot][:, h2 * 512:(h2 + 1) * 512],
                                                               wrA[wslot][:, kt * 128:(kt + 1) * 128],
                                                               oTsb[:, kt * T + nb * 1024 + h2 * 512:
                                                                    kt * T + nb * 1024 + (h2 + 1) * 512],
                                                               start=(kt == 0), stop=(kt == 3)))
                                        vp = sch.last_inc("t", fns, sPE)
                                        wrA_gate[wslot].set(sPE, vp)
                                        sch.wait("v", sPE, vp)
                                        if l == 0:
                                            sch.wait("v", sLD, v_spill)
                                        vv = sch.last_inc("v", [
                                            lambda e, slot=slot, mt=mt, nb=nb: e.tensor_add(
                                                XT[mt][:, nb * 1024:(nb + 1) * 1024],
                                                XT[mt][:, nb * 1024:(nb + 1) * 1024],
                                                pW[slot][:, :])], sVE)
                                        slot_free[slot].set(sVE, vv)
                                        for f in (inserts or {}).get((nb, mt), []):
                                            f()
                                        i += 1

                        pieces0, box0 = ln2_pieces(0)
                        ins0 = {(1, 0): [pieces0[0]],
                                (1, 1): [pieces0[1], pieces0[2]],
                                (1, 2): [pieces0[3], pieces0[4]],
                                (1, 3): [pieces0[5], pieces0[6]],
                                (1, 4): [pieces0[7], pieces0[8]],
                                (1, 5): [pieces0[9]],
                                (1, 6): [pieces0[10]],
                                (1, 7): [pieces0[11]]}
                        emit_wo(inserts=ins0)
                        sch.chain_inc("v", [
                            (lambda pt2: (lambda e: e.tensor_copy(
                                xedge[:, pt2:pt2 + 1], XT[pt2][:, 1023:1024])))(pt2)
                            for pt2 in range(NPT)], sVE)
                        v_h3_0 = emit_w1(0, box0["v_hp_c0"], box0["v_hp"])
                        pieces1, box1 = ln2_pieces(1)
                        ins = {0: [pieces1[0]],
                               1: [pieces1[1], pieces1[2]],
                               2: [pieces1[3], pieces1[4]],
                               3: [pieces1[5], pieces1[6]],
                               4: [pieces1[7], pieces1[8]],
                               5: [pieces1[9]],
                               6: [pieces1[10]],
                               7: [pieces1[11]]}
                        emit_w2(0, v_h3_0, inserts=ins)
                        v_h3_1 = emit_w1(1, box1["v_hp_c0"], box1["v_hp"])
                        emit_w2(1, v_h3_1)

            for l in range(NLAY):
                emit_layerB(l)
            stPB.close()

        if STAGE >= 3:
            with ExitStack() as stf:
                x1r = [sb(f"x1r{i}", [128, T], F32, stf) for i in range(NPT)]
                R = sb("lnrows", [1, 5 * T], F32, stf)
                lnbf = sb("lnbf", [128, T], BF16, stf)
                lnsq = sb("lnsq", [128, T], BF16, stf)
                otok = sb("otok", [128, 4 * D], F32, stf)
                r_sy = R[0:1, 0:T]
                r_sq = R[0:1, T:2 * T]
                r_t = R[0:1, 2 * T:3 * T]
                r_A = R[0:1, 3 * T:4 * T]
                r_B = R[0:1, 4 * T:5 * T]
                dma_phase_gate()
                sX1 = [sem(f"sX1{i}", 16) for i in range(NPT)]
                for pt in range(NPT):
                    sch.emit("y", lambda e, pt=pt: e.dma_start(
                        out=x1r[pt][:, :], in_=x1s[:, pt * T:(pt + 1) * T])
                        .then_inc(sX1[pt].sem, 16))
                    sX1[pt].inc()
                sch.wait("v", sPE, sPE.val)
                with ExitStack() as stp:
                    pSum = ps("pSum", [1, T], stp)
                    pSq = ps("pSq", [1, T], stp)
                    for pt in range(NPT):
                        sch.wait("v", sX1[pt], 16)
                        fns = [lambda e, pt=pt: e.tensor_add(
                                   XT[pt][:, :], XT[pt][:, :], x1r[pt][:, :]),
                               lambda e, pt=pt: e.tensor_copy(lnbf[:, :], XT[pt][:, :]),
                               lambda e: e.tensor_mul(lnsq[:, :], lnbf[:, :], lnbf[:, :])]
                        vc = sch.chain_inc("v", fns, sVE)
                        sch.wait("t", sVE, vc)
                        fns = []
                        for ch in range(4):
                            fns.append(lambda e, ch=ch, pt=pt: e.matmul(
                                pSum[0:1, ch * 512:(ch + 1) * 512], conesc[:, :],
                                lnbf[:, ch * 512:(ch + 1) * 512],
                                start=(pt == 0), stop=(pt == NPT - 1)))
                            fns.append(lambda e, ch=ch, pt=pt: e.matmul(
                                pSq[0:1, ch * 512:(ch + 1) * 512], conesc[:, :],
                                lnsq[:, ch * 512:(ch + 1) * 512],
                                start=(pt == 0), stop=(pt == NPT - 1)))
                        vp = sch.last_inc("t", fns, sPE)
                        sch.wait("v", sPE, vp)
                    sch.wait("s", sPE, sPE.val)
                    vs = sch.last_inc("s", [
                        lambda e: e.copy(r_sy, pSum[0:1, :]),
                        lambda e: e.copy(r_sq, pSq[0:1, :])], sSC)
                sch.wait("v", sSC, vs)
                vv = sch.chain_inc("v", [
                    lambda e: e.tensor_scalar_mul(r_sy, r_sy, 1.0 / D),
                    lambda e: e.tensor_scalar_mul(r_sq, r_sq, 1.0 / D),
                    lambda e: e.tensor_mul(r_t, r_sy, r_sy),
                    lambda e: e.tensor_sub(r_t, r_sq, r_t)], sVE)
                sch.wait("s", sVE, vv)
                vs = sch.chain_inc("s", [
                    lambda e: e.activation(r_t, r_t, AF.Ln, bias=1e-5),
                    lambda e: e.activation(r_A, r_t, AF.Exp, scale=-0.5)], sSC)
                sch.wait("v", sSC, vs)
                vv = sch.last_inc("v", [lambda e: e.scalar_tensor_tensor(
                    r_B, r_sy, -1.0, r_A, ALU.mult, ALU.mult)], sVE)
                v_ch = [0, 0, 0, 0]
                with ExitStack() as stp:
                    pBA = [ps("pBA0", [128, 512], stp), ps("pBA1", [128, 512], stp)]
                    pBB = [ps("pBB0", [128, 512], stp), ps("pBB1", [128, 512], stp)]
                    tmpf = sb("lntmp", [128, 512], F32, stp)
                    ev2 = [0, 0]
                    pO = [ps("pO0", [128, 512], stp), ps("pO1", [128, 512], stp)]
                    ev = [0, 0]
                    idx = 0
                    sSTs = [sem(f"sSTs{i}", 16) for i in range(4)]
                    yr = y_out.rearrange("(tt p) d -> tt p d", p=128)
                    for ch in range(4):
                        buf = ch % 2
                        sch.wait("t", sVE, vv)
                        if ch >= 2:
                            sch.wait("t", sVE, ev2[buf])
                        fns = [lambda e, buf=buf, ch=ch: e.matmul(
                            pBA[buf][:, :], conesrf[:, :],
                            R[0:1, 3 * T + ch * 512: 3 * T + (ch + 1) * 512],
                            start=True, stop=True),
                            lambda e, buf=buf, ch=ch: e.matmul(
                            pBB[buf][:, :], conesrf[:, :],
                            R[0:1, 4 * T + ch * 512: 4 * T + (ch + 1) * 512],
                            start=True, stop=True)]
                        vp = sch.last_inc("t", fns, sPE)
                        sch.wait("v", sPE, vp)
                        fns = []
                        for pt in range(NPT):
                            fns.append(lambda e, pt=pt, ch=ch, buf=buf: e.tensor_mul(
                                tmpf[:, :], XT[pt][:, ch * 512:(ch + 1) * 512], pBA[buf][:, :]))
                            fns.append(lambda e, pt=pt, ch=ch, buf=buf: e.tensor_add(
                                XT[pt][:, ch * 512:(ch + 1) * 512], tmpf[:, :], pBB[buf][:, :]))
                        ev2[buf] = sch.chain_inc("v", fns, sVE)
                        v_ch[ch] = ev2[buf]
                        for tt in range(ch * 4, (ch + 1) * 4):
                            ts = tt % 4
                            for dg in range(2):
                                tbuf = idx % 2
                                sch.wait("t", sVE, v_ch[ch])
                                if idx >= 2:
                                    sch.wait("t", sVE, ev[tbuf])
                                fns = []
                                for j in range(4):
                                    dt_i = dg * 4 + j
                                    fns.append(lambda e, tbuf=tbuf, j=j, tt=tt, dt_i=dt_i:
                                               e.matmul(
                                                   pO[tbuf][:, j * 128:(j + 1) * 128],
                                                   XT[dt_i][:, tt * 128:(tt + 1) * 128],
                                                   cid32[:, :], start=True, stop=True))
                                vp = sch.last_inc("t", fns, sPE)
                                sch.wait("v", sPE, vp)
                                if tt >= 4 and dg == 0:
                                    sch.wait("v", sSTs[ts], sSTs[ts].val)
                                fns = [lambda e, tbuf=tbuf, j=j, ts=ts, dt_i=dg * 4 + j:
                                       e.tensor_copy(
                                           otok[:, ts * D + dt_i * 128:
                                                ts * D + dt_i * 128 + 128],
                                           pO[tbuf][:, j * 128:(j + 1) * 128])
                                       for j in range(4)]
                                ev[tbuf] = sch.last_inc("v", fns, sVE)
                                idx += 1
                            vv6 = sVE.val

                            def stf_(e, tt=tt, ts=ts, vv6=vv6):
                                e.wait_ge(sVE.sem, vv6)
                                return e.dma_start(
                                    out=yr[tt],
                                    in_=otok[:, ts * D:(ts + 1) * D]).then_inc(sSTs[ts].sem, 16)
                            sch.emit("y", stf_)
                            sSTs[ts].inc()
                    for i in range(4):
                        sch.emit("y", lambda e, s=sSTs[i].sem, v=sSTs[i].val:
                                 e.wait_ge(s, v))

        with nc.Block() as blk:
            @blk.tensor
            def _(t):
                for fn in sch.ops["t"]:
                    fn(t)

            @blk.vector
            def _(v):
                for fn in sch.ops["v"]:
                    fn(v)

            @blk.scalar
            def _(s):
                for fn in sch.ops["s"]:
                    fn(s)

            @blk.gpsimd
            def _(g):
                for fn in sch.ops["g"]:
                    fn(g)

            @blk.sync
            def _(y):
                for fn in sch.ops["y"]:
                    fn(y)

    return nc


def kernel(**inputs):
    x = np.ascontiguousarray(inputs["L0_attn_x"]).astype(np.float32, copy=False)
    w = _prep_weights(inputs)
    wspecs = {nm: arr.shape for nm, arr in w.items()}
    nc = build_nc(wspecs)
    in_maps = []
    for c in range(8):
        im = {"x_in": np.ascontiguousarray(x[c])}
        for nm, arr in _CONSTS.items():
            im[nm] = arr
        for nm, arr in w.items():
            im[nm] = arr
        in_maps.append(im)
    res = run_bass_kernel_spmd(nc, in_maps, core_ids=list(range(8)))
    return np.stack([res.results[c]["y_out"] for c in range(8)], axis=0)
